# version 15
# baseline (speedup 1.0000x reference)
import numpy as np
import ml_dtypes

import concourse.bass as bass
import concourse.mybir as mybir
from concourse.bass_utils import run_bass_kernel_spmd

F32 = mybir.dt.float32
BF16 = mybir.dt.bfloat16
U8 = mybir.dt.uint8
AX = mybir.AxisListType.X
AF = mybir.ActivationFunctionType
OP = mybir.AluOpType
BF = ml_dtypes.bfloat16

N_CORES = 8
SEQ = 8192
D = 4096
KHALF = 2049
KPAD = 2176
KC = KPAD // 128
FC = D // 128
OSLAB = 512
NO8 = D // OSLAB
EPS = 1e-5
NPASS = 2
NTOK = SEQ // N_CORES
PTOK = NTOK // NPASS
TBP = PTOK // 128
TB = NTOK // 128

_CONST = {}


def _constants():
    if _CONST:
        return _CONST
    k = np.arange(KPAD, dtype=np.float64)
    f = np.arange(D, dtype=np.float64)
    valid = (k < KHALF).astype(np.float64)
    ang = 2.0 * np.pi * np.outer(f, k) / D
    wc = np.cos(ang) * valid
    ws = -np.sin(ang) * valid
    ck = np.where((k == 0) | (k == D // 2), 1.0, 2.0) * valid / D
    tau = np.arange(D, dtype=np.float64)
    iang = 2.0 * np.pi * np.outer(k, tau) / D
    ctab = np.cos(iang) * ck[:, None]
    vc = np.cos(iang) * ck[:, None]
    vs = -np.sin(iang) * ck[:, None]

    def fm_chunks(tab):
        out = np.empty((KC, 128, FC, 128), dtype=BF)
        for j in range(KC):
            out[j] = tab[:, j * 128:(j + 1) * 128].reshape(FC, 128, 128) \
                .transpose(1, 0, 2)
        return out

    wtab = np.ascontiguousarray(
        np.concatenate([fm_chunks(wc), fm_chunks(ws)], axis=0))

    def tm_chunks(tab):
        out = np.empty((8, KC, 128, 512), dtype=BF)
        for sl in range(8):
            for j in range(KC):
                out[sl, j] = tab[j * 128:(j + 1) * 128,
                                 sl * 512:(sl + 1) * 512]
        return out

    TAU = 2176
    ctab_h = np.zeros((KPAD, TAU), dtype=np.float64)
    ctab_h[:, :2049] = ctab[:, :2049]
    ctab_c = np.empty((5, KC, 128, 512), dtype=BF)
    for sl in range(5):
        w = 512 if sl < 4 else 128
        for j in range(KC):
            ctab_c[sl, j, :, :w] = ctab_h[j * 128:(j + 1) * 128,
                                          sl * 512:sl * 512 + w]
            if w < 512:
                ctab_c[sl, j, :, w:] = 0
    ctab_c = np.ascontiguousarray(ctab_c.transpose(0, 2, 1, 3))
    cpf = np.where((f == 0) | (f == D // 2), 1.0, 2.0) * (f < 2049)
    wc2 = np.cos(ang) * valid * cpf[:, None]
    wtab2 = np.ascontiguousarray(fm_chunks(wc2)[:KC, :, :KC, :])
    irtab = np.ascontiguousarray(
        np.concatenate([tm_chunks(vc), tm_chunks(vs)], axis=1)
        .transpose(0, 2, 1, 3))

    ident = np.ascontiguousarray(np.eye(128), dtype=BF)
    onesb = np.ones((1, 128), dtype=BF)

    import jax
    with jax.default_device(jax.devices("cpu")[0]):
        u = np.asarray(jax.random.uniform(
            jax.random.key(42, impl="threefry2x32"), (1, SEQ, D)))[0]
    masks = [((u >= 0.25 * l) & (u < 0.25 * (l + 1))).astype(np.uint8)
             for l in (1, 2, 3)]

    _CONST.update(wtab=wtab, wtab2=wtab2, ctab=ctab_c, irtab=irtab,
                  ident=ident, onesb=onesb, masks=masks)
    return _CONST


def _core_inputs(core, x, w0, b0, a_w, b_w):
    c = _constants()
    t0 = core * NTOK
    xs = x[0, t0:t0 + NTOK]
    x_fm = np.ascontiguousarray(
        xs.reshape(NPASS, PTOK, D).transpose(0, 2, 1)
        .reshape(NPASS, FC, 128, PTOK).transpose(0, 2, 1, 3))
    x_tm = np.ascontiguousarray(
        xs.reshape(NPASS, TBP, 128, 8, 512).transpose(0, 3, 2, 1, 4))
    w0t = np.ascontiguousarray(
        w0.T.reshape(FC, 128, NO8, OSLAB).transpose(2, 1, 0, 3))
    acat = np.ascontiguousarray(
        a_w.transpose(2, 0, 1).reshape(FC, 128, 64).transpose(1, 0, 2))
    blt = np.ascontiguousarray(
        b_w.transpose(0, 2, 1).reshape(4, 16, NO8, OSLAB)
        .transpose(2, 1, 0, 3))
    m_sl = [np.ascontiguousarray(
        m[t0:t0 + NTOK].reshape(TB, 128, NO8, OSLAB).transpose(2, 1, 0, 3))
        for m in c["masks"]]
    b0sl = np.ascontiguousarray(b0.reshape(NO8, 1, OSLAB))
    return dict(
        x_fm=x_fm.astype(np.float32), x_tm=x_tm.astype(np.float32),
        w0t=w0t.astype(np.float32), b0sl=b0sl.astype(np.float32),
        acat=acat.astype(np.float32), blt=blt.astype(np.float32),
        m1=m_sl[0], m2=m_sl[1], m3=m_sl[2],
        wtab=c["wtab"], wtab2=c["wtab2"], ctab=c["ctab"], irtab=c["irtab"],
        ident=c["ident"], identf=np.eye(128, dtype=np.float32),
        onesb=c["onesb"],
    )


class Plan:
    def __init__(self):
        self.steps = []
        self.counts = {}

    def cnt(self, s):
        return self.counts.get(s, 0)

    def step(self, engine, fn=None, waits=(), incs=()):
        self.steps.append((engine, list(waits), fn, list(incs)))
        for s, n in incs:
            self.counts[s] = self.counts.get(s, 0) + n

    def emit(self, engine_name, eng, sems):
        for e, waits, fn, incs in self.steps:
            if e != engine_name:
                continue
            for s, v in waits:
                if v > 0:
                    eng.wait_ge(sems[s], v)
            if fn is None:
                continue
            instr = fn(eng)
            for s, n in incs:
                instr.then_inc(sems[s], n)


def build_program(debug=False):
    nc = bass.Bass("TRN2", target_bir_lowering=False, debug=False)

    din = lambda name, shape, dt: nc.dram_tensor(
        name, list(shape), dt, kind="ExternalInput").ap()
    x_fm_d = din("x_fm", [NPASS, 128, FC, PTOK], F32)
    x_tm_d = din("x_tm", [NPASS, 8, 128, TBP, 512], F32)
    w0t_d = din("w0t", [NO8, 128, FC, OSLAB], F32)
    b0_d = din("b0sl", [NO8, 1, OSLAB], F32)
    acat_d = din("acat", [128, FC, 64], F32)
    blt_d = din("blt", [NO8, 16, 4, OSLAB], F32)
    m_d = [din(f"m{l}", [NO8, 128, TB, OSLAB], U8) for l in (1, 2, 3)]
    wtab_d = din("wtab", [2 * KC, 128, FC, 128], BF16)
    ctab_d = din("ctab", [5, 128, KC, 512], BF16)
    wtab2_d = din("wtab2", [KC, 128, KC, 128], BF16)
    irtab_d = din("irtab", [8, 128, 2 * KC, 512], BF16)
    ident_d = din("ident", [128, 128], BF16)
    identf_d = din("identf", [128, 128], F32)
    onesb_d = din("onesb", [1, 128], BF16)
    out_d = nc.dram_tensor("out", [TB, 128, D], F32, kind="ExternalOutput").ap()
    dbg = {}
    if debug:
        for nm, shape in [("dbg_F", [128, 2 * KC * PTOK]),
                          ("dbg_corr", [128, TBP * 2176]),
                          ("dbg_sfm", [128, TBP * 2176]),
                          ("dbg_u", [128, 2 * KC * PTOK]),
                          ("dbg_v1", [128, TBP * D]),
                          ("dbg_x1", [128, TB * D]),
                          ("dbg_h", [16, 4 * NTOK]),
                          ("dbg_acat", [128, 2048]),
                          ("dbg_v", [128, TB * D])]:
            dbg[nm] = nc.dram_tensor(nm, shape, BF16, kind="ExternalOutput").ap()

    def dump(nm, src_ap, eng="sync"):
        if debug:
            P.step(eng, lambda e, o=dbg[nm], i=src_ap: e.dma_start(out=o, in_=i),
                   waits=[("ve", P.cnt("ve")), ("te", P.cnt("te")),
                          ("se", P.cnt("se"))], incs=[("sA", 16)])
            for en in ("vector", "tensor", "scalar", "gpsimd"):
                P.step(en, waits=[("sA", P.cnt("sA"))])

    from contextlib import ExitStack
    ctx = ExitStack()
    sb = lambda n, s, dt: ctx.enter_context(nc.sbuf_tensor(n, list(s), dt))
    psT = lambda n, s, dt=F32: ctx.enter_context(nc.psum_tensor(n, list(s), dt))

    ALPHA = sb("ALPHA", [128, 34816], BF16)
    BETA = sb("BETA", [128, 32768], BF16)
    GAMMA = sb("GAMMA", [128, 16384], BF16)
    PTS = sb("PTS", [128, 8704], BF16)
    h_b = sb("h_b", [16, 4 * NTOK], BF16)
    e_scr = sb("e_scr", [128, 512], BF16)
    s2_scr = sb("s2_scr", [128, 512], BF16)
    blt_b = sb("blt_b", [16, 2, 4 * OSLAB], BF16)
    b0_b = sb("b0_b", [1, 2, OSLAB], BF16)
    ctab_b = sb("ctab_b", [128, 4, 512], BF16)
    ident_b = sb("ident_b", [128, 128], BF16)
    identf_b = sb("identf_b", [128, 128], F32)
    onesb_b = sb("onesb_b", [1, 128], BF16)
    m8 = sb("m8", [128, 8], BF16)
    mf = sb("mf", [128, 8], F32)
    sm = sb("sm", [128, 16], F32)
    vsum_p = sb("vsum_p", [128, 64], F32)
    vsq_p = sb("vsq_p", [128, 64], F32)

    def A(o, n, dt=BF16):
        v = ALPHA[:, o:o + n]
        return v.bitcast(dt) if dt != BF16 else v

    def G(o, n, dt=BF16):
        v = GAMMA[:, o:o + n]
        return v.bitcast(dt) if dt != BF16 else v

    def PT(o, n, dt=BF16):
        v = PTS[:, o:o + n]
        return v.bitcast(dt) if dt != BF16 else v

    F_sz = 2 * KC * PTOK
    F_at = lambda kc: A(kc * PTOK, PTOK)
    CORR0 = F_sz
    TAU = 2176
    TSW = [512, 512, 512, 512, 128]
    corr_tb = lambda t: A(CORR0 + t * TAU, TAU)
    corr_tile = lambda t, sl, w: A(CORR0 + t * TAU + sl * 512, w)
    corr_pad = lambda t: A(CORR0 + t * TAU + 2049, 127)
    sfm_blk = lambda t, fcc: A(CORR0 + t * TAU + fcc * 128, 128)
    v1_tile = lambda t, isl: A(CORR0 + t * D + isl * 512, 512)
    v1_row = lambda t: A(CORR0 + t * D, D)
    xfm_blk = lambda fcc: A(CORR0 + fcc * PTOK, PTOK)
    v_tile = lambda g, o8: A((g * 8 + o8) * 512, 512)
    v_row = lambda g: A(g * D, D)
    acat_blk = lambda fcc, l: PT(fcc * 64 + l * 16, 16)

    wtab_blk = lambda buf, fcc: G(buf * 4096 + fcc * 128, 128)
    w2_blk = lambda buf, fcc: G(buf * 2176 + fcc * 128, 128)
    xtm_tile = lambda buf, t: G(buf * 2048 + t * 512, 512)
    irt_g = lambda buf, j: G(4096 + buf * 4096 + j * 512, 512)
    ctab_g = lambda buf, j: G(buf * 4096 + j * 512, 512)
    w0_blk = lambda fcc: G(fcc * 512, 512)
    out_view = lambda buf: G(buf * 8192, 8192, F32)

    P_blk = lambda kc, t: PT(kc * 512 + t * 128, 128)
    tscr = PT(0, 8192, F32)
    mscr = lambda i: PT(i * 1024, 1024, F32)
    x1t = PT(0, 8192, F32)
    acc_v = PT(0, 1024, F32)
    mkt = lambda i: PTS[:, 2048 + i * 2048: 2048 + (i + 1) * 2048].bitcast(U8)

    x1_blk = lambda g, fcc: BETA[:, g * D + fcc * 128: g * D + fcc * 128 + 128]

    bank = [psT(f"bk{i}", [128, 512]) for i in range(8)]
    tr_ps = lambda b: bank[4 + b][:, 0:128]

    def warm():
        P.step("tensor", lambda e: e.matmul(
            bank[6][:, 0:128], ident_b[:], ident_b[:], start=True, stop=True),
            waits=[("ve", P.cnt("ve"))])

    sems = {s: ctx.enter_context(nc.semaphore(s))
            for s in ("te", "ve", "se", "cs", "xf", "tA", "tB",
                      "cA", "cB", "cC", "cD", "iA", "iB", "iC", "iD",
                      "xA", "xB", "wl", "w0", "w1", "w2", "w3",
                      "mk", "ac", "sA", "sB")}
    P = Plan()

    def dma(engine, out_ap, in_ap, sname, waits=()):
        P.step(engine, lambda e, o=out_ap, i=in_ap: e.dma_start(out=o, in_=i),
               waits=waits, incs=[(sname, 16)])

    dma("sync", ident_b[:], ident_d[:], "cs")
    dma("sync", identf_b[:], identf_d[:], "cs")
    dma("sync", onesb_b[:], onesb_d[:], "cs")

    ln1_affine_ve = []
    for ps in range(NPASS):
        for q in range(4):
            w = FC // 4
            dma("gpsimd",
                A(CORR0 + q * w * PTOK, w * PTOK)
                .rearrange("p (c t) -> p c t", c=w),
                x_fm_d[ps, :, q * w:(q + 1) * w, :], "xf",
                waits=[ln1_affine_ve[q] if ps > 0
                       else ("ve", P.cnt("ve"))])

        dma("sync", GAMMA[:, 0:4096].rearrange("p (c k) -> p c k", c=FC),
            wtab_d[0], "tA", waits=[("te", P.cnt("te"))])
        dma("sync", GAMMA[:, 4096:8192].rearrange("p (c k) -> p c k", c=FC),
            wtab_d[1], "tB")
        P.step("tensor", waits=[("xf", P.cnt("xf")), ("cs", P.cnt("cs")),
                                ("ve", P.cnt("ve"))])
        for kc in range(2 * KC):
            buf = kc % 2
            tsem = "tA" if buf == 0 else "tB"
            P.step("tensor", waits=[(tsem, P.cnt(tsem)),
                                    ("ve", max(0, P.cnt("ve") - 1))])
            for fc in range(FC):
                P.step("tensor", lambda e, b=buf, f=fc, k=kc: e.matmul(
                    bank[b][:], wtab_blk(b, f), xfm_blk(f),
                    start=(f == 0), stop=(f == FC - 1)),
                    incs=[("te", 1)] if fc == FC - 1 else [])
            if kc + 2 < 2 * KC:
                dma("sync",
                    GAMMA[:, buf * 4096:buf * 4096 + 4096]
                    .rearrange("p (c k) -> p c k", c=FC),
                    wtab_d[kc + 2], tsem, waits=[("te", P.cnt("te"))])
            P.step("vector", lambda e, b=buf, k=kc: e.tensor_copy(
                out=F_at(k), in_=bank[b][:]),
                waits=[("te", P.cnt("te"))], incs=[("ve", 1)])
        ve_F = P.cnt("ve")
        if ps == 0:
            dump("dbg_F", ALPHA[:, 0:F_sz])

        a3_ve = {}
        for kc in range(KC):
            P.step("scalar", lambda e, k=kc: e.activation(
                out=e_scr[:], in_=F_at(k), func=AF.Square),
                waits=[("ve", P.cnt("ve"))], incs=[("se", 1)])
            P.step("scalar", lambda e, k=kc: e.activation(
                out=s2_scr[:], in_=F_at(KC + k), func=AF.Square),
                incs=[("se", 1)])
            P.step("vector", lambda e, k=kc: e.tensor_add(
                PTS[:, k * 512:(k + 1) * 512], e_scr[:], s2_scr[:]),
                waits=[("se", P.cnt("se"))], incs=[("ve", 1)])
            a3_ve[kc] = P.cnt("ve")
        ve_P = P.cnt("ve")

        CGRP = [(0, 8), (8, 8), (16, 1)]
        def a4_tab(ts_, gi_, buf_, w=()):
            k0, ng = CGRP[gi_]
            dma("sync", GAMMA[:, buf_ * 4096: buf_ * 4096 + ng * 512]
                .rearrange("p (k c) -> p k c", k=ng),
                ctab_d[ts_][:, k0:k0 + ng, :], "cA" if buf_ == 0 else "cB", w)
        a4_tab(0, 0, 0, [("te", P.cnt("te"))])
        a4_tab(0, 1, 1)
        pend_evac = []
        gidx = 0
        for ts in range(5):
            bb = (ts % 2) * 4
            for gi in range(3):
                k0, ng = CGRP[gi]
                buf = gidx % 2
                csem = "cA" if buf == 0 else "cB"
                w = [(csem, P.cnt(csem)), ("ve", a3_ve[k0 + ng - 1])]
                if gi == 0 and ts >= 2:
                    w.append(("ve", pend_evac[ts - 2]))
                P.step("tensor", waits=w)
                for kk2 in range(ng):
                    kc = k0 + kk2
                    for t in range(TBP):
                        P.step("tensor", lambda e, tt=t, k=kc, b=buf, j=kk2,
                               bo=bb, s=ts: e.matmul(
                            bank[bo + tt][:, 0:TSW[s]], P_blk(k, tt),
                            ctab_g(b, j)[:, 0:TSW[s]],
                            start=(k == 0), stop=(k == KC - 1)),
                            incs=[("te", 1)] if t == TBP - 1 else [])
                ngi = gidx + 2
                if ngi < 15:
                    nts, ngj = divmod(ngi, 3)
                    a4_tab(nts, ngj, buf, [("te", P.cnt("te"))])
                gidx += 1
            te_now = P.cnt("te")
            for t in range(TBP):
                P.step("vector", lambda e, tt=t, s=ts, bo=bb: e.tensor_copy(
                    out=corr_tile(tt, s, TSW[s]), in_=bank[bo + tt][:, 0:TSW[s]]),
                    waits=[("te", te_now)], incs=[("ve", 1)])
            pend_evac.append(P.cnt("ve"))
        for t in range(TBP):
            P.step("vector", lambda e, tt=t: e.memset(corr_pad(tt), -1e9),
                   waits=[("ve", P.cnt("ve"))], incs=[("ve", 1)])

        ve_corr = P.cnt("ve")
        if ps == 0:
            dump("dbg_corr", A(CORR0, TBP * TAU))
        for t in range(TBP):
            P.step("vector", lambda e, tt=t: e.max(
                out=m8[:], in_=corr_tb(tt)),
                waits=[("ve", ve_corr)], incs=[("ve", 1)])
            warm()
            P.step("vector", lambda e: e.tensor_copy(out=mf[:], in_=m8[:]),
                   waits=[("ve", P.cnt("ve"))], incs=[("ve", 1)])
            P.step("vector", lambda e: e.tensor_scalar_mul(
                sm[:, 0:1], mf[:, 0:1], -1.0),
                waits=[("ve", P.cnt("ve"))], incs=[("ve", 1)])
            P.step("scalar", lambda e: e.activation(
                out=sm[:, 8:16], in_=mf[:], func=AF.Exp,
                bias=sm[:, 0:1], scale=1.0),
                waits=[("ve", P.cnt("ve"))], incs=[("se", 1)])
            P.step("vector", lambda e: e.reduce_sum(
                sm[:, 1:2], sm[:, 8:16], axis=AX),
                waits=[("se", P.cnt("se"))], incs=[("ve", 1)])
            P.step("scalar", lambda e: e.activation(
                out=sm[:, 2:3], in_=sm[:, 1:2], func=AF.Ln),
                waits=[("ve", P.cnt("ve"))], incs=[("se", 1)])
            P.step("vector", lambda e: e.tensor_sub(
                sm[:, 3:4], sm[:, 0:1], sm[:, 2:3]),
                waits=[("se", P.cnt("se"))], incs=[("ve", 1)])
            P.step("vector", lambda e: e.tensor_copy(
                out=sm[:, 4:5], in_=m8[:, 7:8]),
                waits=[("ve", P.cnt("ve"))], incs=[("ve", 1)])
            warm()
            for cchunk in range(5):
                cs = corr_tile(t, cchunk, TSW[cchunk])
                ww = TSW[cchunk]
                P.step("scalar", lambda e, c=cs, w2=ww: e.activation(
                    out=e_scr[:, 0:w2], in_=c, func=AF.Exp,
                    bias=sm[:, 3:4], scale=1.0),
                    waits=[("ve", P.cnt("ve"))], incs=[("se", 1)])
                P.step("vector", lambda e, c=cs, cc=cchunk, w2=ww:
                       e.scalar_tensor_tensor(
                    out=tscr[:, cc * 512:cc * 512 + w2], in0=c,
                    scalar=sm[:, 4:5], in1=e_scr[:, 0:w2],
                    op0=OP.is_ge, op1=OP.mult),
                    waits=[("se", P.cnt("se"))], incs=[("ve", 1)])
                warm()
            ve_s = P.cnt("ve")
            for fc in range(KC):
                b = fc % 2
                w = [("ve", ve_s)] if fc < 2 else [("ve", P.cnt("ve") - 1)]
                P.step("tensor", lambda e, f=fc, bb=b: e.transpose(
                    tr_ps(bb), tscr[:, f * 128:(f + 1) * 128], identf_b[:]),
                    waits=w, incs=[("te", 1)])
                P.step("vector", lambda e, f=fc, bb=b, tt=t: e.tensor_copy(
                    out=sfm_blk(tt, f), in_=tr_ps(bb)),
                    waits=[("te", P.cnt("te"))], incs=[("ve", 1)])
        ve_sfm = P.cnt("ve")
        if ps == 0:
            dump("dbg_sfm", A(CORR0, TBP * TAU))

        dma("sync", GAMMA[:, 0:2176].rearrange("p (c k) -> p c k", c=KC),
            wtab2_d[0], "tA", waits=[("te", P.cnt("te")), ("ve", ve_sfm)])
        dma("sync", GAMMA[:, 2176:4352].rearrange("p (c k) -> p c k", c=KC),
            wtab2_d[1], "tB")

        def sfm_mov(fcc):
            return A(CORR0, TBP * TAU).rearrange("p (t q) -> p t q", q=TAU)[
                :, :, fcc * 128:fcc * 128 + 128]

        P.step("tensor", waits=[("ve", ve_sfm)])
        for kc in range(KC):
            buf = kc % 2
            tsem = "tA" if buf == 0 else "tB"
            P.step("tensor", waits=[(tsem, P.cnt(tsem)),
                                    ("ve", max(0, P.cnt("ve") - 1))])
            for fc in range(KC):
                P.step("tensor", lambda e, b=buf, f=fc, k=kc: e.matmul(
                    bank[k % 2][:], w2_blk(b, f), sfm_mov(f),
                    start=(f == 0), stop=(f == KC - 1)),
                    incs=[("te", 1)] if fc == KC - 1 else [])
            if kc + 2 < KC:
                dma("sync",
                    GAMMA[:, buf * 2176:buf * 2176 + 2176]
                    .rearrange("p (c k) -> p c k", c=KC),
                    wtab2_d[kc + 2], tsem, waits=[("te", P.cnt("te"))])
            te_now = P.cnt("te")
            fre, fim = F_at(kc), F_at(KC + kc)
            P.step("vector", lambda e, a=fre, k=kc: e.tensor_mul(
                a, a, bank[k % 2][:]),
                waits=[("te", te_now), ("ve", P.cnt("ve"))], incs=[("ve", 1)])
            P.step("vector", lambda e, a=fim, k=kc: e.tensor_mul(
                a, a, bank[k % 2][:]), incs=[("ve", 1)])
            ve_u = P.cnt("ve")
        if ps == 0:
            dump("dbg_u", ALPHA[:, 0:F_sz])

        IGRP = [(0, 8), (8, 8), (16, 8), (24, 8), (32, 2)]
        def a9_tab(isl_, gi_, buf_, w=()):
            k0, ng = IGRP[gi_]
            dma("sync", GAMMA[:, 4096 + buf_ * 4096: 4096 + buf_ * 4096 + ng * 512]
                .rearrange("p (k c) -> p k c", k=ng),
                irtab_d[isl_][:, k0:k0 + ng, :], "iA" if buf_ == 0 else "iB", w)
        a9_tab(0, 0, 0, [("te", P.cnt("te")), ("ve", ve_u)])
        a9_tab(0, 1, 1)
        dma("gpsimd", G(0, 2048).rearrange("p (t c) -> p t c", t=TBP),
            x_tm_d[ps, 0], "xA", waits=[("te", P.cnt("te")), ("ve", ve_u)])
        P.step("tensor", waits=[("ve", ve_u)])
        pend_v1 = []
        gidx9 = 0
        for isl in range(8):
            bb9 = (isl % 2) * 4
            for gi in range(5):
                k0, ng = IGRP[gi]
                buf = gidx9 % 2
                isem = "iA" if buf == 0 else "iB"
                w = [(isem, P.cnt(isem))]
                if gi == 0 and isl >= 2:
                    w.append(("ve", pend_v1[isl - 2]))
                P.step("tensor", waits=w)
                for kk2 in range(ng):
                    kc = k0 + kk2
                    for t in range(TBP):
                        P.step("tensor", lambda e, tt=t, k=kc, b=buf, j=kk2,
                               bo=bb9: e.matmul(
                            bank[bo + tt][:],
                            A(k * PTOK + tt * 128, 128), irt_g(b, j),
                            start=(k == 0), stop=(k == 2 * KC - 1)),
                            incs=[("te", 1)] if t == TBP - 1 else [])
                ngi = gidx9 + 2
                if ngi < 40:
                    nisl, ngj = divmod(ngi, 5)
                    a9_tab(nisl, ngj, buf, [("te", P.cnt("te"))])
                gidx9 += 1
            if isl + 1 < 8:
                xsem2 = "xA" if (isl + 1) % 2 == 0 else "xB"
                dma("gpsimd",
                    G(((isl + 1) % 2) * 2048, 2048)
                    .rearrange("p (t c) -> p t c", t=TBP),
                    x_tm_d[ps, isl + 1], xsem2,
                    waits=[("ve", P.cnt("ve"))])
            te_now = P.cnt("te")
            xb = isl % 2
            xsem = "xA" if xb == 0 else "xB"
            for t in range(TBP):
                P.step("vector", lambda e, tt=t, b=xb, i=isl, bo=bb9: e.tensor_add(
                    v1_tile(tt, i), bank[bo + tt][:], xtm_tile(b, tt)),
                    waits=[("te", te_now), (xsem, P.cnt(xsem))],
                    incs=[("ve", 1)])
                P.step("vector", lambda e, tt=t, i=isl: e.reduce_sum(
                    vsum_p[:, tt * 8 + i: tt * 8 + i + 1], v1_tile(tt, i),
                    axis=AX),
                    waits=[("ve", P.cnt("ve"))], incs=[("ve", 1)])
                P.step("scalar", lambda e, tt=t, i=isl: e.activation(
                    out=s2_scr[:], in_=v1_tile(tt, i), func=AF.Square),
                    waits=[("ve", P.cnt("ve"))], incs=[("se", 1)])
                P.step("vector", lambda e, tt=t, i=isl: e.reduce_sum(
                    vsq_p[:, tt * 8 + i: tt * 8 + i + 1], s2_scr[:], axis=AX),
                    waits=[("se", P.cnt("se"))], incs=[("ve", 1)])
            pend_v1.append(P.cnt("ve"))

        if ps == NPASS - 1:
            post_a9 = (P.cnt("te"), P.cnt("ve"))
        if ps == 0:
            dump("dbg_v1", A(CORR0, TBP * D))
        for t in range(TBP):
            g = ps * TBP + t
            P.step("vector", lambda e, tt=t: e.reduce_sum(
                sm[:, 5:6], vsum_p[:, tt * 8:(tt + 1) * 8], axis=AX),
                waits=[("ve", P.cnt("ve")), ("se", P.cnt("se"))],
                incs=[("ve", 1)])
            P.step("vector", lambda e, tt=t: e.reduce_sum(
                sm[:, 6:7], vsq_p[:, tt * 8:(tt + 1) * 8], axis=AX),
                incs=[("ve", 1)])
            P.step("vector", lambda e: e.tensor_scalar_mul(
                sm[:, 5:6], sm[:, 5:6], 1.0 / D),
                waits=[("ve", P.cnt("ve"))], incs=[("ve", 1)])
            P.step("vector", lambda e: e.tensor_scalar_mul(
                sm[:, 6:7], sm[:, 6:7], 1.0 / D), incs=[("ve", 1)])
            P.step("vector", lambda e: e.tensor_mul(
                sm[:, 7:8], sm[:, 5:6], sm[:, 5:6]),
                waits=[("ve", P.cnt("ve"))], incs=[("ve", 1)])
            P.step("vector", lambda e: e.tensor_sub(
                sm[:, 7:8], sm[:, 6:7], sm[:, 7:8]),
                waits=[("ve", P.cnt("ve"))], incs=[("ve", 1)])
            P.step("vector", lambda e: e.tensor_scalar_add(
                sm[:, 7:8], sm[:, 7:8], EPS),
                waits=[("ve", P.cnt("ve"))], incs=[("ve", 1)])
            P.step("vector", lambda e: e.reciprocal(
                sm[:, 8:9], sm[:, 7:8]),
                waits=[("ve", P.cnt("ve"))], incs=[("ve", 1)])
            warm()
            P.step("scalar", lambda e: e.activation(
                out=sm[:, 9:10], in_=sm[:, 8:9], func=AF.Sqrt),
                waits=[("ve", P.cnt("ve")), ("se", P.cnt("se"))],
                incs=[("se", 1)])
            P.step("vector", lambda e: e.tensor_mul(
                sm[:, 10:11], sm[:, 5:6], sm[:, 9:10]),
                waits=[("se", P.cnt("se"))], incs=[("ve", 1)])
            P.step("vector", lambda e: e.tensor_scalar_mul(
                sm[:, 10:11], sm[:, 10:11], -1.0),
                waits=[("ve", P.cnt("ve")), ("se", P.cnt("se"))],
                incs=[("ve", 1)])
            P.step("scalar", lambda e, tt=t: e.activation(
                out=x1t[:], in_=v1_row(tt), func=AF.Identity,
                bias=sm[:, 10:11], scale=sm[:, 9:10]),
                waits=[("ve", P.cnt("ve")), ("te", P.cnt("te"))],
                incs=[("se", 1)])
            if ps == 0:
                ln1_affine_ve.append(("se", P.cnt("se")))
            warm()
            se_x1 = P.cnt("se")
            for fc in range(FC):
                b = fc % 2
                w = [("se", se_x1)]
                w.append(("ve", P.cnt("ve")) if fc < 2
                         else ("ve", P.cnt("ve") - 1))
                P.step("tensor", lambda e, f=fc, bb=b: e.transpose(
                    tr_ps(bb), x1t[:, f * 128:(f + 1) * 128], identf_b[:]),
                    waits=w, incs=[("te", 1)])
                P.step("vector", lambda e, f=fc, bb=b, gg=g: e.tensor_copy(
                    out=x1_blk(gg, f), in_=tr_ps(bb)),
                    waits=[("te", P.cnt("te"))], incs=[("ve", 1)])
    ve_x1fm = P.cnt("ve")
    if debug:
        dump("dbg_x1", BETA[:, :])

    dma("gpsimd", PTS[:, 0:2048].rearrange("p (c r) -> p c r", c=FC),
        acat_d[:], "ac", waits=[("ve", ve_x1fm), ("te", P.cnt("te"))])

    def x1_mov(th, fcc):
        return BETA[:, :].rearrange("p (g q) -> p g q", q=D)[
            :, th * 4:(th + 1) * 4, fcc * 128:fcc * 128 + 128]

    if debug:
        P.step("sync", waits=[("ac", P.cnt("ac"))])
        dump("dbg_acat", PTS[:, 0:2048])
    P.step("tensor", waits=[("ve", ve_x1fm), ("ac", P.cnt("ac"))])
    for l in range(4):
        for th in range(2):
            bk = bank[l * 2 + th][0:16, :]
            for fc in range(FC):
                P.step("tensor", lambda e, g=bk, ll=l, t=th, f=fc: e.matmul(
                    g, acat_blk(f, ll), x1_mov(t, f),
                    start=(f == 0), stop=(f == FC - 1)),
                    incs=[("te", 1)] if fc == FC - 1 else [])
    te_h = P.cnt("te")
    for l in range(4):
        for th in range(2):
            P.step("vector", lambda e, ll=l, t=th: e.tensor_copy(
                out=h_b[:, ll * NTOK + t * 512: ll * NTOK + t * 512 + 512],
                in_=bank[ll * 2 + t][0:16, :]),
                waits=[("te", te_h)], incs=[("ve", 1)])
    ve_h = P.cnt("ve")
    if debug:
        dump("dbg_h", h_b[:, :])

    wl_sub = {}

    def b_load(o8, waits_gd, waits_sd):
        for q in range(4):
            qs = f"w{q}"
            dma("gpsimd",
                GAMMA[:, q * 4096:(q + 1) * 4096]
                .rearrange("p (c k) -> p c k", c=8),
                w0t_d[o8, :, q * 8:(q + 1) * 8, :], qs,
                waits=waits_gd if q == 0 else ())
            wl_sub[(o8, q)] = (qs, P.cnt(qs))
        dma("gpsimd", b0_b[:, o8 % 2], b0_d[o8], "wl")
        dma("gpsimd", blt_b[:, o8 % 2].rearrange("q (l c) -> q l c", l=4),
            blt_d[o8], "wl")
        for i in range(3):
            dma("sync", mkt(i).rearrange("p (t c) -> p t c", t=TB),
                m_d[i][o8], "mk", waits=waits_sd if i == 0 else ())

    b_load(0, [("te", post_a9[0]), ("ve", post_a9[1])], [("te", P.cnt("te"))])
    pend_stt = {}
    pend_sel = {}
    gseq = 0
    for o8 in range(NO8):
        P.step("tensor", waits=[("mk", P.cnt("mk")),
                                ("ve", P.cnt("ve") if o8 == 0 else 0)])
        for g in range(TB):
            bb = 5 * (gseq % 2)
            w = []
            if gseq >= 2:
                w.append(("ve", pend_stt[gseq - 2]))
            P.step("tensor", waits=w)
            for fc in range(FC):
                if g == 0 and fc % 8 == 0:
                    P.step("tensor", waits=[wl_sub[(o8, fc // 8)]])
                P.step("tensor", lambda e, gg=g, f=fc, bo=bb: e.matmul(
                    bank[bo][:], x1_blk(gg, f), w0_blk(f),
                    start=(f == 0), stop=False))
            if g == 0:
                P.step("tensor", waits=[("wl", P.cnt("wl"))])
            P.step("tensor", lambda e, o=o8, bo=bb: e.matmul(
                bank[bo][:], onesb_b[:], b0_b[:, o % 2],
                start=False, stop=False))
            for j in range(4):
                P.step("tensor", lambda e, gg=g, o=o8, jj=j, bo=bb: e.matmul(
                    bank[bo][:, jj * 128:(jj + 1) * 128],
                    x1_blk(gg, 4 * o + jj), ident_b[:],
                    start=False, stop=(jj == 3)),
                    incs=[("te", 1)] if j == 3 else [])
            if gseq >= 1:
                P.step("tensor", waits=[("ve", pend_sel[gseq - 1])])
            for l in range(4):
                P.step("tensor", lambda e, ll=l, gg=g, o=o8: e.matmul(
                    bank[1 + ll][:],
                    h_b[:, ll * NTOK + gg * 128: ll * NTOK + gg * 128 + 128],
                    blt_b[:, o % 2, ll * OSLAB:(ll + 1) * OSLAB],
                    start=True, stop=True),
                    incs=[("te", 1)] if l == 3 else [])
            te_now = P.cnt("te")
            P.step("vector", lambda e: e.tensor_copy(
                out=acc_v[:], in_=bank[1][:]),
                waits=[("te", te_now), ("mk", P.cnt("mk")),
                       ("ve", P.cnt("ve"))], incs=[("ve", 1)])
            for i in range(3):
                P.step("vector", lambda e, ii=i, gg=g: e.copy_predicated(
                    out=acc_v[:],
                    mask=mkt(ii)[:, gg * OSLAB:(gg + 1) * OSLAB],
                    data=bank[2 + ii][:]),
                    waits=[("ve", P.cnt("ve"))], incs=[("ve", 1)])
            pend_sel[gseq] = P.cnt("ve")
            P.step("vector", lambda e, gg=g, o=o8, bo=bb: e.scalar_tensor_tensor(
                out=v_tile(gg, o), in0=acc_v[:], scalar=8.0, in1=bank[bo][:],
                op0=OP.mult, op1=OP.add),
                waits=[("ve", P.cnt("ve"))], incs=[("ve", 1)])
            pend_stt[gseq] = P.cnt("ve")
            P.step("vector", lambda e, gg=g, o=o8: e.reduce_sum(
                vsum_p[:, gg * 8 + o: gg * 8 + o + 1], v_tile(gg, o), axis=AX),
                waits=[("ve", P.cnt("ve"))], incs=[("ve", 1)])
            P.step("scalar", lambda e, gg=g, o=o8: e.activation(
                out=s2_scr[:], in_=v_tile(gg, o), func=AF.Square),
                waits=[("ve", P.cnt("ve"))], incs=[("se", 1)])
            P.step("vector", lambda e, gg=g, o=o8: e.reduce_sum(
                vsq_p[:, gg * 8 + o: gg * 8 + o + 1], s2_scr[:], axis=AX),
                waits=[("se", P.cnt("se"))], incs=[("ve", 1)])
            gseq += 1
        if o8 + 1 < NO8:
            b_load(o8 + 1, [("te", P.cnt("te"))], [("ve", P.cnt("ve"))])

    te_b = P.cnt("te")
    for g in range(TB):
        P.step("vector", lambda e, gg=g: e.reduce_sum(
            sm[:, 5:6], vsum_p[:, gg * 8:(gg + 1) * 8], axis=AX),
            waits=[("ve", P.cnt("ve")), ("te", te_b)], incs=[("ve", 1)])
        P.step("vector", lambda e, gg=g: e.reduce_sum(
            sm[:, 6:7], vsq_p[:, gg * 8:(gg + 1) * 8], axis=AX),
            incs=[("ve", 1)])
        P.step("vector", lambda e: e.tensor_scalar_mul(
            sm[:, 5:6], sm[:, 5:6], 1.0 / D),
            waits=[("ve", P.cnt("ve"))], incs=[("ve", 1)])
        P.step("vector", lambda e: e.tensor_scalar_mul(
            sm[:, 6:7], sm[:, 6:7], 1.0 / D), incs=[("ve", 1)])
        P.step("vector", lambda e: e.tensor_mul(
            sm[:, 7:8], sm[:, 5:6], sm[:, 5:6]),
            waits=[("ve", P.cnt("ve"))], incs=[("ve", 1)])
        P.step("vector", lambda e: e.tensor_sub(
            sm[:, 7:8], sm[:, 6:7], sm[:, 7:8]),
            waits=[("ve", P.cnt("ve"))], incs=[("ve", 1)])
        P.step("vector", lambda e: e.tensor_scalar_add(
            sm[:, 7:8], sm[:, 7:8], EPS),
            waits=[("ve", P.cnt("ve"))], incs=[("ve", 1)])
        P.step("vector", lambda e: e.reciprocal(
            sm[:, 8:9], sm[:, 7:8]),
            waits=[("ve", P.cnt("ve"))], incs=[("ve", 1)])
        P.step("scalar", lambda e: e.activation(
            out=sm[:, 9:10], in_=sm[:, 8:9], func=AF.Sqrt),
            waits=[("ve", P.cnt("ve")), ("se", P.cnt("se"))],
            incs=[("se", 1)])
        P.step("vector", lambda e: e.tensor_mul(
            sm[:, 10:11], sm[:, 5:6], sm[:, 9:10]),
            waits=[("se", P.cnt("se"))], incs=[("ve", 1)])
        P.step("vector", lambda e: e.tensor_scalar_mul(
            sm[:, 10:11], sm[:, 10:11], -1.0),
            waits=[("ve", P.cnt("ve")), ("se", P.cnt("se"))],
            incs=[("ve", 1)])
        b = g % 2
        ssem = "sA" if b == 0 else "sB"
        w = [("ve", P.cnt("ve"))]
        if g >= 2:
            w.append((ssem, P.cnt(ssem)))
        P.step("scalar", lambda e, gg=g, bb=b: e.activation(
            out=out_view(bb), in_=v_row(gg), func=AF.Identity,
            bias=sm[:, 10:11], scale=sm[:, 9:10]),
            waits=w, incs=[("se", 1)])
        dma("sync", out_d[g], out_view(b), ssem,
            waits=[("se", P.cnt("se"))])

    P.step("sync", waits=[("sA", P.cnt("sA")), ("sB", P.cnt("sB")),
                          ("wl", P.cnt("wl")), ("xf", P.cnt("xf"))])

    with nc.Block() as block:
        @block.sync
        def _(eng):
            P.emit("sync", eng, sems)

        @block.gpsimd
        def _(eng):
            P.emit("gpsimd", eng, sems)

        @block.tensor
        def _(eng):
            P.emit("tensor", eng, sems)

        @block.vector
        def _(eng):
            P.emit("vector", eng, sems)

        @block.scalar
        def _(eng):
            P.emit("scalar", eng, sems)

    ctx.close()
    return nc


_NC = {}


def kernel(L0_attn_x, L0_ffn_W0, L0_ffn_b0, L0_ffn_A, L0_ffn_B):
    x = np.asarray(L0_attn_x, dtype=np.float32)
    w0 = np.asarray(L0_ffn_W0, dtype=np.float32)
    b0 = np.asarray(L0_ffn_b0, dtype=np.float32)
    a_w = np.asarray(L0_ffn_A, dtype=np.float32)
    b_w = np.asarray(L0_ffn_B, dtype=np.float32)
    if "nc" not in _NC:
        _NC["nc"] = build_program()
    in_maps = [_core_inputs(c, x, w0, b0, a_w, b_w) for c in range(N_CORES)]
    res = run_bass_kernel_spmd(_NC["nc"], in_maps, core_ids=list(range(N_CORES)))
    out = np.concatenate([r["out"].reshape(-1, D) for r in res.results], axis=0)
    return out.reshape(1, SEQ, D).astype(np.float32)


# version 16
# speedup vs baseline: 1.2281x; 1.2281x over previous
import numpy as np
import ml_dtypes

import concourse.bass as bass
import concourse.mybir as mybir
from concourse.bass_utils import run_bass_kernel_spmd

F32 = mybir.dt.float32
BF16 = mybir.dt.bfloat16
U8 = mybir.dt.uint8
AX = mybir.AxisListType.X
AF = mybir.ActivationFunctionType
OP = mybir.AluOpType
BF = ml_dtypes.bfloat16

N_CORES = 8
SEQ = 8192
D = 4096
KHALF = 2049
KPAD = 2176
KC = KPAD // 128
FC = D // 128
OSLAB = 512
NO8 = D // OSLAB
EPS = 1e-5
NPASS = 2
NTOK = SEQ // N_CORES
PTOK = NTOK // NPASS
TBP = PTOK // 128
TB = NTOK // 128

_CONST = {}


def _constants():
    if _CONST:
        return _CONST
    k = np.arange(KPAD, dtype=np.float64)
    f = np.arange(D, dtype=np.float64)
    valid = (k < KHALF).astype(np.float64)
    ang = 2.0 * np.pi * np.outer(f, k) / D
    wc = np.cos(ang) * valid
    ws = -np.sin(ang) * valid
    ck = np.where((k == 0) | (k == D // 2), 1.0, 2.0) * valid / D
    tau = np.arange(D, dtype=np.float64)
    iang = 2.0 * np.pi * np.outer(k, tau) / D
    ctab = np.cos(iang) * ck[:, None]
    vc = np.cos(iang) * ck[:, None]
    vs = -np.sin(iang) * ck[:, None]

    def fm_chunks(tab):
        out = np.empty((KC, 128, FC, 128), dtype=BF)
        for j in range(KC):
            out[j] = tab[:, j * 128:(j + 1) * 128].reshape(FC, 128, 128) \
                .transpose(1, 0, 2)
        return out

    wtab = np.ascontiguousarray(
        np.concatenate([fm_chunks(wc), fm_chunks(ws)], axis=0))

    def tm_chunks(tab):
        out = np.empty((8, KC, 128, 512), dtype=BF)
        for sl in range(8):
            for j in range(KC):
                out[sl, j] = tab[j * 128:(j + 1) * 128,
                                 sl * 512:(sl + 1) * 512]
        return out

    TAU = 2176
    ctab_h = np.zeros((KPAD, TAU), dtype=np.float64)
    ctab_h[:, :2049] = ctab[:, :2049]
    ctab_c = np.empty((5, KC, 128, 512), dtype=BF)
    for sl in range(5):
        w = 512 if sl < 4 else 128
        for j in range(KC):
            ctab_c[sl, j, :, :w] = ctab_h[j * 128:(j + 1) * 128,
                                          sl * 512:sl * 512 + w]
            if w < 512:
                ctab_c[sl, j, :, w:] = 0
    ctab_c = np.ascontiguousarray(ctab_c.transpose(0, 2, 1, 3))
    cpf = np.where((f == 0) | (f == D // 2), 1.0, 2.0) * (f < 2049)
    wc2 = np.cos(ang) * valid * cpf[:, None]
    wtab2 = np.ascontiguousarray(fm_chunks(wc2)[:KC, :, :KC, :])
    irtab = np.ascontiguousarray(
        np.concatenate([tm_chunks(vc), tm_chunks(vs)], axis=1)
        .transpose(0, 2, 1, 3))

    ident = np.ascontiguousarray(np.eye(128), dtype=BF)
    onesb = np.ones((1, 128), dtype=BF)

    import jax
    with jax.default_device(jax.devices("cpu")[0]):
        u = np.asarray(jax.random.uniform(
            jax.random.key(42, impl="threefry2x32"), (1, SEQ, D)))[0]
    masks = [((u >= 0.25 * l) & (u < 0.25 * (l + 1))).astype(np.uint8)
             for l in (1, 2, 3)]

    _CONST.update(wtab=wtab, wtab2=wtab2, ctab=ctab_c, irtab=irtab,
                  ident=ident, onesb=onesb, masks=masks)
    return _CONST


def _core_inputs(core, x, w0, b0, a_w, b_w):
    c = _constants()
    t0 = core * NTOK
    xs = x[0, t0:t0 + NTOK]
    x_fm = np.ascontiguousarray(
        xs.reshape(NPASS, PTOK, D).transpose(0, 2, 1)
        .reshape(NPASS, FC, 128, PTOK).transpose(0, 2, 1, 3))
    x_tm = np.ascontiguousarray(
        xs.reshape(NPASS, TBP, 128, 8, 512).transpose(0, 3, 2, 1, 4))
    w0t = np.ascontiguousarray(
        w0.T.reshape(FC, 128, NO8, OSLAB).transpose(2, 1, 0, 3))
    acat = np.ascontiguousarray(
        a_w.transpose(2, 0, 1).reshape(FC, 128, 64).transpose(1, 0, 2))
    blt = np.ascontiguousarray(
        b_w.transpose(0, 2, 1).reshape(4, 16, NO8, OSLAB)
        .transpose(2, 1, 0, 3))
    m_sl = [np.ascontiguousarray(
        m[t0:t0 + NTOK].reshape(TB, 128, NO8, OSLAB).transpose(2, 1, 0, 3))
        for m in c["masks"]]
    b0sl = np.ascontiguousarray(b0.reshape(NO8, 1, OSLAB))
    return dict(
        x_fm=x_fm.astype(np.float32), x_tm=x_tm.astype(np.float32),
        w0t=w0t.astype(np.float32), b0sl=b0sl.astype(np.float32),
        acat=acat.astype(np.float32), blt=blt.astype(np.float32),
        m1=m_sl[0], m2=m_sl[1], m3=m_sl[2],
        wtab=c["wtab"], wtab2=c["wtab2"], ctab=c["ctab"], irtab=c["irtab"],
        ident=c["ident"], identf=np.eye(128, dtype=np.float32),
        onesb=c["onesb"],
    )


class Plan:
    def __init__(self):
        self.steps = []
        self.counts = {}

    def cnt(self, s):
        return self.counts.get(s, 0)

    def step(self, engine, fn=None, waits=(), incs=()):
        self.steps.append((engine, list(waits), fn, list(incs)))
        for s, n in incs:
            self.counts[s] = self.counts.get(s, 0) + n

    def emit(self, engine_name, eng, sems):
        for e, waits, fn, incs in self.steps:
            if e != engine_name:
                continue
            for s, v in waits:
                if v > 0:
                    eng.wait_ge(sems[s], v)
            if fn is None:
                continue
            instr = fn(eng)
            for s, n in incs:
                instr.then_inc(sems[s], n)


def build_program(debug=False):
    nc = bass.Bass("TRN2", target_bir_lowering=False, debug=False)

    din = lambda name, shape, dt: nc.dram_tensor(
        name, list(shape), dt, kind="ExternalInput").ap()
    x_fm_d = din("x_fm", [NPASS, 128, FC, PTOK], F32)
    x_tm_d = din("x_tm", [NPASS, 8, 128, TBP, 512], F32)
    w0t_d = din("w0t", [NO8, 128, FC, OSLAB], F32)
    b0_d = din("b0sl", [NO8, 1, OSLAB], F32)
    acat_d = din("acat", [128, FC, 64], F32)
    blt_d = din("blt", [NO8, 16, 4, OSLAB], F32)
    m_d = [din(f"m{l}", [NO8, 128, TB, OSLAB], U8) for l in (1, 2, 3)]
    wtab_d = din("wtab", [2 * KC, 128, FC, 128], BF16)
    ctab_d = din("ctab", [5, 128, KC, 512], BF16)
    wtab2_d = din("wtab2", [KC, 128, KC, 128], BF16)
    irtab_d = din("irtab", [8, 128, 2 * KC, 512], BF16)
    ident_d = din("ident", [128, 128], BF16)
    identf_d = din("identf", [128, 128], F32)
    onesb_d = din("onesb", [1, 128], BF16)
    out_d = nc.dram_tensor("out", [TB, 128, D], F32, kind="ExternalOutput").ap()
    dbg = {}
    if debug:
        for nm, shape in [("dbg_F", [128, 2 * KC * PTOK]),
                          ("dbg_corr", [128, TBP * 2176]),
                          ("dbg_sfm", [128, TBP * 2176]),
                          ("dbg_u", [128, 2 * KC * PTOK]),
                          ("dbg_v1", [128, TBP * D]),
                          ("dbg_x1", [128, TB * D]),
                          ("dbg_h", [16, 4 * NTOK]),
                          ("dbg_acat", [128, 2048]),
                          ("dbg_v", [128, TB * D])]:
            dbg[nm] = nc.dram_tensor(nm, shape, BF16, kind="ExternalOutput").ap()

    def dump(nm, src_ap, eng="sync"):
        if debug:
            P.step(eng, lambda e, o=dbg[nm], i=src_ap: e.dma_start(out=o, in_=i),
                   waits=[("ve", P.cnt("ve")), ("te", P.cnt("te")),
                          ("se", P.cnt("se"))], incs=[("sA", 16)])
            for en in ("vector", "tensor", "scalar", "gpsimd"):
                P.step(en, waits=[("sA", P.cnt("sA"))])

    from contextlib import ExitStack
    ctx = ExitStack()
    sb = lambda n, s, dt: ctx.enter_context(nc.sbuf_tensor(n, list(s), dt))
    psT = lambda n, s, dt=F32: ctx.enter_context(nc.psum_tensor(n, list(s), dt))

    ALPHA = sb("ALPHA", [128, 34816], BF16)
    BETA = sb("BETA", [128, 32768], BF16)
    GAMMA = sb("GAMMA", [128, 16384], BF16)
    PTS = sb("PTS", [128, 8704], BF16)
    h_b = sb("h_b", [16, 4 * NTOK], BF16)
    e_scr = sb("e_scr", [128, 512], BF16)
    s2_scr = sb("s2_scr", [128, 512], BF16)
    blt_b = sb("blt_b", [16, 2, 4 * OSLAB], BF16)
    b0_b = sb("b0_b", [1, 2, OSLAB], BF16)
    ctab_b = sb("ctab_b", [128, 4, 512], BF16)
    ident_b = sb("ident_b", [128, 128], BF16)
    identf_b = sb("identf_b", [128, 128], F32)
    onesb_b = sb("onesb_b", [1, 128], BF16)
    m8 = sb("m8", [128, 8], BF16)
    mf = sb("mf", [128, 8], F32)
    sm = sb("sm", [128, 16], F32)
    vsum_p = sb("vsum_p", [128, 64], F32)
    vsq_p = sb("vsq_p", [128, 64], F32)

    def A(o, n, dt=BF16):
        v = ALPHA[:, o:o + n]
        return v.bitcast(dt) if dt != BF16 else v

    def G(o, n, dt=BF16):
        v = GAMMA[:, o:o + n]
        return v.bitcast(dt) if dt != BF16 else v

    def PT(o, n, dt=BF16):
        v = PTS[:, o:o + n]
        return v.bitcast(dt) if dt != BF16 else v

    F_sz = 2 * KC * PTOK
    F_at = lambda kc: A(kc * PTOK, PTOK)
    CORR0 = F_sz
    TAU = 2176
    TSW = [512, 512, 512, 512, 128]
    corr_tb = lambda t: A(CORR0 + t * TAU, TAU)
    corr_tile = lambda t, sl, w: A(CORR0 + t * TAU + sl * 512, w)
    corr_pad = lambda t: A(CORR0 + t * TAU + 2049, 127)
    sfm_blk = lambda t, fcc: A(CORR0 + t * TAU + fcc * 128, 128)
    v1_tile = lambda t, isl: A(CORR0 + t * D + isl * 512, 512)
    v1_row = lambda t: A(CORR0 + t * D, D)
    xfm_blk = lambda fcc: A(CORR0 + fcc * PTOK, PTOK)
    v_tile = lambda g, o8: A((g * 8 + o8) * 512, 512)
    v_row = lambda g: A(g * D, D)
    acat_blk = lambda fcc, l: PT(fcc * 64 + l * 16, 16)

    wtab_blk = lambda buf, fcc: G(buf * 4096 + fcc * 128, 128)
    w2_blk = lambda buf, fcc: G(buf * 2176 + fcc * 128, 128)
    xtm_tile = lambda buf, t: G(buf * 2048 + t * 512, 512)
    irt_g = lambda buf, j: G(4096 + buf * 4096 + j * 512, 512)
    ctab_g = lambda buf, j: G(buf * 4096 + j * 512, 512)
    w0_blk = lambda fcc: G(fcc * 512, 512)
    out_view = lambda buf: G(buf * 8192, 8192, F32)

    P_blk = lambda kc, t: PT(kc * 512 + t * 128, 128)
    tscr = PT(0, 8192, F32)
    mscr = lambda i: PT(i * 1024, 1024, F32)
    x1t = PT(0, 8192, F32)
    acc_v = PT(0, 1024, F32)
    mkt = lambda i: PTS[:, 2048 + i * 2048: 2048 + (i + 1) * 2048].bitcast(U8)

    x1_blk = lambda g, fcc: BETA[:, g * D + fcc * 128: g * D + fcc * 128 + 128]

    bank = [psT(f"bk{i}", [128, 512]) for i in range(8)]
    tr_ps = lambda b: bank[4 + b][:, 0:128]

    def warm():
        P.step("tensor", lambda e: e.matmul(
            bank[6][:, 0:128], ident_b[:], ident_b[:], start=True, stop=True),
            waits=[("ve", P.cnt("ve"))])

    sems = {s: ctx.enter_context(nc.semaphore(s))
            for s in ("te", "ve", "se", "cs", "xf", "tA", "tB",
                      "cA", "cB", "cC", "cD", "iA", "iB", "iC", "iD",
                      "xA", "xB", "wl", "w0", "w1", "w2", "w3",
                      "mk", "ac", "sA", "sB", "tq")}
    P = Plan()

    def dma(engine, out_ap, in_ap, sname, waits=()):
        P.step(engine, lambda e, o=out_ap, i=in_ap: e.dma_start(out=o, in_=i),
               waits=waits, incs=[(sname, 16)])

    dma("sync", ident_b[:], ident_d[:], "cs")
    dma("sync", identf_b[:], identf_d[:], "cs")
    dma("sync", onesb_b[:], onesb_d[:], "cs")

    ln1_affine_ve = []
    for ps in range(NPASS):
        for q in range(4):
            w = FC // 4
            dma("gpsimd",
                A(CORR0 + q * w * PTOK, w * PTOK)
                .rearrange("p (c t) -> p c t", c=w),
                x_fm_d[ps, :, q * w:(q + 1) * w, :], "xf",
                waits=[("ve", ln1_affine_ve[q] if ps > 0 else P.cnt("ve"))])

        dma("sync", GAMMA[:, 0:4096].rearrange("p (c k) -> p c k", c=FC),
            wtab_d[0], "tA", waits=[("te", P.cnt("te"))])
        dma("sync", GAMMA[:, 4096:8192].rearrange("p (c k) -> p c k", c=FC),
            wtab_d[1], "tB")
        P.step("tensor", waits=[("xf", P.cnt("xf")), ("cs", P.cnt("cs")),
                                ("ve", P.cnt("ve"))])
        for kc in range(2 * KC):
            buf = kc % 2
            tsem = "tA" if buf == 0 else "tB"
            P.step("tensor", waits=[(tsem, P.cnt(tsem)),
                                    ("ve", max(0, P.cnt("ve") - 1))])
            for fc in range(FC):
                P.step("tensor", lambda e, b=buf, f=fc, k=kc: e.matmul(
                    bank[b][:], wtab_blk(b, f), xfm_blk(f),
                    start=(f == 0), stop=(f == FC - 1)),
                    incs=[("te", 1)] if fc == FC - 1 else [])
            if kc + 2 < 2 * KC:
                dma("sync",
                    GAMMA[:, buf * 4096:buf * 4096 + 4096]
                    .rearrange("p (c k) -> p c k", c=FC),
                    wtab_d[kc + 2], tsem, waits=[("te", P.cnt("te"))])
            P.step("vector", lambda e, b=buf, k=kc: e.tensor_copy(
                out=F_at(k), in_=bank[b][:]),
                waits=[("te", P.cnt("te"))], incs=[("ve", 1)])
        ve_F = P.cnt("ve")
        if ps == 0:
            dump("dbg_F", ALPHA[:, 0:F_sz])

        a3_ve = {}
        for kc in range(KC):
            P.step("scalar", lambda e, k=kc: e.activation(
                out=e_scr[:], in_=F_at(k), func=AF.Square),
                waits=[("ve", P.cnt("ve"))], incs=[("se", 1)])
            P.step("scalar", lambda e, k=kc: e.activation(
                out=s2_scr[:], in_=F_at(KC + k), func=AF.Square),
                incs=[("se", 1)])
            P.step("vector", lambda e, k=kc: e.tensor_add(
                PTS[:, k * 512:(k + 1) * 512], e_scr[:], s2_scr[:]),
                waits=[("se", P.cnt("se"))], incs=[("ve", 1)])
            a3_ve[kc] = P.cnt("ve")
        ve_P = P.cnt("ve")

        CGRP = [(0, 8), (8, 8), (16, 1)]
        def a4_tab(ts_, gi_, buf_, w=()):
            k0, ng = CGRP[gi_]
            dma("sync", GAMMA[:, buf_ * 4096: buf_ * 4096 + ng * 512]
                .rearrange("p (k c) -> p k c", k=ng),
                ctab_d[ts_][:, k0:k0 + ng, :], "cA" if buf_ == 0 else "cB", w)
        a4_tab(0, 0, 0, [("te", P.cnt("te"))])
        a4_tab(0, 1, 1)
        pend_evac = []
        gidx = 0
        for ts in range(5):
            bb = (ts % 2) * 4
            for gi in range(3):
                k0, ng = CGRP[gi]
                buf = gidx % 2
                csem = "cA" if buf == 0 else "cB"
                w = [(csem, P.cnt(csem)), ("ve", a3_ve[k0 + ng - 1])]
                if gi == 0 and ts >= 2:
                    w.append(("ve", pend_evac[ts - 2]))
                P.step("tensor", waits=w)
                for kk2 in range(ng):
                    kc = k0 + kk2
                    for t in range(TBP):
                        P.step("tensor", lambda e, tt=t, k=kc, b=buf, j=kk2,
                               bo=bb, s=ts: e.matmul(
                            bank[bo + tt][:, 0:TSW[s]], P_blk(k, tt),
                            ctab_g(b, j)[:, 0:TSW[s]],
                            start=(k == 0), stop=(k == KC - 1)),
                            incs=[("te", 1)] if t == TBP - 1 else [])
                ngi = gidx + 2
                if ngi < 15:
                    nts, ngj = divmod(ngi, 3)
                    a4_tab(nts, ngj, buf, [("te", P.cnt("te"))])
                gidx += 1
            te_now = P.cnt("te")
            for t in range(TBP):
                P.step("vector", lambda e, tt=t, s=ts, bo=bb: e.tensor_copy(
                    out=corr_tile(tt, s, TSW[s]), in_=bank[bo + tt][:, 0:TSW[s]]),
                    waits=[("te", te_now)], incs=[("ve", 1)])
            pend_evac.append(P.cnt("ve"))
        for t in range(TBP):
            P.step("vector", lambda e, tt=t: e.memset(corr_pad(tt), -1e9),
                   waits=[("ve", P.cnt("ve"))], incs=[("ve", 1)])

        ve_corr = P.cnt("ve")
        if ps == 0:
            dump("dbg_corr", A(CORR0, TBP * TAU))
        for t in range(TBP):
            P.step("vector", lambda e, tt=t: e.max(
                out=m8[:], in_=corr_tb(tt)),
                waits=[("ve", ve_corr)], incs=[("ve", 1)])
            warm()
            P.step("vector", lambda e: e.tensor_copy(out=mf[:], in_=m8[:]),
                   waits=[("ve", P.cnt("ve"))], incs=[("ve", 1)])
            P.step("vector", lambda e: e.tensor_scalar_mul(
                sm[:, 0:1], mf[:, 0:1], -1.0),
                waits=[("ve", P.cnt("ve"))], incs=[("ve", 1)])
            P.step("scalar", lambda e: e.activation(
                out=sm[:, 8:16], in_=mf[:], func=AF.Exp,
                bias=sm[:, 0:1], scale=1.0),
                waits=[("ve", P.cnt("ve"))], incs=[("se", 1)])
            P.step("vector", lambda e: e.reduce_sum(
                sm[:, 1:2], sm[:, 8:16], axis=AX),
                waits=[("se", P.cnt("se"))], incs=[("ve", 1)])
            P.step("scalar", lambda e: e.activation(
                out=sm[:, 2:3], in_=sm[:, 1:2], func=AF.Ln),
                waits=[("ve", P.cnt("ve"))], incs=[("se", 1)])
            P.step("vector", lambda e: e.tensor_sub(
                sm[:, 3:4], sm[:, 0:1], sm[:, 2:3]),
                waits=[("se", P.cnt("se"))], incs=[("ve", 1)])
            P.step("vector", lambda e: e.tensor_copy(
                out=sm[:, 4:5], in_=m8[:, 7:8]),
                waits=[("ve", P.cnt("ve"))], incs=[("ve", 1)])
            warm()
            for cchunk in range(5):
                cs = corr_tile(t, cchunk, TSW[cchunk])
                ww = TSW[cchunk]
                P.step("scalar", lambda e, c=cs, w2=ww: e.activation(
                    out=e_scr[:, 0:w2], in_=c, func=AF.Exp,
                    bias=sm[:, 3:4], scale=1.0),
                    waits=[("ve", P.cnt("ve"))], incs=[("se", 1)])
                P.step("vector", lambda e, c=cs, cc=cchunk, w2=ww:
                       e.scalar_tensor_tensor(
                    out=tscr[:, cc * 512:cc * 512 + w2], in0=c,
                    scalar=sm[:, 4:5], in1=e_scr[:, 0:w2],
                    op0=OP.is_ge, op1=OP.mult),
                    waits=[("se", P.cnt("se"))], incs=[("ve", 1)])
                warm()
            ve_s = P.cnt("ve")
            for fc in range(KC):
                b = fc % 2
                w = [("ve", ve_s)] if fc < 2 else [("ve", P.cnt("ve") - 1)]
                P.step("tensor", lambda e, f=fc, bb=b: e.transpose(
                    tr_ps(bb), tscr[:, f * 128:(f + 1) * 128], identf_b[:]),
                    waits=w, incs=[("te", 1)])
                P.step("vector", lambda e, f=fc, bb=b, tt=t: e.tensor_copy(
                    out=sfm_blk(tt, f), in_=tr_ps(bb)),
                    waits=[("te", P.cnt("te"))], incs=[("ve", 1)])
        ve_sfm = P.cnt("ve")
        if ps == 0:
            dump("dbg_sfm", A(CORR0, TBP * TAU))

        dma("sync", GAMMA[:, 0:2176].rearrange("p (c k) -> p c k", c=KC),
            wtab2_d[0], "tA", waits=[("te", P.cnt("te")), ("ve", ve_sfm)])
        dma("sync", GAMMA[:, 2176:4352].rearrange("p (c k) -> p c k", c=KC),
            wtab2_d[1], "tB")

        def sfm_mov(fcc):
            return A(CORR0, TBP * TAU).rearrange("p (t q) -> p t q", q=TAU)[
                :, :, fcc * 128:fcc * 128 + 128]

        P.step("tensor", waits=[("ve", ve_sfm)])
        for kc in range(KC):
            buf = kc % 2
            tsem = "tA" if buf == 0 else "tB"
            P.step("tensor", waits=[(tsem, P.cnt(tsem)),
                                    ("ve", max(0, P.cnt("ve") - 1))])
            for fc in range(KC):
                P.step("tensor", lambda e, b=buf, f=fc, k=kc: e.matmul(
                    bank[k % 2][:], w2_blk(b, f), sfm_mov(f),
                    start=(f == 0), stop=(f == KC - 1)),
                    incs=[("te", 1)] if fc == KC - 1 else [])
            if kc + 2 < KC:
                dma("sync",
                    GAMMA[:, buf * 2176:buf * 2176 + 2176]
                    .rearrange("p (c k) -> p c k", c=KC),
                    wtab2_d[kc + 2], tsem, waits=[("te", P.cnt("te"))])
            te_now = P.cnt("te")
            fre, fim = F_at(kc), F_at(KC + kc)
            P.step("vector", lambda e, a=fre, k=kc: e.tensor_mul(
                a, a, bank[k % 2][:]),
                waits=[("te", te_now), ("ve", P.cnt("ve"))], incs=[("ve", 1)])
            P.step("vector", lambda e, a=fim, k=kc: e.tensor_mul(
                a, a, bank[k % 2][:]), incs=[("ve", 1)])
            ve_u = P.cnt("ve")
        if ps == 0:
            dump("dbg_u", ALPHA[:, 0:F_sz])

        IGRP = [(0, 8), (8, 8), (16, 8), (24, 8), (32, 2)]
        def a9_tab(isl_, gi_, buf_, w=()):
            k0, ng = IGRP[gi_]
            dma("sync", GAMMA[:, 4096 + buf_ * 4096: 4096 + buf_ * 4096 + ng * 512]
                .rearrange("p (k c) -> p k c", k=ng),
                irtab_d[isl_][:, k0:k0 + ng, :], "iA" if buf_ == 0 else "iB", w)
        a9_tab(0, 0, 0, [("te", P.cnt("te")), ("ve", ve_u)])
        a9_tab(0, 1, 1)
        dma("gpsimd", G(0, 2048).rearrange("p (t c) -> p t c", t=TBP),
            x_tm_d[ps, 0], "xA", waits=[("te", P.cnt("te")), ("ve", ve_u)])
        P.step("tensor", waits=[("ve", ve_u)])
        pend_v1 = []
        gidx9 = 0
        for isl in range(8):
            bb9 = (isl % 2) * 4
            for gi in range(5):
                k0, ng = IGRP[gi]
                buf = gidx9 % 2
                isem = "iA" if buf == 0 else "iB"
                w = [(isem, P.cnt(isem))]
                if gi == 0 and isl >= 2:
                    w.append(("ve", pend_v1[isl - 2]))
                P.step("tensor", waits=w)
                for kk2 in range(ng):
                    kc = k0 + kk2
                    for t in range(TBP):
                        P.step("tensor", lambda e, tt=t, k=kc, b=buf, j=kk2,
                               bo=bb9: e.matmul(
                            bank[bo + tt][:],
                            A(k * PTOK + tt * 128, 128), irt_g(b, j),
                            start=(k == 0), stop=(k == 2 * KC - 1)),
                            incs=[("te", 1)] if t == TBP - 1 else [])
                ngi = gidx9 + 2
                if ngi < 40:
                    nisl, ngj = divmod(ngi, 5)
                    a9_tab(nisl, ngj, buf, [("te", P.cnt("te"))])
                gidx9 += 1
            if isl + 1 < 8:
                xsem2 = "xA" if (isl + 1) % 2 == 0 else "xB"
                dma("gpsimd",
                    G(((isl + 1) % 2) * 2048, 2048)
                    .rearrange("p (t c) -> p t c", t=TBP),
                    x_tm_d[ps, isl + 1], xsem2,
                    waits=[("ve", P.cnt("ve"))])
            te_now = P.cnt("te")
            xb = isl % 2
            xsem = "xA" if xb == 0 else "xB"
            for t in range(TBP):
                P.step("vector", lambda e, tt=t, b=xb, i=isl, bo=bb9: e.tensor_add(
                    v1_tile(tt, i), bank[bo + tt][:], xtm_tile(b, tt)),
                    waits=[("te", te_now), (xsem, P.cnt(xsem))],
                    incs=[("ve", 1)])
                P.step("vector", lambda e, tt=t, i=isl: e.reduce_sum(
                    vsum_p[:, tt * 8 + i: tt * 8 + i + 1], v1_tile(tt, i),
                    axis=AX),
                    waits=[("ve", P.cnt("ve"))], incs=[("ve", 1)])
                P.step("scalar", lambda e, tt=t, i=isl: e.activation(
                    out=s2_scr[:], in_=v1_tile(tt, i), func=AF.Square),
                    waits=[("ve", P.cnt("ve"))], incs=[("se", 1)])
                P.step("vector", lambda e, tt=t, i=isl: e.reduce_sum(
                    vsq_p[:, tt * 8 + i: tt * 8 + i + 1], s2_scr[:], axis=AX),
                    waits=[("se", P.cnt("se"))], incs=[("ve", 1)])
            pend_v1.append(P.cnt("ve"))

        if ps == NPASS - 1:
            post_a9 = (P.cnt("te"), P.cnt("ve"))
        if ps == 0:
            dump("dbg_v1", A(CORR0, TBP * D))
        for t in range(TBP):
            g = ps * TBP + t
            P.step("vector", lambda e, tt=t: e.reduce_sum(
                sm[:, 5:6], vsum_p[:, tt * 8:(tt + 1) * 8], axis=AX),
                waits=[("ve", P.cnt("ve"))], incs=[("ve", 1)])
            P.step("vector", lambda e, tt=t: e.reduce_sum(
                sm[:, 6:7], vsq_p[:, tt * 8:(tt + 1) * 8], axis=AX),
                incs=[("ve", 1)])
            P.step("vector", lambda e: e.tensor_scalar_mul(
                sm[:, 5:6], sm[:, 5:6], 1.0 / D),
                waits=[("ve", P.cnt("ve"))], incs=[("ve", 1)])
            P.step("vector", lambda e: e.tensor_scalar_mul(
                sm[:, 6:7], sm[:, 6:7], 1.0 / D), incs=[("ve", 1)])
            P.step("vector", lambda e: e.tensor_mul(
                sm[:, 7:8], sm[:, 5:6], sm[:, 5:6]),
                waits=[("ve", P.cnt("ve"))], incs=[("ve", 1)])
            P.step("vector", lambda e: e.tensor_sub(
                sm[:, 7:8], sm[:, 6:7], sm[:, 7:8]),
                waits=[("ve", P.cnt("ve"))], incs=[("ve", 1)])
            P.step("vector", lambda e: e.tensor_scalar_add(
                sm[:, 7:8], sm[:, 7:8], EPS),
                waits=[("ve", P.cnt("ve"))], incs=[("ve", 1)])
            P.step("vector", lambda e: e.reciprocal(
                sm[:, 8:9], sm[:, 7:8]),
                waits=[("ve", P.cnt("ve"))], incs=[("ve", 1)])
            warm()
            P.step("scalar", lambda e: e.activation(
                out=sm[:, 9:10], in_=sm[:, 8:9], func=AF.Sqrt),
                waits=[("ve", P.cnt("ve"))], incs=[("se", 1)])
            P.step("vector", lambda e: e.tensor_mul(
                sm[:, 10:11], sm[:, 5:6], sm[:, 9:10]),
                waits=[("se", P.cnt("se"))], incs=[("ve", 1)])
            P.step("vector", lambda e: e.tensor_scalar_mul(
                sm[:, 10:11], sm[:, 10:11], -1.0),
                waits=[("ve", P.cnt("ve"))], incs=[("ve", 1)])
            P.step("vector", lambda e, tt=t: e.tensor_scalar(
                out=x1t[:], in0=v1_row(tt), scalar1=sm[:, 9:10],
                scalar2=sm[:, 10:11], op0=OP.mult, op1=OP.add),
                waits=[("ve", P.cnt("ve"))], incs=[("ve", 1)])
            if ps == 0:
                ln1_affine_ve.append(P.cnt("ve"))
            warm()
            ve_x1 = P.cnt("ve")
            for fc in range(FC):
                b = fc % 2
                w = [("ve", ve_x1)] if fc < 2 else [("ve", P.cnt("ve") - 1)]
                P.step("tensor", lambda e, f=fc, bb=b: e.transpose(
                    tr_ps(bb), x1t[:, f * 128:(f + 1) * 128], identf_b[:]),
                    waits=w, incs=[("te", 1)])
                P.step("vector", lambda e, f=fc, bb=b, gg=g: e.tensor_copy(
                    out=x1_blk(gg, f), in_=tr_ps(bb)),
                    waits=[("te", P.cnt("te"))], incs=[("ve", 1)])
    ve_x1fm = P.cnt("ve")
    if debug:
        dump("dbg_x1", BETA[:, :])

    dma("gpsimd", PTS[:, 0:2048].rearrange("p (c r) -> p c r", c=FC),
        acat_d[:], "ac", waits=[("ve", ve_x1fm), ("te", P.cnt("te"))])

    def x1_mov(th, fcc):
        return BETA[:, :].rearrange("p (g q) -> p g q", q=D)[
            :, th * 4:(th + 1) * 4, fcc * 128:fcc * 128 + 128]

    if debug:
        P.step("sync", waits=[("ac", P.cnt("ac"))])
        dump("dbg_acat", PTS[:, 0:2048])
    P.step("tensor", waits=[("ve", ve_x1fm), ("ac", P.cnt("ac"))])
    for l in range(4):
        for th in range(2):
            bk = bank[l * 2 + th][0:16, :]
            for fc in range(FC):
                P.step("tensor", lambda e, g=bk, ll=l, t=th, f=fc: e.matmul(
                    g, acat_blk(f, ll), x1_mov(t, f),
                    start=(f == 0), stop=(f == FC - 1)),
                    incs=[("te", 1)] if fc == FC - 1 else [])
    te_h = P.cnt("te")
    for l in range(4):
        for th in range(2):
            P.step("vector", lambda e, ll=l, t=th: e.tensor_copy(
                out=h_b[:, ll * NTOK + t * 512: ll * NTOK + t * 512 + 512],
                in_=bank[ll * 2 + t][0:16, :]),
                waits=[("te", te_h)], incs=[("ve", 1)])
    ve_h = P.cnt("ve")
    if debug:
        dump("dbg_h", h_b[:, :])


    def b_quarters(o8, extra_waits=()):
        for q in range(4):
            w = list(extra_waits) if q == 0 else []
            if o8 >= 1:
                w.append(("tq", 4 * (o8 - 1) + q + 1))
            dma("gpsimd",
                GAMMA[:, q * 4096:(q + 1) * 4096]
                .rearrange("p (c k) -> p c k", c=8),
                w0t_d[o8, :, q * 8:(q + 1) * 8, :], f"w{q}", waits=w)

    def b_small(o8, waits_b):
        dma("gpsimd", b0_b[:, o8 % 2], b0_d[o8], "wl", waits=waits_b)
        dma("gpsimd", blt_b[:, o8 % 2].rearrange("q (l c) -> q l c", l=4),
            blt_d[o8], "wl")

    def b_masks(o8, waits_m):
        for i in range(3):
            dma("sync", mkt(i).rearrange("p (t c) -> p t c", t=TB),
                m_d[i][o8], "mk", waits=waits_m if i == 0 else ())

    b_quarters(0, extra_waits=[("te", post_a9[0]), ("ve", post_a9[1])])
    b_small(0, ())
    b_masks(0, [("te", P.cnt("te"))])

    SETS = [bank[0:4], bank[4:8]]
    NH = NO8 * 2
    evac_ve = {}
    sel_ve = {}
    lora_te = {}
    tail_te = {}

    def app_lora(k, gi):
        o8, h = k // 2, k % 2
        g = h * 4 + gi
        st = SETS[k % 2]
        if gi == 0:
            w = [("ve", evac_ve[k]), ("wl", 32 * (o8 + 1))]
        else:
            w = [("ve", sel_ve[(k, gi - 1)])]
        P.step("tensor", waits=w)
        for l in range(4):
            P.step("tensor", lambda e, ll=l, gg=g, o=o8, s=st: e.matmul(
                s[ll][:],
                h_b[:, ll * NTOK + gg * 128: ll * NTOK + gg * 128 + 128],
                blt_b[:, o % 2, ll * OSLAB:(ll + 1) * OSLAB],
                start=True, stop=True),
                incs=[("te", 1)] if l == 3 else [])
        lora_te[(k, gi)] = P.cnt("te")

    def app_sel(k, gi):
        o8, h = k // 2, k % 2
        g = h * 4 + gi
        st = SETS[k % 2]
        P.step("vector", lambda e, s=st: e.tensor_copy(
            out=acc_v[:], in_=s[0][:]),
            waits=[("te", lora_te[(k, gi)]), ("mk", 48 * (o8 + 1)),
                   ("ve", P.cnt("ve"))], incs=[("ve", 1)])
        for i in range(3):
            P.step("vector", lambda e, ii=i, gg=g, s=st: e.copy_predicated(
                out=acc_v[:],
                mask=mkt(ii)[:, gg * OSLAB:(gg + 1) * OSLAB],
                data=s[1 + ii][:]),
                waits=[("ve", P.cnt("ve"))], incs=[("ve", 1)])
        sel_ve[(k, gi)] = P.cnt("ve")
        P.step("vector", lambda e, gg=g, o=o8: e.scalar_tensor_tensor(
            out=v_tile(gg, o), in0=acc_v[:], scalar=8.0, in1=v_tile(gg, o),
            op0=OP.mult, op1=OP.add),
            waits=[("ve", P.cnt("ve"))], incs=[("ve", 1)])
        P.step("vector", lambda e, gg=g, o=o8: e.reduce_sum(
            vsum_p[:, gg * 8 + o: gg * 8 + o + 1], v_tile(gg, o), axis=AX),
            waits=[("ve", P.cnt("ve"))], incs=[("ve", 1)])
        P.step("scalar", lambda e, gg=g, o=o8: e.activation(
            out=s2_scr[:], in_=v_tile(gg, o), func=AF.Square),
            waits=[("ve", P.cnt("ve"))], incs=[("se", 1)])
        P.step("vector", lambda e, gg=g, o=o8: e.reduce_sum(
            vsq_p[:, gg * 8 + o: gg * 8 + o + 1], s2_scr[:], axis=AX),
            waits=[("se", P.cnt("se"))], incs=[("ve", 1)])

    for k in range(NH):
        o8, h = k // 2, k % 2
        st = SETS[k % 2]
        P.step("tensor",
               waits=[("ve", ve_h if k < 2 else sel_ve[(k - 2, 3)])])
        for fc in range(FC):
            if h == 0 and fc % 8 == 0:
                P.step("tensor", waits=[(f"w{fc // 8}", 16 * (o8 + 1))])
            for gi in range(4):
                g = h * 4 + gi
                inc = [("tq", 1)] if (h == 1 and fc % 8 == 7 and gi == 3) \
                    else []
                P.step("tensor", lambda e, gg=g, f=fc, s=st, i=gi: e.matmul(
                    s[i][:], x1_blk(gg, f), w0_blk(f),
                    start=(f == 0), stop=False), incs=inc)
            if k > 0 and fc % 8 == 7:
                app_lora(k - 1, fc // 8)
                app_sel(k - 1, fc // 8)
        P.step("tensor", waits=[("wl", 32 * (o8 + 1))])
        for gi in range(4):
            g = h * 4 + gi
            P.step("tensor", lambda e, o=o8, s=st, i=gi: e.matmul(
                s[i][:], onesb_b[:], b0_b[:, o % 2],
                start=False, stop=False))
            for j in range(4):
                P.step("tensor", lambda e, gg=g, o=o8, jj=j, s=st, i=gi:
                       e.matmul(
                           s[i][:, jj * 128:(jj + 1) * 128],
                           x1_blk(gg, 4 * o + jj), ident_b[:],
                           start=False, stop=(jj == 3)),
                       incs=[("te", 1)] if j == 3 else [])
        tail_te[k] = P.cnt("te")
        for gi in range(4):
            g = h * 4 + gi
            P.step("vector", lambda e, gg=g, o=o8, s=st, i=gi: e.tensor_copy(
                out=v_tile(gg, o), in_=s[i][:]),
                waits=[("te", tail_te[k]), ("ve", P.cnt("ve"))],
                incs=[("ve", 1)])
        evac_ve[k] = P.cnt("ve")
        if h == 0 and o8 + 1 < NO8:
            b_quarters(o8 + 1)
            b_small(o8 + 1, [("te", P.cnt("te"))])
        if h == 0 and o8 >= 1:
            b_masks(o8, [("ve", P.cnt("ve"))])
    for gi in range(4):
        app_lora(NH - 1, gi)
        app_sel(NH - 1, gi)

    te_b = P.cnt("te")
    for g in range(TB):
        P.step("vector", lambda e, gg=g: e.reduce_sum(
            sm[:, 5:6], vsum_p[:, gg * 8:(gg + 1) * 8], axis=AX),
            waits=[("ve", P.cnt("ve")), ("te", te_b)], incs=[("ve", 1)])
        P.step("vector", lambda e, gg=g: e.reduce_sum(
            sm[:, 6:7], vsq_p[:, gg * 8:(gg + 1) * 8], axis=AX),
            incs=[("ve", 1)])
        P.step("vector", lambda e: e.tensor_scalar_mul(
            sm[:, 5:6], sm[:, 5:6], 1.0 / D),
            waits=[("ve", P.cnt("ve"))], incs=[("ve", 1)])
        P.step("vector", lambda e: e.tensor_scalar_mul(
            sm[:, 6:7], sm[:, 6:7], 1.0 / D), incs=[("ve", 1)])
        P.step("vector", lambda e: e.tensor_mul(
            sm[:, 7:8], sm[:, 5:6], sm[:, 5:6]),
            waits=[("ve", P.cnt("ve"))], incs=[("ve", 1)])
        P.step("vector", lambda e: e.tensor_sub(
            sm[:, 7:8], sm[:, 6:7], sm[:, 7:8]),
            waits=[("ve", P.cnt("ve"))], incs=[("ve", 1)])
        P.step("vector", lambda e: e.tensor_scalar_add(
            sm[:, 7:8], sm[:, 7:8], EPS),
            waits=[("ve", P.cnt("ve"))], incs=[("ve", 1)])
        P.step("vector", lambda e: e.reciprocal(
            sm[:, 8:9], sm[:, 7:8]),
            waits=[("ve", P.cnt("ve"))], incs=[("ve", 1)])
        P.step("scalar", lambda e: e.activation(
            out=sm[:, 9:10], in_=sm[:, 8:9], func=AF.Sqrt),
            waits=[("ve", P.cnt("ve"))], incs=[("se", 1)])
        P.step("vector", lambda e: e.tensor_mul(
            sm[:, 10:11], sm[:, 5:6], sm[:, 9:10]),
            waits=[("se", P.cnt("se"))], incs=[("ve", 1)])
        P.step("vector", lambda e: e.tensor_scalar_mul(
            sm[:, 10:11], sm[:, 10:11], -1.0),
            waits=[("ve", P.cnt("ve"))], incs=[("ve", 1)])
        b = g % 2
        ssem = "sA" if b == 0 else "sB"
        w = [("ve", P.cnt("ve"))]
        if g >= 2:
            w.append((ssem, P.cnt(ssem)))
        P.step("vector", lambda e, gg=g, bb=b: e.tensor_scalar(
            out=out_view(bb), in0=v_row(gg), scalar1=sm[:, 9:10],
            scalar2=sm[:, 10:11], op0=OP.mult, op1=OP.add),
            waits=w, incs=[("ve", 1)])
        dma("sync", out_d[g], out_view(b), ssem, waits=[("ve", P.cnt("ve"))])

    P.step("sync", waits=[("sA", P.cnt("sA")), ("sB", P.cnt("sB")),
                          ("wl", P.cnt("wl")), ("xf", P.cnt("xf"))])

    with nc.Block() as block:
        @block.sync
        def _(eng):
            P.emit("sync", eng, sems)

        @block.gpsimd
        def _(eng):
            P.emit("gpsimd", eng, sems)

        @block.tensor
        def _(eng):
            P.emit("tensor", eng, sems)

        @block.vector
        def _(eng):
            P.emit("vector", eng, sems)

        @block.scalar
        def _(eng):
            P.emit("scalar", eng, sems)

    ctx.close()
    return nc


_NC = {}


def kernel(L0_attn_x, L0_ffn_W0, L0_ffn_b0, L0_ffn_A, L0_ffn_B):
    x = np.asarray(L0_attn_x, dtype=np.float32)
    w0 = np.asarray(L0_ffn_W0, dtype=np.float32)
    b0 = np.asarray(L0_ffn_b0, dtype=np.float32)
    a_w = np.asarray(L0_ffn_A, dtype=np.float32)
    b_w = np.asarray(L0_ffn_B, dtype=np.float32)
    if "nc" not in _NC:
        _NC["nc"] = build_program()
    in_maps = [_core_inputs(c, x, w0, b0, a_w, b_w) for c in range(N_CORES)]
    res = run_bass_kernel_spmd(_NC["nc"], in_maps, core_ids=list(range(N_CORES)))
    out = np.concatenate([r["out"].reshape(-1, D) for r in res.results], axis=0)
    return out.reshape(1, SEQ, D).astype(np.float32)


# version 17
# speedup vs baseline: 1.2310x; 1.0024x over previous
import numpy as np
import ml_dtypes

import concourse.bass as bass
import concourse.mybir as mybir
from concourse.bass_utils import run_bass_kernel_spmd

F32 = mybir.dt.float32
BF16 = mybir.dt.bfloat16
U8 = mybir.dt.uint8
AX = mybir.AxisListType.X
AF = mybir.ActivationFunctionType
OP = mybir.AluOpType
BF = ml_dtypes.bfloat16

N_CORES = 8
SEQ = 8192
D = 4096
KHALF = 2049
KPAD = 2176
KC = KPAD // 128
FC = D // 128
OSLAB = 512
NO8 = D // OSLAB
EPS = 1e-5
NPASS = 2
NTOK = SEQ // N_CORES
PTOK = NTOK // NPASS
TBP = PTOK // 128
TB = NTOK // 128

_CONST = {}


def _constants():
    if _CONST:
        return _CONST
    k = np.arange(KPAD, dtype=np.float64)
    f = np.arange(D, dtype=np.float64)
    valid = (k < KHALF).astype(np.float64)
    ang = 2.0 * np.pi * np.outer(f, k) / D
    wc = np.cos(ang) * valid
    ws = -np.sin(ang) * valid
    ck = np.where((k == 0) | (k == D // 2), 1.0, 2.0) * valid / D
    tau = np.arange(D, dtype=np.float64)
    iang = 2.0 * np.pi * np.outer(k, tau) / D
    ctab = np.cos(iang) * ck[:, None]
    vc = np.cos(iang) * ck[:, None]
    vs = -np.sin(iang) * ck[:, None]

    def fm_chunks(tab):
        out = np.empty((KC, 128, FC, 128), dtype=BF)
        for j in range(KC):
            out[j] = tab[:, j * 128:(j + 1) * 128].reshape(FC, 128, 128) \
                .transpose(1, 0, 2)
        return out

    wtab = np.ascontiguousarray(
        np.concatenate([fm_chunks(wc), fm_chunks(ws)], axis=0))

    def tm_chunks(tab):
        out = np.empty((8, KC, 128, 512), dtype=BF)
        for sl in range(8):
            for j in range(KC):
                out[sl, j] = tab[j * 128:(j + 1) * 128,
                                 sl * 512:(sl + 1) * 512]
        return out

    TAU = 2176
    ctab_h = np.zeros((KPAD, TAU), dtype=np.float64)
    ctab_h[:, :2049] = ctab[:, :2049]
    ctab_c = np.empty((5, KC, 128, 512), dtype=BF)
    for sl in range(5):
        w = 512 if sl < 4 else 128
        for j in range(KC):
            ctab_c[sl, j, :, :w] = ctab_h[j * 128:(j + 1) * 128,
                                          sl * 512:sl * 512 + w]
            if w < 512:
                ctab_c[sl, j, :, w:] = 0
    ctab_c = np.ascontiguousarray(ctab_c.transpose(0, 2, 1, 3))
    cpf = np.where((f == 0) | (f == D // 2), 1.0, 2.0) * (f < 2049)
    wc2 = np.cos(ang) * valid * cpf[:, None]
    wtab2 = np.ascontiguousarray(fm_chunks(wc2)[:KC, :, :KC, :])
    irtab = np.ascontiguousarray(
        np.concatenate([tm_chunks(vc), tm_chunks(vs)], axis=1)
        .transpose(0, 2, 1, 3))

    ident = np.ascontiguousarray(np.eye(128), dtype=BF)
    onesb = np.ones((1, 128), dtype=BF)

    import jax
    with jax.default_device(jax.devices("cpu")[0]):
        u = np.asarray(jax.random.uniform(
            jax.random.key(42, impl="threefry2x32"), (1, SEQ, D)))[0]
    masks = [((u >= 0.25 * l) & (u < 0.25 * (l + 1))).astype(np.uint8)
             for l in (1, 2, 3)]

    _CONST.update(wtab=wtab, wtab2=wtab2, ctab=ctab_c, irtab=irtab,
                  ident=ident, onesb=onesb, masks=masks)
    return _CONST


def _core_inputs(core, x, w0, b0, a_w, b_w):
    c = _constants()
    t0 = core * NTOK
    xs = x[0, t0:t0 + NTOK]
    x_fm = np.ascontiguousarray(
        xs.reshape(NPASS, PTOK, D).transpose(0, 2, 1)
        .reshape(NPASS, FC, 128, PTOK).transpose(0, 2, 1, 3))
    x_tm = np.ascontiguousarray(
        xs.reshape(NPASS, TBP, 128, 8, 512).transpose(0, 3, 2, 1, 4))
    w0t = np.ascontiguousarray(
        w0.T.reshape(FC, 128, NO8, OSLAB).transpose(2, 1, 0, 3))
    acat = np.ascontiguousarray(
        a_w.transpose(2, 0, 1).reshape(FC, 128, 64).transpose(1, 0, 2))
    blt = np.ascontiguousarray(
        b_w.transpose(0, 2, 1).reshape(4, 16, NO8, OSLAB)
        .transpose(2, 1, 0, 3))
    m_sl = [np.ascontiguousarray(
        m[t0:t0 + NTOK].reshape(TB, 128, NO8, OSLAB).transpose(2, 1, 0, 3))
        for m in c["masks"]]
    b0sl = np.ascontiguousarray(b0.reshape(NO8, 1, OSLAB))
    return dict(
        x_fm=x_fm.astype(np.float32), x_tm=x_tm.astype(np.float32),
        w0t=w0t.astype(np.float32), b0sl=b0sl.astype(np.float32),
        acat=acat.astype(np.float32), blt=blt.astype(np.float32),
        m1=m_sl[0], m2=m_sl[1], m3=m_sl[2],
        wtab=c["wtab"], wtab2=c["wtab2"], ctab=c["ctab"], irtab=c["irtab"],
        ident=c["ident"], identf=np.eye(128, dtype=np.float32),
        onesb=c["onesb"],
    )


class Plan:
    def __init__(self):
        self.steps = []
        self.counts = {}

    def cnt(self, s):
        return self.counts.get(s, 0)

    def step(self, engine, fn=None, waits=(), incs=()):
        self.steps.append((engine, list(waits), fn, list(incs)))
        for s, n in incs:
            self.counts[s] = self.counts.get(s, 0) + n

    def emit(self, engine_name, eng, sems):
        for e, waits, fn, incs in self.steps:
            if e != engine_name:
                continue
            for s, v in waits:
                if v > 0:
                    eng.wait_ge(sems[s], v)
            if fn is None:
                continue
            instr = fn(eng)
            for s, n in incs:
                instr.then_inc(sems[s], n)


def build_program(debug=False):
    nc = bass.Bass("TRN2", target_bir_lowering=False, debug=False)

    din = lambda name, shape, dt: nc.dram_tensor(
        name, list(shape), dt, kind="ExternalInput").ap()
    x_fm_d = din("x_fm", [NPASS, 128, FC, PTOK], F32)
    x_tm_d = din("x_tm", [NPASS, 8, 128, TBP, 512], F32)
    w0t_d = din("w0t", [NO8, 128, FC, OSLAB], F32)
    b0_d = din("b0sl", [NO8, 1, OSLAB], F32)
    acat_d = din("acat", [128, FC, 64], F32)
    blt_d = din("blt", [NO8, 16, 4, OSLAB], F32)
    m_d = [din(f"m{l}", [NO8, 128, TB, OSLAB], U8) for l in (1, 2, 3)]
    wtab_d = din("wtab", [2 * KC, 128, FC, 128], BF16)
    ctab_d = din("ctab", [5, 128, KC, 512], BF16)
    wtab2_d = din("wtab2", [KC, 128, KC, 128], BF16)
    irtab_d = din("irtab", [8, 128, 2 * KC, 512], BF16)
    ident_d = din("ident", [128, 128], BF16)
    identf_d = din("identf", [128, 128], F32)
    onesb_d = din("onesb", [1, 128], BF16)
    out_d = nc.dram_tensor("out", [TB, 128, D], F32, kind="ExternalOutput").ap()
    dbg = {}
    if debug:
        for nm, shape in [("dbg_F", [128, 2 * KC * PTOK]),
                          ("dbg_corr", [128, TBP * 2176]),
                          ("dbg_sfm", [128, TBP * 2176]),
                          ("dbg_u", [128, 2 * KC * PTOK]),
                          ("dbg_v1", [128, TBP * D]),
                          ("dbg_x1", [128, TB * D]),
                          ("dbg_h", [16, 4 * NTOK]),
                          ("dbg_acat", [128, 2048]),
                          ("dbg_v", [128, TB * D])]:
            dbg[nm] = nc.dram_tensor(nm, shape, BF16, kind="ExternalOutput").ap()

    def dump(nm, src_ap, eng="sync"):
        if debug:
            P.step(eng, lambda e, o=dbg[nm], i=src_ap: e.dma_start(out=o, in_=i),
                   waits=[("ve", P.cnt("ve")), ("te", P.cnt("te")),
                          ("se", P.cnt("se"))], incs=[("sA", 16)])
            for en in ("vector", "tensor", "scalar", "gpsimd"):
                P.step(en, waits=[("sA", P.cnt("sA"))])

    from contextlib import ExitStack
    ctx = ExitStack()
    sb = lambda n, s, dt: ctx.enter_context(nc.sbuf_tensor(n, list(s), dt))
    psT = lambda n, s, dt=F32: ctx.enter_context(nc.psum_tensor(n, list(s), dt))

    ALPHA = sb("ALPHA", [128, 34816], BF16)
    BETA = sb("BETA", [128, 32768], BF16)
    GAMMA = sb("GAMMA", [128, 16384], BF16)
    PTS = sb("PTS", [128, 8704], BF16)
    h_b = sb("h_b", [16, 4 * NTOK], BF16)
    e_scr = sb("e_scr", [128, 512], BF16)
    s2_scr = sb("s2_scr", [128, 512], BF16)
    blt_b = sb("blt_b", [16, 2, 4 * OSLAB], BF16)
    b0_b = sb("b0_b", [1, 2, OSLAB], BF16)
    ctab_b = sb("ctab_b", [128, 4, 512], BF16)
    ident_b = sb("ident_b", [128, 128], BF16)
    identf_b = sb("identf_b", [128, 128], F32)
    onesb_b = sb("onesb_b", [1, 128], BF16)
    m8 = sb("m8", [128, 8], BF16)
    mf = sb("mf", [128, 8], F32)
    sm = sb("sm", [128, 16], F32)
    vsum_p = sb("vsum_p", [128, 64], F32)
    vsq_p = sb("vsq_p", [128, 64], F32)

    def A(o, n, dt=BF16):
        v = ALPHA[:, o:o + n]
        return v.bitcast(dt) if dt != BF16 else v

    def G(o, n, dt=BF16):
        v = GAMMA[:, o:o + n]
        return v.bitcast(dt) if dt != BF16 else v

    def PT(o, n, dt=BF16):
        v = PTS[:, o:o + n]
        return v.bitcast(dt) if dt != BF16 else v

    F_sz = 2 * KC * PTOK
    F_at = lambda kc: A(kc * PTOK, PTOK)
    CORR0 = F_sz
    TAU = 2176
    TSW = [512, 512, 512, 512, 128]
    corr_tb = lambda t: A(CORR0 + t * TAU, TAU)
    corr_tile = lambda t, sl, w: A(CORR0 + t * TAU + sl * 512, w)
    corr_pad = lambda t: A(CORR0 + t * TAU + 2049, 127)
    sfm_blk = lambda t, fcc: A(CORR0 + t * TAU + fcc * 128, 128)
    v1_tile = lambda t, isl: A(CORR0 + t * D + isl * 512, 512)
    v1_row = lambda t: A(CORR0 + t * D, D)
    xfm_blk = lambda fcc: A(CORR0 + fcc * PTOK, PTOK)
    v_tile = lambda g, o8: A((g * 8 + o8) * 512, 512)
    v_row = lambda g: A(g * D, D)
    acat_blk = lambda fcc, l: PT(fcc * 64 + l * 16, 16)

    wtab_blk = lambda buf, fcc: G(buf * 4096 + fcc * 128, 128)
    w2_blk = lambda buf, fcc: G(buf * 2176 + fcc * 128, 128)
    xtm_tile = lambda buf, t: G(buf * 2048 + t * 512, 512)
    irt_g = lambda buf, j: G(4096 + buf * 4096 + j * 512, 512)
    ctab_g = lambda buf, j: G(buf * 4096 + j * 512, 512)
    w0_blk = lambda fcc: G(fcc * 512, 512)
    out_view = lambda buf: G(buf * 8192, 8192, F32)

    P_blk = lambda kc, t: PT(kc * 512 + t * 128, 128)
    tscr = PT(0, 8192, F32)
    mscr = lambda i: PT(i * 1024, 1024, F32)
    x1t = PT(0, 8192, F32)
    acc_v = PT(0, 1024, F32)
    mkt = lambda i: PTS[:, 2048 + i * 2048: 2048 + (i + 1) * 2048].bitcast(U8)

    x1_blk = lambda g, fcc: BETA[:, g * D + fcc * 128: g * D + fcc * 128 + 128]

    bank = [psT(f"bk{i}", [128, 512]) for i in range(8)]
    tr_ps = lambda b: bank[4 + b][:, 0:128]

    def warm():
        P.step("tensor", lambda e: e.matmul(
            bank[6][:, 0:128], ident_b[:], ident_b[:], start=True, stop=True),
            waits=[("ve", P.cnt("ve"))])

    sems = {s: ctx.enter_context(nc.semaphore(s))
            for s in ("te", "ve", "se", "cs", "xf", "tA", "tB",
                      "cA", "cB", "cC", "cD", "iA", "iB", "iC", "iD",
                      "xA", "xB", "wl", "w0", "w1", "w2", "w3",
                      "mk", "ac", "sA", "sB", "tq")}
    P = Plan()

    def dma(engine, out_ap, in_ap, sname, waits=()):
        P.step(engine, lambda e, o=out_ap, i=in_ap: e.dma_start(out=o, in_=i),
               waits=waits, incs=[(sname, 16)])

    dma("sync", ident_b[:], ident_d[:], "cs")
    dma("sync", identf_b[:], identf_d[:], "cs")
    dma("sync", onesb_b[:], onesb_d[:], "cs")

    ln1_affine_ve = []
    for ps in range(NPASS):
        for q in range(4):
            w = FC // 4
            dma("gpsimd",
                A(CORR0 + q * w * PTOK, w * PTOK)
                .rearrange("p (c t) -> p c t", c=w),
                x_fm_d[ps, :, q * w:(q + 1) * w, :], "xf",
                waits=[("ve", ln1_affine_ve[q] if ps > 0 else P.cnt("ve"))])

        dma("sync", GAMMA[:, 0:4096].rearrange("p (c k) -> p c k", c=FC),
            wtab_d[0], "tA", waits=[("te", P.cnt("te"))])
        dma("sync", GAMMA[:, 4096:8192].rearrange("p (c k) -> p c k", c=FC),
            wtab_d[1], "tB")
        P.step("tensor", waits=[("xf", P.cnt("xf")), ("cs", P.cnt("cs")),
                                ("ve", P.cnt("ve"))])
        for kc in range(2 * KC):
            buf = kc % 2
            tsem = "tA" if buf == 0 else "tB"
            P.step("tensor", waits=[(tsem, P.cnt(tsem)),
                                    ("ve", max(0, P.cnt("ve") - 1))])
            for fc in range(FC):
                P.step("tensor", lambda e, b=buf, f=fc, k=kc: e.matmul(
                    bank[b][:], wtab_blk(b, f), xfm_blk(f),
                    start=(f == 0), stop=(f == FC - 1)),
                    incs=[("te", 1)] if fc == FC - 1 else [])
            if kc + 2 < 2 * KC:
                dma("sync",
                    GAMMA[:, buf * 4096:buf * 4096 + 4096]
                    .rearrange("p (c k) -> p c k", c=FC),
                    wtab_d[kc + 2], tsem, waits=[("te", P.cnt("te"))])
            P.step("vector", lambda e, b=buf, k=kc: e.tensor_copy(
                out=F_at(k), in_=bank[b][:]),
                waits=[("te", P.cnt("te"))], incs=[("ve", 1)])
        ve_F = P.cnt("ve")
        if ps == 0:
            dump("dbg_F", ALPHA[:, 0:F_sz])

        a3_ve = {}
        for kc in range(KC):
            P.step("scalar", lambda e, k=kc: e.activation(
                out=e_scr[:], in_=F_at(k), func=AF.Square),
                waits=[("ve", P.cnt("ve"))], incs=[("se", 1)])
            P.step("scalar", lambda e, k=kc: e.activation(
                out=s2_scr[:], in_=F_at(KC + k), func=AF.Square),
                incs=[("se", 1)])
            P.step("vector", lambda e, k=kc: e.tensor_add(
                PTS[:, k * 512:(k + 1) * 512], e_scr[:], s2_scr[:]),
                waits=[("se", P.cnt("se"))], incs=[("ve", 1)])
            a3_ve[kc] = P.cnt("ve")
        ve_P = P.cnt("ve")

        CGRP = [(0, 8), (8, 8), (16, 1)]
        def a4_tab(ts_, gi_, buf_, w=()):
            k0, ng = CGRP[gi_]
            dma("sync", GAMMA[:, buf_ * 4096: buf_ * 4096 + ng * 512]
                .rearrange("p (k c) -> p k c", k=ng),
                ctab_d[ts_][:, k0:k0 + ng, :], "cA" if buf_ == 0 else "cB", w)
        a4_tab(0, 0, 0, [("te", P.cnt("te"))])
        a4_tab(0, 1, 1)
        pend_evac = []
        gidx = 0
        for ts in range(5):
            bb = (ts % 2) * 4
            for gi in range(3):
                k0, ng = CGRP[gi]
                buf = gidx % 2
                csem = "cA" if buf == 0 else "cB"
                w = [(csem, P.cnt(csem)), ("ve", a3_ve[k0 + ng - 1])]
                if gi == 0 and ts >= 2:
                    w.append(("ve", pend_evac[ts - 2]))
                P.step("tensor", waits=w)
                for kk2 in range(ng):
                    kc = k0 + kk2
                    for t in range(TBP):
                        P.step("tensor", lambda e, tt=t, k=kc, b=buf, j=kk2,
                               bo=bb, s=ts: e.matmul(
                            bank[bo + tt][:, 0:TSW[s]], P_blk(k, tt),
                            ctab_g(b, j)[:, 0:TSW[s]],
                            start=(k == 0), stop=(k == KC - 1)),
                            incs=[("te", 1)] if t == TBP - 1 else [])
                ngi = gidx + 2
                if ngi < 15:
                    nts, ngj = divmod(ngi, 3)
                    a4_tab(nts, ngj, buf, [("te", P.cnt("te"))])
                gidx += 1
            te_now = P.cnt("te")
            for t in range(TBP):
                P.step("vector", lambda e, tt=t, s=ts, bo=bb: e.tensor_copy(
                    out=corr_tile(tt, s, TSW[s]), in_=bank[bo + tt][:, 0:TSW[s]]),
                    waits=[("te", te_now)], incs=[("ve", 1)])
            pend_evac.append(P.cnt("ve"))
        for t in range(TBP):
            P.step("vector", lambda e, tt=t: e.memset(corr_pad(tt), -1e9),
                   waits=[("ve", P.cnt("ve"))], incs=[("ve", 1)])

        ve_corr = P.cnt("ve")
        if ps == 0:
            dump("dbg_corr", A(CORR0, TBP * TAU))
        for t in range(TBP):
            P.step("vector", lambda e, tt=t: e.max(
                out=m8[:], in_=corr_tb(tt)),
                waits=[("ve", ve_corr)], incs=[("ve", 1)])
            warm()
            P.step("vector", lambda e: e.tensor_copy(out=mf[:], in_=m8[:]),
                   waits=[("ve", P.cnt("ve"))], incs=[("ve", 1)])
            P.step("vector", lambda e: e.tensor_scalar_mul(
                sm[:, 0:1], mf[:, 0:1], -1.0),
                waits=[("ve", P.cnt("ve"))], incs=[("ve", 1)])
            P.step("scalar", lambda e: e.activation(
                out=sm[:, 8:16], in_=mf[:], func=AF.Exp,
                bias=sm[:, 0:1], scale=1.0),
                waits=[("ve", P.cnt("ve"))], incs=[("se", 1)])
            P.step("vector", lambda e: e.reduce_sum(
                sm[:, 1:2], sm[:, 8:16], axis=AX),
                waits=[("se", P.cnt("se"))], incs=[("ve", 1)])
            P.step("scalar", lambda e: e.activation(
                out=sm[:, 2:3], in_=sm[:, 1:2], func=AF.Ln),
                waits=[("ve", P.cnt("ve"))], incs=[("se", 1)])
            P.step("vector", lambda e: e.tensor_sub(
                sm[:, 3:4], sm[:, 0:1], sm[:, 2:3]),
                waits=[("se", P.cnt("se"))], incs=[("ve", 1)])
            P.step("vector", lambda e: e.tensor_copy(
                out=sm[:, 4:5], in_=m8[:, 7:8]),
                waits=[("ve", P.cnt("ve"))], incs=[("ve", 1)])
            warm()
            for cchunk in range(5):
                cs = corr_tile(t, cchunk, TSW[cchunk])
                ww = TSW[cchunk]
                P.step("scalar", lambda e, c=cs, w2=ww: e.activation(
                    out=e_scr[:, 0:w2], in_=c, func=AF.Exp,
                    bias=sm[:, 3:4], scale=1.0),
                    waits=[("ve", P.cnt("ve"))], incs=[("se", 1)])
                P.step("vector", lambda e, c=cs, cc=cchunk, w2=ww:
                       e.scalar_tensor_tensor(
                    out=tscr[:, cc * 512:cc * 512 + w2], in0=c,
                    scalar=sm[:, 4:5], in1=e_scr[:, 0:w2],
                    op0=OP.is_ge, op1=OP.mult),
                    waits=[("se", P.cnt("se"))], incs=[("ve", 1)])
                warm()
            ve_s = P.cnt("ve")
            for fc in range(KC):
                b = fc % 2
                w = [("ve", ve_s)] if fc < 2 else [("ve", P.cnt("ve") - 1)]
                P.step("tensor", lambda e, f=fc, bb=b: e.transpose(
                    tr_ps(bb), tscr[:, f * 128:(f + 1) * 128], identf_b[:]),
                    waits=w, incs=[("te", 1)])
                P.step("vector", lambda e, f=fc, bb=b, tt=t: e.tensor_copy(
                    out=sfm_blk(tt, f), in_=tr_ps(bb)),
                    waits=[("te", P.cnt("te"))], incs=[("ve", 1)])
        ve_sfm = P.cnt("ve")
        if ps == 0:
            dump("dbg_sfm", A(CORR0, TBP * TAU))

        dma("sync", GAMMA[:, 0:2176].rearrange("p (c k) -> p c k", c=KC),
            wtab2_d[0], "tA", waits=[("te", P.cnt("te")), ("ve", ve_sfm)])
        dma("sync", GAMMA[:, 2176:4352].rearrange("p (c k) -> p c k", c=KC),
            wtab2_d[1], "tB")

        def sfm_mov(fcc):
            return A(CORR0, TBP * TAU).rearrange("p (t q) -> p t q", q=TAU)[
                :, :, fcc * 128:fcc * 128 + 128]

        P.step("tensor", waits=[("ve", ve_sfm)])
        for kc in range(KC):
            buf = kc % 2
            tsem = "tA" if buf == 0 else "tB"
            P.step("tensor", waits=[(tsem, P.cnt(tsem)),
                                    ("ve", max(0, P.cnt("ve") - 1))])
            for fc in range(KC):
                P.step("tensor", lambda e, b=buf, f=fc, k=kc: e.matmul(
                    bank[k % 2][:], w2_blk(b, f), sfm_mov(f),
                    start=(f == 0), stop=(f == KC - 1)),
                    incs=[("te", 1)] if fc == KC - 1 else [])
            if kc + 2 < KC:
                dma("sync",
                    GAMMA[:, buf * 2176:buf * 2176 + 2176]
                    .rearrange("p (c k) -> p c k", c=KC),
                    wtab2_d[kc + 2], tsem, waits=[("te", P.cnt("te"))])
            te_now = P.cnt("te")
            fre, fim = F_at(kc), F_at(KC + kc)
            P.step("vector", lambda e, a=fre, k=kc: e.tensor_mul(
                a, a, bank[k % 2][:]),
                waits=[("te", te_now), ("ve", P.cnt("ve"))], incs=[("ve", 1)])
            P.step("vector", lambda e, a=fim, k=kc: e.tensor_mul(
                a, a, bank[k % 2][:]), incs=[("ve", 1)])
            ve_u = P.cnt("ve")
        if ps == 0:
            dump("dbg_u", ALPHA[:, 0:F_sz])

        IGRP = [(0, 8), (8, 8), (16, 8), (24, 8), (32, 2)]
        def a9_tab(isl_, gi_, buf_, w=()):
            k0, ng = IGRP[gi_]
            dma("sync", GAMMA[:, 4096 + buf_ * 4096: 4096 + buf_ * 4096 + ng * 512]
                .rearrange("p (k c) -> p k c", k=ng),
                irtab_d[isl_][:, k0:k0 + ng, :], "iA" if buf_ == 0 else "iB", w)
        a9_tab(0, 0, 0, [("te", P.cnt("te")), ("ve", ve_u)])
        a9_tab(0, 1, 1)
        dma("gpsimd", G(0, 2048).rearrange("p (t c) -> p t c", t=TBP),
            x_tm_d[ps, 0], "xA", waits=[("te", P.cnt("te")), ("ve", ve_u)])
        P.step("tensor", waits=[("ve", ve_u)])
        pend_v1 = []
        gidx9 = 0
        for isl in range(8):
            bb9 = (isl % 2) * 4
            for gi in range(5):
                k0, ng = IGRP[gi]
                buf = gidx9 % 2
                isem = "iA" if buf == 0 else "iB"
                w = [(isem, P.cnt(isem))]
                if gi == 0 and isl >= 2:
                    w.append(("ve", pend_v1[isl - 2]))
                P.step("tensor", waits=w)
                for kk2 in range(ng):
                    kc = k0 + kk2
                    for t in range(TBP):
                        P.step("tensor", lambda e, tt=t, k=kc, b=buf, j=kk2,
                               bo=bb9: e.matmul(
                            bank[bo + tt][:],
                            A(k * PTOK + tt * 128, 128), irt_g(b, j),
                            start=(k == 0), stop=(k == 2 * KC - 1)),
                            incs=[("te", 1)] if t == TBP - 1 else [])
                ngi = gidx9 + 2
                if ngi < 40:
                    nisl, ngj = divmod(ngi, 5)
                    a9_tab(nisl, ngj, buf, [("te", P.cnt("te"))])
                gidx9 += 1
            if isl + 1 < 8:
                xsem2 = "xA" if (isl + 1) % 2 == 0 else "xB"
                dma("gpsimd",
                    G(((isl + 1) % 2) * 2048, 2048)
                    .rearrange("p (t c) -> p t c", t=TBP),
                    x_tm_d[ps, isl + 1], xsem2,
                    waits=[("ve", P.cnt("ve"))])
            te_now = P.cnt("te")
            xb = isl % 2
            xsem = "xA" if xb == 0 else "xB"
            for t in range(TBP):
                P.step("vector", lambda e, tt=t, b=xb, i=isl, bo=bb9: e.tensor_add(
                    v1_tile(tt, i), bank[bo + tt][:], xtm_tile(b, tt)),
                    waits=[("te", te_now), (xsem, P.cnt(xsem))],
                    incs=[("ve", 1)])
                P.step("vector", lambda e, tt=t, i=isl: e.reduce_sum(
                    vsum_p[:, tt * 8 + i: tt * 8 + i + 1], v1_tile(tt, i),
                    axis=AX),
                    waits=[("ve", P.cnt("ve"))], incs=[("ve", 1)])
                P.step("scalar", lambda e, tt=t, i=isl: e.activation(
                    out=s2_scr[:], in_=v1_tile(tt, i), func=AF.Square),
                    waits=[("ve", P.cnt("ve"))], incs=[("se", 1)])
                P.step("vector", lambda e, tt=t, i=isl: e.reduce_sum(
                    vsq_p[:, tt * 8 + i: tt * 8 + i + 1], s2_scr[:], axis=AX),
                    waits=[("se", P.cnt("se"))], incs=[("ve", 1)])
            pend_v1.append(P.cnt("ve"))

        if ps == NPASS - 1:
            post_a9 = (P.cnt("te"), P.cnt("ve"))
        if ps == 0:
            dump("dbg_v1", A(CORR0, TBP * D))
        for t in range(TBP):
            g = ps * TBP + t
            P.step("vector", lambda e, tt=t: e.reduce_sum(
                sm[:, 5:6], vsum_p[:, tt * 8:(tt + 1) * 8], axis=AX),
                waits=[("ve", P.cnt("ve"))], incs=[("ve", 1)])
            P.step("vector", lambda e, tt=t: e.reduce_sum(
                sm[:, 6:7], vsq_p[:, tt * 8:(tt + 1) * 8], axis=AX),
                incs=[("ve", 1)])
            P.step("vector", lambda e: e.tensor_scalar_mul(
                sm[:, 5:6], sm[:, 5:6], 1.0 / D),
                waits=[("ve", P.cnt("ve"))], incs=[("ve", 1)])
            P.step("vector", lambda e: e.tensor_scalar_mul(
                sm[:, 6:7], sm[:, 6:7], 1.0 / D), incs=[("ve", 1)])
            P.step("vector", lambda e: e.tensor_mul(
                sm[:, 7:8], sm[:, 5:6], sm[:, 5:6]),
                waits=[("ve", P.cnt("ve"))], incs=[("ve", 1)])
            P.step("vector", lambda e: e.tensor_sub(
                sm[:, 7:8], sm[:, 6:7], sm[:, 7:8]),
                waits=[("ve", P.cnt("ve"))], incs=[("ve", 1)])
            P.step("vector", lambda e: e.tensor_scalar_add(
                sm[:, 7:8], sm[:, 7:8], EPS),
                waits=[("ve", P.cnt("ve"))], incs=[("ve", 1)])
            P.step("vector", lambda e: e.reciprocal(
                sm[:, 8:9], sm[:, 7:8]),
                waits=[("ve", P.cnt("ve"))], incs=[("ve", 1)])
            warm()
            P.step("scalar", lambda e: e.activation(
                out=sm[:, 9:10], in_=sm[:, 8:9], func=AF.Sqrt),
                waits=[("ve", P.cnt("ve"))], incs=[("se", 1)])
            P.step("vector", lambda e: e.tensor_mul(
                sm[:, 10:11], sm[:, 5:6], sm[:, 9:10]),
                waits=[("se", P.cnt("se"))], incs=[("ve", 1)])
            P.step("vector", lambda e: e.tensor_scalar_mul(
                sm[:, 10:11], sm[:, 10:11], -1.0),
                waits=[("ve", P.cnt("ve"))], incs=[("ve", 1)])
            P.step("vector", lambda e, tt=t: e.tensor_scalar(
                out=x1t[:], in0=v1_row(tt), scalar1=sm[:, 9:10],
                scalar2=sm[:, 10:11], op0=OP.mult, op1=OP.add),
                waits=[("ve", P.cnt("ve"))], incs=[("ve", 1)])
            if ps == 0:
                ln1_affine_ve.append(P.cnt("ve"))
            warm()
            ve_x1 = P.cnt("ve")
            for fc in range(FC):
                b = fc % 2
                w = [("ve", ve_x1)] if fc < 2 else [("ve", P.cnt("ve") - 1)]
                P.step("tensor", lambda e, f=fc, bb=b: e.transpose(
                    tr_ps(bb), x1t[:, f * 128:(f + 1) * 128], identf_b[:]),
                    waits=w, incs=[("te", 1)])
                P.step("vector", lambda e, f=fc, bb=b, gg=g: e.tensor_copy(
                    out=x1_blk(gg, f), in_=tr_ps(bb)),
                    waits=[("te", P.cnt("te"))], incs=[("ve", 1)])
    ve_x1fm = P.cnt("ve")
    if debug:
        dump("dbg_x1", BETA[:, :])

    dma("gpsimd", PTS[:, 0:2048].rearrange("p (c r) -> p c r", c=FC),
        acat_d[:], "ac", waits=[("ve", ve_x1fm), ("te", P.cnt("te"))])

    def x1_mov(th, fcc):
        return BETA[:, :].rearrange("p (g q) -> p g q", q=D)[
            :, th * 4:(th + 1) * 4, fcc * 128:fcc * 128 + 128]

    if debug:
        P.step("sync", waits=[("ac", P.cnt("ac"))])
        dump("dbg_acat", PTS[:, 0:2048])
    P.step("tensor", waits=[("ve", ve_x1fm), ("ac", P.cnt("ac"))])
    for l in range(4):
        for th in range(2):
            bk = bank[l * 2 + th][0:16, :]
            for fc in range(FC):
                P.step("tensor", lambda e, g=bk, ll=l, t=th, f=fc: e.matmul(
                    g, acat_blk(f, ll), x1_mov(t, f),
                    start=(f == 0), stop=(f == FC - 1)),
                    incs=[("te", 1)] if fc == FC - 1 else [])
    te_h = P.cnt("te")
    for l in range(4):
        for th in range(2):
            P.step("vector", lambda e, ll=l, t=th: e.tensor_copy(
                out=h_b[:, ll * NTOK + t * 512: ll * NTOK + t * 512 + 512],
                in_=bank[ll * 2 + t][0:16, :]),
                waits=[("te", te_h)], incs=[("ve", 1)])
    ve_h = P.cnt("ve")
    if debug:
        dump("dbg_h", h_b[:, :])


    def b_quarters(o8, extra_waits=()):
        for q in range(4):
            w = list(extra_waits) if q == 0 else []
            if o8 >= 1:
                w.append(("tq", 4 * (o8 - 1) + q + 1))
            dma("gpsimd",
                GAMMA[:, q * 4096:(q + 1) * 4096]
                .rearrange("p (c k) -> p c k", c=8),
                w0t_d[o8, :, q * 8:(q + 1) * 8, :], f"w{q}", waits=w)

    def b_small(o8, waits_b):
        dma("gpsimd", b0_b[:, o8 % 2], b0_d[o8], "wl", waits=waits_b)
        dma("gpsimd", blt_b[:, o8 % 2].rearrange("q (l c) -> q l c", l=4),
            blt_d[o8], "wl")

    def b_masks(o8, waits_m):
        for i in range(3):
            dma("sync", mkt(i).rearrange("p (t c) -> p t c", t=TB),
                m_d[i][o8], "mk", waits=waits_m if i == 0 else ())

    b_quarters(0, extra_waits=[("te", post_a9[0]), ("ve", post_a9[1])])
    b_small(0, ())
    b_masks(0, [("te", P.cnt("te"))])

    SETS = [bank[0:4], bank[4:8]]
    NH = NO8 * 2
    evac_ve = {}
    sel_ve = {}
    lora_te = {}
    tail_te = {}

    def app_lora(k, gi):
        o8, h = k // 2, k % 2
        g = h * 4 + gi
        st = SETS[k % 2]
        if gi == 0:
            w = [("ve", evac_ve[k]), ("wl", 32 * (o8 + 1))]
        else:
            w = [("ve", sel_ve[(k, gi - 1)])]
        P.step("tensor", waits=w)
        for l in range(4):
            P.step("tensor", lambda e, ll=l, gg=g, o=o8, s=st: e.matmul(
                s[ll][:],
                h_b[:, ll * NTOK + gg * 128: ll * NTOK + gg * 128 + 128],
                blt_b[:, o % 2, ll * OSLAB:(ll + 1) * OSLAB],
                start=True, stop=True),
                incs=[("te", 1)] if l == 3 else [])
        lora_te[(k, gi)] = P.cnt("te")

    def app_sel(k, gi):
        o8, h = k // 2, k % 2
        g = h * 4 + gi
        st = SETS[k % 2]
        P.step("vector", lambda e, s=st: e.tensor_copy(
            out=acc_v[:], in_=s[0][:]),
            waits=[("te", lora_te[(k, gi)]), ("mk", 48 * (o8 + 1)),
                   ("ve", P.cnt("ve"))], incs=[("ve", 1)])
        for i in range(3):
            P.step("vector", lambda e, ii=i, gg=g, s=st: e.copy_predicated(
                out=acc_v[:],
                mask=mkt(ii)[:, gg * OSLAB:(gg + 1) * OSLAB],
                data=s[1 + ii][:]),
                waits=[("ve", P.cnt("ve"))], incs=[("ve", 1)])
        sel_ve[(k, gi)] = P.cnt("ve")
        P.step("vector", lambda e, gg=g, o=o8: e.scalar_tensor_tensor(
            out=v_tile(gg, o), in0=acc_v[:], scalar=8.0, in1=v_tile(gg, o),
            op0=OP.mult, op1=OP.add,
            accum_out=vsum_p[:, gg * 8 + o: gg * 8 + o + 1]),
            waits=[("ve", P.cnt("ve"))], incs=[("ve", 1)])
        P.step("vector", lambda e, gg=g, o=o8: e.scalar_tensor_tensor(
            out=s2_scr[:], in0=v_tile(gg, o), scalar=1.0, in1=v_tile(gg, o),
            op0=OP.mult, op1=OP.mult,
            accum_out=vsq_p[:, gg * 8 + o: gg * 8 + o + 1]),
            waits=[("ve", P.cnt("ve"))], incs=[("ve", 1)])

    for k in range(NH):
        o8, h = k // 2, k % 2
        st = SETS[k % 2]
        P.step("tensor",
               waits=[("ve", ve_h if k < 2 else sel_ve[(k - 2, 3)])])
        for fc in range(FC):
            if h == 0 and fc % 8 == 0:
                P.step("tensor", waits=[(f"w{fc // 8}", 16 * (o8 + 1))])
            for gi in range(4):
                g = h * 4 + gi
                inc = [("tq", 1)] if (h == 1 and fc % 8 == 7 and gi == 3) \
                    else []
                P.step("tensor", lambda e, gg=g, f=fc, s=st, i=gi: e.matmul(
                    s[i][:], x1_blk(gg, f), w0_blk(f),
                    start=(f == 0), stop=False), incs=inc)
            if k > 0 and fc % 8 == 7:
                app_lora(k - 1, fc // 8)
                app_sel(k - 1, fc // 8)
        P.step("tensor", waits=[("wl", 32 * (o8 + 1))])
        for gi in range(4):
            g = h * 4 + gi
            P.step("tensor", lambda e, o=o8, s=st, i=gi: e.matmul(
                s[i][:], onesb_b[:], b0_b[:, o % 2],
                start=False, stop=False))
            for j in range(4):
                P.step("tensor", lambda e, gg=g, o=o8, jj=j, s=st, i=gi:
                       e.matmul(
                           s[i][:, jj * 128:(jj + 1) * 128],
                           x1_blk(gg, 4 * o + jj), ident_b[:],
                           start=False, stop=(jj == 3)),
                       incs=[("te", 1)] if j == 3 else [])
        tail_te[k] = P.cnt("te")
        for gi in range(4):
            g = h * 4 + gi
            P.step("vector", lambda e, gg=g, o=o8, s=st, i=gi: e.tensor_copy(
                out=v_tile(gg, o), in_=s[i][:]),
                waits=[("te", tail_te[k]), ("ve", P.cnt("ve"))],
                incs=[("ve", 1)])
        evac_ve[k] = P.cnt("ve")
        if h == 0 and o8 + 1 < NO8:
            b_quarters(o8 + 1)
            b_small(o8 + 1, [("te", P.cnt("te"))])
        if h == 0 and o8 >= 1:
            b_masks(o8, [("ve", P.cnt("ve"))])
    for gi in range(4):
        app_lora(NH - 1, gi)
        app_sel(NH - 1, gi)

    te_b = P.cnt("te")
    for g in range(TB):
        P.step("vector", lambda e, gg=g: e.reduce_sum(
            sm[:, 5:6], vsum_p[:, gg * 8:(gg + 1) * 8], axis=AX),
            waits=[("ve", P.cnt("ve")), ("te", te_b)], incs=[("ve", 1)])
        P.step("vector", lambda e, gg=g: e.reduce_sum(
            sm[:, 6:7], vsq_p[:, gg * 8:(gg + 1) * 8], axis=AX),
            incs=[("ve", 1)])
        P.step("vector", lambda e: e.tensor_scalar_mul(
            sm[:, 5:6], sm[:, 5:6], 1.0 / D),
            waits=[("ve", P.cnt("ve"))], incs=[("ve", 1)])
        P.step("vector", lambda e: e.tensor_scalar_mul(
            sm[:, 6:7], sm[:, 6:7], 1.0 / D), incs=[("ve", 1)])
        P.step("vector", lambda e: e.tensor_mul(
            sm[:, 7:8], sm[:, 5:6], sm[:, 5:6]),
            waits=[("ve", P.cnt("ve"))], incs=[("ve", 1)])
        P.step("vector", lambda e: e.tensor_sub(
            sm[:, 7:8], sm[:, 6:7], sm[:, 7:8]),
            waits=[("ve", P.cnt("ve"))], incs=[("ve", 1)])
        P.step("vector", lambda e: e.tensor_scalar_add(
            sm[:, 7:8], sm[:, 7:8], EPS),
            waits=[("ve", P.cnt("ve"))], incs=[("ve", 1)])
        P.step("vector", lambda e: e.reciprocal(
            sm[:, 8:9], sm[:, 7:8]),
            waits=[("ve", P.cnt("ve"))], incs=[("ve", 1)])
        P.step("scalar", lambda e: e.activation(
            out=sm[:, 9:10], in_=sm[:, 8:9], func=AF.Sqrt),
            waits=[("ve", P.cnt("ve"))], incs=[("se", 1)])
        P.step("vector", lambda e: e.tensor_mul(
            sm[:, 10:11], sm[:, 5:6], sm[:, 9:10]),
            waits=[("se", P.cnt("se"))], incs=[("ve", 1)])
        P.step("vector", lambda e: e.tensor_scalar_mul(
            sm[:, 10:11], sm[:, 10:11], -1.0),
            waits=[("ve", P.cnt("ve"))], incs=[("ve", 1)])
        b = g % 2
        ssem = "sA" if b == 0 else "sB"
        w = [("ve", P.cnt("ve"))]
        if g >= 2:
            w.append((ssem, P.cnt(ssem)))
        P.step("vector", lambda e, gg=g, bb=b: e.tensor_scalar(
            out=out_view(bb), in0=v_row(gg), scalar1=sm[:, 9:10],
            scalar2=sm[:, 10:11], op0=OP.mult, op1=OP.add),
            waits=w, incs=[("ve", 1)])
        dma("sync", out_d[g], out_view(b), ssem, waits=[("ve", P.cnt("ve"))])

    P.step("sync", waits=[("sA", P.cnt("sA")), ("sB", P.cnt("sB")),
                          ("wl", P.cnt("wl")), ("xf", P.cnt("xf"))])

    with nc.Block() as block:
        @block.sync
        def _(eng):
            P.emit("sync", eng, sems)

        @block.gpsimd
        def _(eng):
            P.emit("gpsimd", eng, sems)

        @block.tensor
        def _(eng):
            P.emit("tensor", eng, sems)

        @block.vector
        def _(eng):
            P.emit("vector", eng, sems)

        @block.scalar
        def _(eng):
            P.emit("scalar", eng, sems)

    ctx.close()
    return nc


_NC = {}


def kernel(L0_attn_x, L0_ffn_W0, L0_ffn_b0, L0_ffn_A, L0_ffn_B):
    x = np.asarray(L0_attn_x, dtype=np.float32)
    w0 = np.asarray(L0_ffn_W0, dtype=np.float32)
    b0 = np.asarray(L0_ffn_b0, dtype=np.float32)
    a_w = np.asarray(L0_ffn_A, dtype=np.float32)
    b_w = np.asarray(L0_ffn_B, dtype=np.float32)
    if "nc" not in _NC:
        _NC["nc"] = build_program()
    in_maps = [_core_inputs(c, x, w0, b0, a_w, b_w) for c in range(N_CORES)]
    res = run_bass_kernel_spmd(_NC["nc"], in_maps, core_ids=list(range(N_CORES)))
    out = np.concatenate([r["out"].reshape(-1, D) for r in res.results], axis=0)
    return out.reshape(1, SEQ, D).astype(np.float32)


# version 18
# speedup vs baseline: 1.2572x; 1.0212x over previous
import numpy as np
import ml_dtypes

import concourse.bass as bass
import concourse.mybir as mybir
from concourse.bass_utils import run_bass_kernel_spmd

F32 = mybir.dt.float32
BF16 = mybir.dt.bfloat16
U8 = mybir.dt.uint8
AX = mybir.AxisListType.X
AF = mybir.ActivationFunctionType
OP = mybir.AluOpType
BF = ml_dtypes.bfloat16

N_CORES = 8
SEQ = 8192
D = 4096
KHALF = 2049
KPAD = 2176
KC = KPAD // 128
FC = D // 128
OSLAB = 512
NO8 = D // OSLAB
EPS = 1e-5
NPASS = 2
NTOK = SEQ // N_CORES
PTOK = NTOK // NPASS
TBP = PTOK // 128
TB = NTOK // 128

_CONST = {}


def _constants():
    if _CONST:
        return _CONST
    k = np.arange(KPAD, dtype=np.float64)
    f = np.arange(D, dtype=np.float64)
    valid = (k < KHALF).astype(np.float64)
    ang = 2.0 * np.pi * np.outer(f, k) / D
    wc = np.cos(ang) * valid
    ws = -np.sin(ang) * valid
    ck = np.where((k == 0) | (k == D // 2), 1.0, 2.0) * valid / D
    tau = np.arange(D, dtype=np.float64)
    iang = 2.0 * np.pi * np.outer(k, tau) / D
    ctab = np.cos(iang) * ck[:, None]
    vc = np.cos(iang) * ck[:, None]
    vs = -np.sin(iang) * ck[:, None]

    def fm_chunks(tab):
        out = np.empty((KC, 128, FC, 128), dtype=BF)
        for j in range(KC):
            out[j] = tab[:, j * 128:(j + 1) * 128].reshape(FC, 128, 128) \
                .transpose(1, 0, 2)
        return out

    wtab = np.ascontiguousarray(
        np.concatenate([fm_chunks(wc), fm_chunks(ws)], axis=0))

    def tm_chunks(tab):
        out = np.empty((8, KC, 128, 512), dtype=BF)
        for sl in range(8):
            for j in range(KC):
                out[sl, j] = tab[j * 128:(j + 1) * 128,
                                 sl * 512:(sl + 1) * 512]
        return out

    TAU = 2176
    ctab_h = np.zeros((KPAD, TAU), dtype=np.float64)
    ctab_h[:, :2049] = ctab[:, :2049]
    ctab_c = np.empty((5, KC, 128, 512), dtype=BF)
    for sl in range(5):
        w = 512 if sl < 4 else 128
        for j in range(KC):
            ctab_c[sl, j, :, :w] = ctab_h[j * 128:(j + 1) * 128,
                                          sl * 512:sl * 512 + w]
            if w < 512:
                ctab_c[sl, j, :, w:] = 0
    ctab_c = np.ascontiguousarray(ctab_c.transpose(0, 2, 1, 3))
    cpf = np.where((f == 0) | (f == D // 2), 1.0, 2.0) * (f < 2049)
    wc2 = np.cos(ang) * valid * cpf[:, None]
    wtab2 = np.ascontiguousarray(fm_chunks(wc2)[:KC, :, :KC, :])
    irtab = np.ascontiguousarray(
        np.concatenate([tm_chunks(vc), tm_chunks(vs)], axis=1)
        .transpose(0, 2, 1, 3))

    ident = np.ascontiguousarray(np.eye(128), dtype=BF)
    onesb = np.ones((1, 128), dtype=BF)

    import jax
    with jax.default_device(jax.devices("cpu")[0]):
        u = np.asarray(jax.random.uniform(
            jax.random.key(42, impl="threefry2x32"), (1, SEQ, D)))[0]
    masks = [((u >= 0.25 * l) & (u < 0.25 * (l + 1))).astype(np.uint8)
             for l in (1, 2, 3)]

    _CONST.update(wtab=wtab, wtab2=wtab2, ctab=ctab_c, irtab=irtab,
                  ident=ident, onesb=onesb, masks=masks)
    return _CONST


def _core_inputs(core, x, w0, b0, a_w, b_w):
    c = _constants()
    t0 = core * NTOK
    xs = x[0, t0:t0 + NTOK]
    x_fm = np.ascontiguousarray(
        xs.reshape(NPASS, PTOK, D).transpose(0, 2, 1)
        .reshape(NPASS, FC, 128, PTOK).transpose(0, 2, 1, 3))
    x_tm = np.ascontiguousarray(
        xs.reshape(NPASS, TBP, 128, 8, 512).transpose(0, 3, 2, 1, 4))
    w0t = np.ascontiguousarray(
        w0.T.reshape(FC, 128, NO8, OSLAB).transpose(2, 1, 0, 3))
    acat = np.ascontiguousarray(
        a_w.transpose(2, 0, 1).reshape(FC, 128, 64).transpose(1, 0, 2))
    blt = np.ascontiguousarray(
        b_w.transpose(0, 2, 1).reshape(4, 16, NO8, OSLAB)
        .transpose(2, 1, 0, 3))
    m_sl = [np.ascontiguousarray(
        m[t0:t0 + NTOK].reshape(TB, 128, NO8, OSLAB).transpose(2, 1, 0, 3))
        for m in c["masks"]]
    b0sl = np.ascontiguousarray(b0.reshape(NO8, 1, OSLAB))
    return dict(
        x_fm=x_fm.astype(np.float32), x_tm=x_tm.astype(np.float32),
        w0t=w0t.astype(np.float32), b0sl=b0sl.astype(np.float32),
        acat=acat.astype(np.float32), blt=blt.astype(np.float32),
        m1=m_sl[0], m2=m_sl[1], m3=m_sl[2],
        wtab=c["wtab"], wtab2=c["wtab2"], ctab=c["ctab"], irtab=c["irtab"],
        ident=c["ident"], identf=np.eye(128, dtype=np.float32),
        onesb=c["onesb"],
    )


class Plan:
    def __init__(self):
        self.steps = []
        self.counts = {}

    def cnt(self, s):
        return self.counts.get(s, 0)

    def step(self, engine, fn=None, waits=(), incs=()):
        self.steps.append((engine, list(waits), fn, list(incs)))
        for s, n in incs:
            self.counts[s] = self.counts.get(s, 0) + n

    def emit(self, engine_name, eng, sems):
        for e, waits, fn, incs in self.steps:
            if e != engine_name:
                continue
            for s, v in waits:
                if v > 0:
                    eng.wait_ge(sems[s], v)
            if fn is None:
                continue
            instr = fn(eng)
            for s, n in incs:
                instr.then_inc(sems[s], n)


def build_program(debug=False):
    nc = bass.Bass("TRN2", target_bir_lowering=False, debug=False)

    din = lambda name, shape, dt: nc.dram_tensor(
        name, list(shape), dt, kind="ExternalInput").ap()
    x_fm_d = din("x_fm", [NPASS, 128, FC, PTOK], F32)
    x_tm_d = din("x_tm", [NPASS, 8, 128, TBP, 512], F32)
    w0t_d = din("w0t", [NO8, 128, FC, OSLAB], F32)
    b0_d = din("b0sl", [NO8, 1, OSLAB], F32)
    acat_d = din("acat", [128, FC, 64], F32)
    blt_d = din("blt", [NO8, 16, 4, OSLAB], F32)
    m_d = [din(f"m{l}", [NO8, 128, TB, OSLAB], U8) for l in (1, 2, 3)]
    wtab_d = din("wtab", [2 * KC, 128, FC, 128], BF16)
    ctab_d = din("ctab", [5, 128, KC, 512], BF16)
    wtab2_d = din("wtab2", [KC, 128, KC, 128], BF16)
    irtab_d = din("irtab", [8, 128, 2 * KC, 512], BF16)
    ident_d = din("ident", [128, 128], BF16)
    identf_d = din("identf", [128, 128], F32)
    onesb_d = din("onesb", [1, 128], BF16)
    out_d = nc.dram_tensor("out", [TB, 128, D], F32, kind="ExternalOutput").ap()
    dbg = {}
    if debug:
        for nm, shape in [("dbg_F", [128, 2 * KC * PTOK]),
                          ("dbg_corr", [128, TBP * 2176]),
                          ("dbg_sfm", [128, TBP * 2176]),
                          ("dbg_u", [128, 2 * KC * PTOK]),
                          ("dbg_v1", [128, TBP * D]),
                          ("dbg_x1", [128, TB * D]),
                          ("dbg_h", [16, 4 * NTOK]),
                          ("dbg_acat", [128, 2048]),
                          ("dbg_v", [128, TB * D])]:
            dbg[nm] = nc.dram_tensor(nm, shape, BF16, kind="ExternalOutput").ap()

    def dump(nm, src_ap, eng="sync"):
        if debug:
            P.step(eng, lambda e, o=dbg[nm], i=src_ap: e.dma_start(out=o, in_=i),
                   waits=[("ve", P.cnt("ve")), ("te", P.cnt("te")),
                          ("se", P.cnt("se"))], incs=[("sA", 16)])
            for en in ("vector", "tensor", "scalar", "gpsimd"):
                P.step(en, waits=[("sA", P.cnt("sA"))])

    from contextlib import ExitStack
    ctx = ExitStack()
    sb = lambda n, s, dt: ctx.enter_context(nc.sbuf_tensor(n, list(s), dt))
    psT = lambda n, s, dt=F32: ctx.enter_context(nc.psum_tensor(n, list(s), dt))

    ALPHA = sb("ALPHA", [128, 34816], BF16)
    BETA = sb("BETA", [128, 32768], BF16)
    GAMMA = sb("GAMMA", [128, 16384], BF16)
    PTS = sb("PTS", [128, 8704], BF16)
    h_b = sb("h_b", [16, 4 * NTOK], BF16)
    e_scr = sb("e_scr", [128, 512], BF16)
    s2_scr = sb("s2_scr", [128, 512], BF16)
    blt_b = sb("blt_b", [16, 2, 4 * OSLAB], BF16)
    b0_b = sb("b0_b", [1, 2, OSLAB], BF16)
    ctab_b = sb("ctab_b", [128, 4, 512], BF16)
    ident_b = sb("ident_b", [128, 128], BF16)
    identf_b = sb("identf_b", [128, 128], F32)
    onesb_b = sb("onesb_b", [1, 128], BF16)
    m8 = sb("m8", [128, 8], BF16)
    mf = sb("mf", [128, 8], F32)
    sm = sb("sm", [128, 16], F32)
    vsum_p = sb("vsum_p", [128, 64], F32)
    vsq_p = sb("vsq_p", [128, 64], F32)

    def A(o, n, dt=BF16):
        v = ALPHA[:, o:o + n]
        return v.bitcast(dt) if dt != BF16 else v

    def G(o, n, dt=BF16):
        v = GAMMA[:, o:o + n]
        return v.bitcast(dt) if dt != BF16 else v

    def PT(o, n, dt=BF16):
        v = PTS[:, o:o + n]
        return v.bitcast(dt) if dt != BF16 else v

    F_sz = 2 * KC * PTOK
    F_at = lambda kc: A(kc * PTOK, PTOK)
    CORR0 = F_sz
    TAU = 2176
    TSW = [512, 512, 512, 512, 128]
    corr_tb = lambda t: A(CORR0 + t * TAU, TAU)
    corr_tile = lambda t, sl, w: A(CORR0 + t * TAU + sl * 512, w)
    corr_pad = lambda t: A(CORR0 + t * TAU + 2049, 127)
    sfm_blk = lambda t, fcc: A(CORR0 + t * TAU + fcc * 128, 128)
    v1_tile = lambda t, isl: A(CORR0 + t * D + isl * 512, 512)
    v1_row = lambda t: A(CORR0 + t * D, D)
    xfm_blk = lambda fcc: A(CORR0 + fcc * PTOK, PTOK)
    v_tile = lambda g, o8: A((g * 8 + o8) * 512, 512)
    v_row = lambda g: A(g * D, D)
    acat_blk = lambda fcc, l: PT(fcc * 64 + l * 16, 16)

    wtab_blk = lambda buf, fcc: G(buf * 4096 + fcc * 128, 128)
    w2_blk = lambda buf, fcc: G(buf * 2176 + fcc * 128, 128)
    xtm_tile = lambda buf, t: G(buf * 2048 + t * 512, 512)
    irt_g = lambda buf, j: G(4096 + buf * 4096 + j * 512, 512)
    ctab_g = lambda buf, j: G(buf * 4096 + j * 512, 512)
    w0_blk = lambda fcc: G(fcc * 512, 512)
    out_view = lambda buf: G(buf * 8192, 8192, F32)

    P_blk = lambda kc, t: PT(kc * 512 + t * 128, 128)
    tscr = PT(0, 8192, F32)
    mscr = lambda i: PT(i * 1024, 1024, F32)
    x1t = PT(0, 8192, F32)
    acc_v = PT(0, 1024, F32)
    mkt = lambda i: PTS[:, 2048 + i * 2048: 2048 + (i + 1) * 2048].bitcast(U8)

    x1_blk = lambda g, fcc: BETA[:, g * D + fcc * 128: g * D + fcc * 128 + 128]

    bank = [psT(f"bk{i}", [128, 512]) for i in range(8)]
    tr_ps = lambda b: bank[4 + b][:, 0:128]

    def warm():
        P.step("tensor", lambda e: e.matmul(
            bank[6][:, 0:128], ident_b[:], ident_b[:], start=True, stop=True),
            waits=[("ve", P.cnt("ve"))])

    sems = {s: ctx.enter_context(nc.semaphore(s))
            for s in ("te", "ve", "se", "cs", "xf", "tA", "tB",
                      "cA", "cB", "cC", "cD", "iA", "iB", "iC", "iD",
                      "xA", "xB", "wl", "w0", "w1", "w2", "w3",
                      "mk", "ac", "sA", "sB", "tq")}
    P = Plan()

    def dma(engine, out_ap, in_ap, sname, waits=()):
        P.step(engine, lambda e, o=out_ap, i=in_ap: e.dma_start(out=o, in_=i),
               waits=waits, incs=[(sname, 16)])

    dma("sync", ident_b[:], ident_d[:], "cs")
    dma("sync", identf_b[:], identf_d[:], "cs")
    dma("sync", onesb_b[:], onesb_d[:], "cs")

    ln1_affine_ve = []
    for ps in range(NPASS):
        for q in range(4):
            w = FC // 4
            dma("gpsimd",
                A(CORR0 + q * w * PTOK, w * PTOK)
                .rearrange("p (c t) -> p c t", c=w),
                x_fm_d[ps, :, q * w:(q + 1) * w, :], "xf",
                waits=[("ve", ln1_affine_ve[q] if ps > 0 else P.cnt("ve"))])

        dma("sync", GAMMA[:, 0:4096].rearrange("p (c k) -> p c k", c=FC),
            wtab_d[0], "tA", waits=[("te", P.cnt("te"))])
        dma("sync", GAMMA[:, 4096:8192].rearrange("p (c k) -> p c k", c=FC),
            wtab_d[1], "tB")
        P.step("tensor", waits=[("xf", P.cnt("xf")), ("cs", P.cnt("cs")),
                                ("ve", P.cnt("ve"))])
        a2_evac = {}
        a3_ve = {}
        for kc in range(2 * KC):
            buf = kc % 2
            tsem = "tA" if buf == 0 else "tB"
            w = [(tsem, P.cnt(tsem))]
            if kc >= 2:
                w.append(("ve", a2_evac[kc - 2]))
            P.step("tensor", waits=w)
            for fc in range(FC):
                P.step("tensor", lambda e, b=buf, f=fc, k=kc: e.matmul(
                    bank[b][:], wtab_blk(b, f), xfm_blk(f),
                    start=(f == 0), stop=(f == FC - 1)),
                    incs=[("te", 1)] if fc == FC - 1 else [])
            if kc + 2 < 2 * KC:
                dma("sync",
                    GAMMA[:, buf * 4096:buf * 4096 + 4096]
                    .rearrange("p (c k) -> p c k", c=FC),
                    wtab_d[kc + 2], tsem, waits=[("te", P.cnt("te"))])
            P.step("vector", lambda e, b=buf, k=kc: e.tensor_copy(
                out=F_at(k), in_=bank[b][:]),
                waits=[("te", P.cnt("te"))], incs=[("ve", 1)])
            a2_evac[kc] = P.cnt("ve")
            if kc >= KC:
                k3 = kc - KC
                P.step("scalar", lambda e, k=k3: e.activation(
                    out=e_scr[:], in_=F_at(k), func=AF.Square),
                    waits=[("ve", a2_evac[kc])], incs=[("se", 1)])
                P.step("scalar", lambda e, k=k3: e.activation(
                    out=s2_scr[:], in_=F_at(KC + k), func=AF.Square),
                    incs=[("se", 1)])
                P.step("vector", lambda e, k=k3: e.tensor_add(
                    PTS[:, k * 512:(k + 1) * 512], e_scr[:], s2_scr[:]),
                    waits=[("se", P.cnt("se"))], incs=[("ve", 1)])
                a3_ve[k3] = P.cnt("ve")
        ve_F = P.cnt("ve")
        ve_P = P.cnt("ve")
        if ps == 0:
            dump("dbg_F", ALPHA[:, 0:F_sz])

        CGRP = [(0, 8), (8, 8), (16, 1)]
        def a4_tab(ts_, gi_, buf_, w=()):
            k0, ng = CGRP[gi_]
            dma("sync", GAMMA[:, buf_ * 4096: buf_ * 4096 + ng * 512]
                .rearrange("p (k c) -> p k c", k=ng),
                ctab_d[ts_][:, k0:k0 + ng, :], "cA" if buf_ == 0 else "cB", w)
        a4_tab(0, 0, 0, [("te", P.cnt("te"))])
        a4_tab(0, 1, 1)
        pend_evac = []
        gidx = 0
        for ts in range(5):
            bb = (ts % 2) * 4
            for gi in range(3):
                k0, ng = CGRP[gi]
                buf = gidx % 2
                csem = "cA" if buf == 0 else "cB"
                vtgt = a3_ve[k0 + ng - 1]
                if ts == 0:
                    vtgt = max(vtgt, a2_evac[2 * KC - 1])
                w = [(csem, P.cnt(csem)), ("ve", vtgt)]
                if gi == 0 and ts >= 2:
                    w.append(("ve", pend_evac[ts - 2]))
                P.step("tensor", waits=w)
                for kk2 in range(ng):
                    kc = k0 + kk2
                    for t in range(TBP):
                        P.step("tensor", lambda e, tt=t, k=kc, b=buf, j=kk2,
                               bo=bb, s=ts: e.matmul(
                            bank[bo + tt][:, 0:TSW[s]], P_blk(k, tt),
                            ctab_g(b, j)[:, 0:TSW[s]],
                            start=(k == 0), stop=(k == KC - 1)),
                            incs=[("te", 1)] if t == TBP - 1 else [])
                ngi = gidx + 2
                if ngi < 15:
                    nts, ngj = divmod(ngi, 3)
                    a4_tab(nts, ngj, buf, [("te", P.cnt("te"))])
                gidx += 1
            te_now = P.cnt("te")
            for t in range(TBP):
                P.step("vector", lambda e, tt=t, s=ts, bo=bb: e.tensor_copy(
                    out=corr_tile(tt, s, TSW[s]), in_=bank[bo + tt][:, 0:TSW[s]]),
                    waits=[("te", te_now)], incs=[("ve", 1)])
            pend_evac.append(P.cnt("ve"))
        for t in range(TBP):
            P.step("vector", lambda e, tt=t: e.memset(corr_pad(tt), -1e9),
                   waits=[("ve", P.cnt("ve"))], incs=[("ve", 1)])

        ve_corr = P.cnt("ve")
        if ps == 0:
            dump("dbg_corr", A(CORR0, TBP * TAU))
        for t in range(TBP):
            P.step("vector", lambda e, tt=t: e.max(
                out=m8[:], in_=corr_tb(tt)),
                waits=[("ve", ve_corr)], incs=[("ve", 1)])
            warm()
            P.step("vector", lambda e: e.tensor_copy(out=mf[:], in_=m8[:]),
                   waits=[("ve", P.cnt("ve"))], incs=[("ve", 1)])
            P.step("vector", lambda e: e.tensor_scalar_mul(
                sm[:, 0:1], mf[:, 0:1], -1.0),
                waits=[("ve", P.cnt("ve"))], incs=[("ve", 1)])
            P.step("scalar", lambda e: e.activation(
                out=sm[:, 8:16], in_=mf[:], func=AF.Exp,
                bias=sm[:, 0:1], scale=1.0),
                waits=[("ve", P.cnt("ve"))], incs=[("se", 1)])
            P.step("vector", lambda e: e.reduce_sum(
                sm[:, 1:2], sm[:, 8:16], axis=AX),
                waits=[("se", P.cnt("se"))], incs=[("ve", 1)])
            P.step("scalar", lambda e: e.activation(
                out=sm[:, 2:3], in_=sm[:, 1:2], func=AF.Ln),
                waits=[("ve", P.cnt("ve"))], incs=[("se", 1)])
            P.step("vector", lambda e: e.tensor_sub(
                sm[:, 3:4], sm[:, 0:1], sm[:, 2:3]),
                waits=[("se", P.cnt("se"))], incs=[("ve", 1)])
            P.step("vector", lambda e: e.tensor_copy(
                out=sm[:, 4:5], in_=m8[:, 7:8]),
                waits=[("ve", P.cnt("ve"))], incs=[("ve", 1)])
            warm()
            for cchunk in range(5):
                cs = corr_tile(t, cchunk, TSW[cchunk])
                ww = TSW[cchunk]
                P.step("scalar", lambda e, c=cs, w2=ww: e.activation(
                    out=e_scr[:, 0:w2], in_=c, func=AF.Exp,
                    bias=sm[:, 3:4], scale=1.0),
                    waits=[("ve", P.cnt("ve"))], incs=[("se", 1)])
                P.step("vector", lambda e, c=cs, cc=cchunk, w2=ww:
                       e.scalar_tensor_tensor(
                    out=tscr[:, cc * 512:cc * 512 + w2], in0=c,
                    scalar=sm[:, 4:5], in1=e_scr[:, 0:w2],
                    op0=OP.is_ge, op1=OP.mult),
                    waits=[("se", P.cnt("se"))], incs=[("ve", 1)])
                warm()
            ve_s = P.cnt("ve")
            for fc in range(KC):
                b = fc % 2
                w = [("ve", ve_s)] if fc < 2 else [("ve", P.cnt("ve") - 1)]
                P.step("tensor", lambda e, f=fc, bb=b: e.transpose(
                    tr_ps(bb), tscr[:, f * 128:(f + 1) * 128], identf_b[:]),
                    waits=w, incs=[("te", 1)])
                P.step("vector", lambda e, f=fc, bb=b, tt=t: e.tensor_copy(
                    out=sfm_blk(tt, f), in_=tr_ps(bb)),
                    waits=[("te", P.cnt("te"))], incs=[("ve", 1)])
        ve_sfm = P.cnt("ve")
        if ps == 0:
            dump("dbg_sfm", A(CORR0, TBP * TAU))

        dma("sync", GAMMA[:, 0:2176].rearrange("p (c k) -> p c k", c=KC),
            wtab2_d[0], "tA", waits=[("te", P.cnt("te")), ("ve", ve_sfm)])
        dma("sync", GAMMA[:, 2176:4352].rearrange("p (c k) -> p c k", c=KC),
            wtab2_d[1], "tB")

        def sfm_mov(fcc):
            return A(CORR0, TBP * TAU).rearrange("p (t q) -> p t q", q=TAU)[
                :, :, fcc * 128:fcc * 128 + 128]

        P.step("tensor", waits=[("ve", ve_sfm)])
        for kc in range(KC):
            buf = kc % 2
            tsem = "tA" if buf == 0 else "tB"
            P.step("tensor", waits=[(tsem, P.cnt(tsem)),
                                    ("ve", max(0, P.cnt("ve") - 1))])
            for fc in range(KC):
                P.step("tensor", lambda e, b=buf, f=fc, k=kc: e.matmul(
                    bank[k % 2][:], w2_blk(b, f), sfm_mov(f),
                    start=(f == 0), stop=(f == KC - 1)),
                    incs=[("te", 1)] if fc == KC - 1 else [])
            if kc + 2 < KC:
                dma("sync",
                    GAMMA[:, buf * 2176:buf * 2176 + 2176]
                    .rearrange("p (c k) -> p c k", c=KC),
                    wtab2_d[kc + 2], tsem, waits=[("te", P.cnt("te"))])
            te_now = P.cnt("te")
            fre, fim = F_at(kc), F_at(KC + kc)
            P.step("vector", lambda e, a=fre, k=kc: e.tensor_mul(
                a, a, bank[k % 2][:]),
                waits=[("te", te_now), ("ve", P.cnt("ve"))], incs=[("ve", 1)])
            P.step("vector", lambda e, a=fim, k=kc: e.tensor_mul(
                a, a, bank[k % 2][:]), incs=[("ve", 1)])
            ve_u = P.cnt("ve")
        if ps == 0:
            dump("dbg_u", ALPHA[:, 0:F_sz])

        IGRP = [(0, 8), (8, 8), (16, 8), (24, 8), (32, 2)]
        def a9_tab(isl_, gi_, buf_, w=()):
            k0, ng = IGRP[gi_]
            dma("sync", GAMMA[:, 4096 + buf_ * 4096: 4096 + buf_ * 4096 + ng * 512]
                .rearrange("p (k c) -> p k c", k=ng),
                irtab_d[isl_][:, k0:k0 + ng, :], "iA" if buf_ == 0 else "iB", w)
        a9_tab(0, 0, 0, [("te", P.cnt("te")), ("ve", ve_u)])
        a9_tab(0, 1, 1)
        dma("gpsimd", G(0, 2048).rearrange("p (t c) -> p t c", t=TBP),
            x_tm_d[ps, 0], "xA", waits=[("te", P.cnt("te")), ("ve", ve_u)])
        P.step("tensor", waits=[("ve", ve_u)])
        pend_v1 = []
        gidx9 = 0
        for isl in range(8):
            bb9 = (isl % 2) * 4
            for gi in range(5):
                k0, ng = IGRP[gi]
                buf = gidx9 % 2
                isem = "iA" if buf == 0 else "iB"
                w = [(isem, P.cnt(isem))]
                if gi == 0 and isl >= 2:
                    w.append(("ve", pend_v1[isl - 2]))
                P.step("tensor", waits=w)
                for kk2 in range(ng):
                    kc = k0 + kk2
                    for t in range(TBP):
                        P.step("tensor", lambda e, tt=t, k=kc, b=buf, j=kk2,
                               bo=bb9: e.matmul(
                            bank[bo + tt][:],
                            A(k * PTOK + tt * 128, 128), irt_g(b, j),
                            start=(k == 0), stop=(k == 2 * KC - 1)),
                            incs=[("te", 1)] if t == TBP - 1 else [])
                ngi = gidx9 + 2
                if ngi < 40:
                    nisl, ngj = divmod(ngi, 5)
                    a9_tab(nisl, ngj, buf, [("te", P.cnt("te"))])
                gidx9 += 1
            if isl + 1 < 8:
                xsem2 = "xA" if (isl + 1) % 2 == 0 else "xB"
                dma("gpsimd",
                    G(((isl + 1) % 2) * 2048, 2048)
                    .rearrange("p (t c) -> p t c", t=TBP),
                    x_tm_d[ps, isl + 1], xsem2,
                    waits=[("ve", P.cnt("ve"))])
            te_now = P.cnt("te")
            xb = isl % 2
            xsem = "xA" if xb == 0 else "xB"
            for t in range(TBP):
                P.step("vector", lambda e, tt=t, b=xb, i=isl, bo=bb9: e.tensor_add(
                    v1_tile(tt, i), bank[bo + tt][:], xtm_tile(b, tt)),
                    waits=[("te", te_now), (xsem, P.cnt(xsem))],
                    incs=[("ve", 1)])
                P.step("vector", lambda e, tt=t, i=isl: e.reduce_sum(
                    vsum_p[:, tt * 8 + i: tt * 8 + i + 1], v1_tile(tt, i),
                    axis=AX),
                    waits=[("ve", P.cnt("ve"))], incs=[("ve", 1)])
                P.step("scalar", lambda e, tt=t, i=isl: e.activation(
                    out=s2_scr[:], in_=v1_tile(tt, i), func=AF.Square),
                    waits=[("ve", P.cnt("ve"))], incs=[("se", 1)])
                P.step("vector", lambda e, tt=t, i=isl: e.reduce_sum(
                    vsq_p[:, tt * 8 + i: tt * 8 + i + 1], s2_scr[:], axis=AX),
                    waits=[("se", P.cnt("se"))], incs=[("ve", 1)])
            pend_v1.append(P.cnt("ve"))

        if ps == NPASS - 1:
            post_a9 = (P.cnt("te"), P.cnt("ve"))
        if ps == 0:
            dump("dbg_v1", A(CORR0, TBP * D))
        for t in range(TBP):
            g = ps * TBP + t
            P.step("vector", lambda e, tt=t: e.reduce_sum(
                sm[:, 5:6], vsum_p[:, tt * 8:(tt + 1) * 8], axis=AX),
                waits=[("ve", P.cnt("ve"))], incs=[("ve", 1)])
            P.step("vector", lambda e, tt=t: e.reduce_sum(
                sm[:, 6:7], vsq_p[:, tt * 8:(tt + 1) * 8], axis=AX),
                incs=[("ve", 1)])
            P.step("vector", lambda e: e.tensor_scalar_mul(
                sm[:, 5:6], sm[:, 5:6], 1.0 / D),
                waits=[("ve", P.cnt("ve"))], incs=[("ve", 1)])
            P.step("vector", lambda e: e.tensor_scalar_mul(
                sm[:, 6:7], sm[:, 6:7], 1.0 / D), incs=[("ve", 1)])
            P.step("vector", lambda e: e.tensor_mul(
                sm[:, 7:8], sm[:, 5:6], sm[:, 5:6]),
                waits=[("ve", P.cnt("ve"))], incs=[("ve", 1)])
            P.step("vector", lambda e: e.tensor_sub(
                sm[:, 7:8], sm[:, 6:7], sm[:, 7:8]),
                waits=[("ve", P.cnt("ve"))], incs=[("ve", 1)])
            P.step("vector", lambda e: e.tensor_scalar_add(
                sm[:, 7:8], sm[:, 7:8], EPS),
                waits=[("ve", P.cnt("ve"))], incs=[("ve", 1)])
            P.step("vector", lambda e: e.reciprocal(
                sm[:, 8:9], sm[:, 7:8]),
                waits=[("ve", P.cnt("ve"))], incs=[("ve", 1)])
            warm()
            P.step("scalar", lambda e: e.activation(
                out=sm[:, 9:10], in_=sm[:, 8:9], func=AF.Sqrt),
                waits=[("ve", P.cnt("ve"))], incs=[("se", 1)])
            P.step("vector", lambda e: e.tensor_mul(
                sm[:, 10:11], sm[:, 5:6], sm[:, 9:10]),
                waits=[("se", P.cnt("se"))], incs=[("ve", 1)])
            P.step("vector", lambda e: e.tensor_scalar_mul(
                sm[:, 10:11], sm[:, 10:11], -1.0),
                waits=[("ve", P.cnt("ve"))], incs=[("ve", 1)])
            P.step("vector", lambda e, tt=t: e.tensor_scalar(
                out=x1t[:], in0=v1_row(tt), scalar1=sm[:, 9:10],
                scalar2=sm[:, 10:11], op0=OP.mult, op1=OP.add),
                waits=[("ve", P.cnt("ve"))], incs=[("ve", 1)])
            if ps == 0:
                ln1_affine_ve.append(P.cnt("ve"))
            warm()
            ve_x1 = P.cnt("ve")
            for fc in range(FC):
                b = fc % 2
                w = [("ve", ve_x1)] if fc < 2 else [("ve", P.cnt("ve") - 1)]
                P.step("tensor", lambda e, f=fc, bb=b: e.transpose(
                    tr_ps(bb), x1t[:, f * 128:(f + 1) * 128], identf_b[:]),
                    waits=w, incs=[("te", 1)])
                P.step("vector", lambda e, f=fc, bb=b, gg=g: e.tensor_copy(
                    out=x1_blk(gg, f), in_=tr_ps(bb)),
                    waits=[("te", P.cnt("te"))], incs=[("ve", 1)])
    ve_x1fm = P.cnt("ve")
    if debug:
        dump("dbg_x1", BETA[:, :])

    dma("gpsimd", PTS[:, 0:2048].rearrange("p (c r) -> p c r", c=FC),
        acat_d[:], "ac", waits=[("ve", ve_x1fm), ("te", P.cnt("te"))])

    def x1_mov(th, fcc):
        return BETA[:, :].rearrange("p (g q) -> p g q", q=D)[
            :, th * 4:(th + 1) * 4, fcc * 128:fcc * 128 + 128]

    if debug:
        P.step("sync", waits=[("ac", P.cnt("ac"))])
        dump("dbg_acat", PTS[:, 0:2048])
    P.step("tensor", waits=[("ve", ve_x1fm), ("ac", P.cnt("ac"))])
    for l in range(4):
        for th in range(2):
            bk = bank[l * 2 + th][0:16, :]
            for fc in range(FC):
                P.step("tensor", lambda e, g=bk, ll=l, t=th, f=fc: e.matmul(
                    g, acat_blk(f, ll), x1_mov(t, f),
                    start=(f == 0), stop=(f == FC - 1)),
                    incs=[("te", 1)] if fc == FC - 1 else [])
    te_h = P.cnt("te")
    for l in range(4):
        for th in range(2):
            P.step("vector", lambda e, ll=l, t=th: e.tensor_copy(
                out=h_b[:, ll * NTOK + t * 512: ll * NTOK + t * 512 + 512],
                in_=bank[ll * 2 + t][0:16, :]),
                waits=[("te", te_h)], incs=[("ve", 1)])
    ve_h = P.cnt("ve")
    if debug:
        dump("dbg_h", h_b[:, :])


    def b_quarters(o8, extra_waits=()):
        for q in range(4):
            w = list(extra_waits) if q == 0 else []
            if o8 >= 1:
                w.append(("tq", 4 * (o8 - 1) + q + 1))
            dma("gpsimd",
                GAMMA[:, q * 4096:(q + 1) * 4096]
                .rearrange("p (c k) -> p c k", c=8),
                w0t_d[o8, :, q * 8:(q + 1) * 8, :], f"w{q}", waits=w)

    def b_small(o8, waits_b):
        dma("gpsimd", b0_b[:, o8 % 2], b0_d[o8], "wl", waits=waits_b)
        dma("gpsimd", blt_b[:, o8 % 2].rearrange("q (l c) -> q l c", l=4),
            blt_d[o8], "wl")

    def b_masks(o8, waits_m):
        for i in range(3):
            dma("sync", mkt(i).rearrange("p (t c) -> p t c", t=TB),
                m_d[i][o8], "mk", waits=waits_m if i == 0 else ())

    b_quarters(0, extra_waits=[("te", post_a9[0]), ("ve", post_a9[1])])
    b_small(0, ())
    b_masks(0, [("te", P.cnt("te"))])

    SETS = [bank[0:4], bank[4:8]]
    NH = NO8 * 2
    evac_ve = {}
    sel_ve = {}
    lora_te = {}
    tail_te = {}

    def app_lora(k, gi):
        o8, h = k // 2, k % 2
        g = h * 4 + gi
        st = SETS[k % 2]
        if gi == 0:
            w = [("ve", evac_ve[k]), ("wl", 32 * (o8 + 1))]
        else:
            w = [("ve", sel_ve[(k, gi - 1)])]
        P.step("tensor", waits=w)
        for l in range(4):
            P.step("tensor", lambda e, ll=l, gg=g, o=o8, s=st: e.matmul(
                s[ll][:],
                h_b[:, ll * NTOK + gg * 128: ll * NTOK + gg * 128 + 128],
                blt_b[:, o % 2, ll * OSLAB:(ll + 1) * OSLAB],
                start=True, stop=True),
                incs=[("te", 1)] if l == 3 else [])
        lora_te[(k, gi)] = P.cnt("te")

    def app_sel(k, gi):
        o8, h = k // 2, k % 2
        g = h * 4 + gi
        st = SETS[k % 2]
        P.step("vector", lambda e, s=st: e.tensor_copy(
            out=acc_v[:], in_=s[0][:]),
            waits=[("te", lora_te[(k, gi)]), ("mk", 48 * (o8 + 1)),
                   ("ve", P.cnt("ve"))], incs=[("ve", 1)])
        for i in range(3):
            P.step("vector", lambda e, ii=i, gg=g, s=st: e.copy_predicated(
                out=acc_v[:],
                mask=mkt(ii)[:, gg * OSLAB:(gg + 1) * OSLAB],
                data=s[1 + ii][:]),
                waits=[("ve", P.cnt("ve"))], incs=[("ve", 1)])
        sel_ve[(k, gi)] = P.cnt("ve")
        P.step("vector", lambda e, gg=g, o=o8: e.scalar_tensor_tensor(
            out=v_tile(gg, o), in0=acc_v[:], scalar=8.0, in1=v_tile(gg, o),
            op0=OP.mult, op1=OP.add,
            accum_out=vsum_p[:, gg * 8 + o: gg * 8 + o + 1]),
            waits=[("ve", P.cnt("ve"))], incs=[("ve", 1)])
        P.step("vector", lambda e, gg=g, o=o8: e.scalar_tensor_tensor(
            out=s2_scr[:], in0=v_tile(gg, o), scalar=1.0, in1=v_tile(gg, o),
            op0=OP.mult, op1=OP.mult,
            accum_out=vsq_p[:, gg * 8 + o: gg * 8 + o + 1]),
            waits=[("ve", P.cnt("ve"))], incs=[("ve", 1)])

    for k in range(NH):
        o8, h = k // 2, k % 2
        st = SETS[k % 2]
        P.step("tensor",
               waits=[("ve", ve_h if k < 2 else sel_ve[(k - 2, 3)])])
        for fc in range(FC):
            if h == 0 and fc % 8 == 0:
                P.step("tensor", waits=[(f"w{fc // 8}", 16 * (o8 + 1))])
            for gi in range(4):
                g = h * 4 + gi
                inc = [("tq", 1)] if (h == 1 and fc % 8 == 7 and gi == 3) \
                    else []
                P.step("tensor", lambda e, gg=g, f=fc, s=st, i=gi: e.matmul(
                    s[i][:], x1_blk(gg, f), w0_blk(f),
                    start=(f == 0), stop=False), incs=inc)
            if k > 0 and fc % 8 == 7:
                app_lora(k - 1, fc // 8)
                app_sel(k - 1, fc // 8)
        P.step("tensor", waits=[("wl", 32 * (o8 + 1))])
        for gi in range(4):
            g = h * 4 + gi
            P.step("tensor", lambda e, o=o8, s=st, i=gi: e.matmul(
                s[i][:], onesb_b[:], b0_b[:, o % 2],
                start=False, stop=False))
            for j in range(4):
                P.step("tensor", lambda e, gg=g, o=o8, jj=j, s=st, i=gi:
                       e.matmul(
                           s[i][:, jj * 128:(jj + 1) * 128],
                           x1_blk(gg, 4 * o + jj), ident_b[:],
                           start=False, stop=(jj == 3)),
                       incs=[("te", 1)] if j == 3 else [])
        tail_te[k] = P.cnt("te")
        for gi in range(4):
            g = h * 4 + gi
            P.step("vector", lambda e, gg=g, o=o8, s=st, i=gi: e.tensor_copy(
                out=v_tile(gg, o), in_=s[i][:]),
                waits=[("te", tail_te[k]), ("ve", P.cnt("ve"))],
                incs=[("ve", 1)])
        evac_ve[k] = P.cnt("ve")
        if h == 0 and o8 + 1 < NO8:
            b_quarters(o8 + 1)
            b_small(o8 + 1, [("te", P.cnt("te"))])
        if h == 0 and o8 >= 1:
            b_masks(o8, [("ve", P.cnt("ve"))])
    for gi in range(4):
        app_lora(NH - 1, gi)
        app_sel(NH - 1, gi)

    te_b = P.cnt("te")
    for g in range(TB):
        P.step("vector", lambda e, gg=g: e.reduce_sum(
            sm[:, 5:6], vsum_p[:, gg * 8:(gg + 1) * 8], axis=AX),
            waits=[("ve", P.cnt("ve")), ("te", te_b)], incs=[("ve", 1)])
        P.step("vector", lambda e, gg=g: e.reduce_sum(
            sm[:, 6:7], vsq_p[:, gg * 8:(gg + 1) * 8], axis=AX),
            incs=[("ve", 1)])
        P.step("vector", lambda e: e.tensor_scalar_mul(
            sm[:, 5:6], sm[:, 5:6], 1.0 / D),
            waits=[("ve", P.cnt("ve"))], incs=[("ve", 1)])
        P.step("vector", lambda e: e.tensor_scalar_mul(
            sm[:, 6:7], sm[:, 6:7], 1.0 / D), incs=[("ve", 1)])
        P.step("vector", lambda e: e.tensor_mul(
            sm[:, 7:8], sm[:, 5:6], sm[:, 5:6]),
            waits=[("ve", P.cnt("ve"))], incs=[("ve", 1)])
        P.step("vector", lambda e: e.tensor_sub(
            sm[:, 7:8], sm[:, 6:7], sm[:, 7:8]),
            waits=[("ve", P.cnt("ve"))], incs=[("ve", 1)])
        P.step("vector", lambda e: e.tensor_scalar_add(
            sm[:, 7:8], sm[:, 7:8], EPS),
            waits=[("ve", P.cnt("ve"))], incs=[("ve", 1)])
        P.step("vector", lambda e: e.reciprocal(
            sm[:, 8:9], sm[:, 7:8]),
            waits=[("ve", P.cnt("ve"))], incs=[("ve", 1)])
        P.step("scalar", lambda e: e.activation(
            out=sm[:, 9:10], in_=sm[:, 8:9], func=AF.Sqrt),
            waits=[("ve", P.cnt("ve"))], incs=[("se", 1)])
        P.step("vector", lambda e: e.tensor_mul(
            sm[:, 10:11], sm[:, 5:6], sm[:, 9:10]),
            waits=[("se", P.cnt("se"))], incs=[("ve", 1)])
        P.step("vector", lambda e: e.tensor_scalar_mul(
            sm[:, 10:11], sm[:, 10:11], -1.0),
            waits=[("ve", P.cnt("ve"))], incs=[("ve", 1)])
        b = g % 2
        ssem = "sA" if b == 0 else "sB"
        w = [("ve", P.cnt("ve"))]
        if g >= 2:
            w.append((ssem, P.cnt(ssem)))
        P.step("vector", lambda e, gg=g, bb=b: e.tensor_scalar(
            out=out_view(bb), in0=v_row(gg), scalar1=sm[:, 9:10],
            scalar2=sm[:, 10:11], op0=OP.mult, op1=OP.add),
            waits=w, incs=[("ve", 1)])
        dma("sync", out_d[g], out_view(b), ssem, waits=[("ve", P.cnt("ve"))])

    P.step("sync", waits=[("sA", P.cnt("sA")), ("sB", P.cnt("sB")),
                          ("wl", P.cnt("wl")), ("xf", P.cnt("xf"))])

    with nc.Block() as block:
        @block.sync
        def _(eng):
            P.emit("sync", eng, sems)

        @block.gpsimd
        def _(eng):
            P.emit("gpsimd", eng, sems)

        @block.tensor
        def _(eng):
            P.emit("tensor", eng, sems)

        @block.vector
        def _(eng):
            P.emit("vector", eng, sems)

        @block.scalar
        def _(eng):
            P.emit("scalar", eng, sems)

    ctx.close()
    return nc


_NC = {}


def kernel(L0_attn_x, L0_ffn_W0, L0_ffn_b0, L0_ffn_A, L0_ffn_B):
    x = np.asarray(L0_attn_x, dtype=np.float32)
    w0 = np.asarray(L0_ffn_W0, dtype=np.float32)
    b0 = np.asarray(L0_ffn_b0, dtype=np.float32)
    a_w = np.asarray(L0_ffn_A, dtype=np.float32)
    b_w = np.asarray(L0_ffn_B, dtype=np.float32)
    if "nc" not in _NC:
        _NC["nc"] = build_program()
    in_maps = [_core_inputs(c, x, w0, b0, a_w, b_w) for c in range(N_CORES)]
    res = run_bass_kernel_spmd(_NC["nc"], in_maps, core_ids=list(range(N_CORES)))
    out = np.concatenate([r["out"].reshape(-1, D) for r in res.results], axis=0)
    return out.reshape(1, SEQ, D).astype(np.float32)


# version 19
# speedup vs baseline: 1.2637x; 1.0052x over previous
import numpy as np
import ml_dtypes

import concourse.bass as bass
import concourse.mybir as mybir
from concourse.bass_utils import run_bass_kernel_spmd

F32 = mybir.dt.float32
BF16 = mybir.dt.bfloat16
U8 = mybir.dt.uint8
AX = mybir.AxisListType.X
AF = mybir.ActivationFunctionType
OP = mybir.AluOpType
BF = ml_dtypes.bfloat16

N_CORES = 8
SEQ = 8192
D = 4096
KHALF = 2049
KPAD = 2176
KC = KPAD // 128
FC = D // 128
OSLAB = 512
NO8 = D // OSLAB
EPS = 1e-5
NPASS = 2
NTOK = SEQ // N_CORES
PTOK = NTOK // NPASS
TBP = PTOK // 128
TB = NTOK // 128

_CONST = {}


def _constants():
    if _CONST:
        return _CONST
    k = np.arange(KPAD, dtype=np.float64)
    f = np.arange(D, dtype=np.float64)
    valid = (k < KHALF).astype(np.float64)
    ang = 2.0 * np.pi * np.outer(f, k) / D
    wc = np.cos(ang) * valid
    ws = -np.sin(ang) * valid
    ck = np.where((k == 0) | (k == D // 2), 1.0, 2.0) * valid / D
    tau = np.arange(D, dtype=np.float64)
    iang = 2.0 * np.pi * np.outer(k, tau) / D
    ctab = np.cos(iang) * ck[:, None]
    vc = np.cos(iang) * ck[:, None]
    vs = -np.sin(iang) * ck[:, None]

    def fm_chunks(tab):
        out = np.empty((KC, 128, FC, 128), dtype=BF)
        for j in range(KC):
            out[j] = tab[:, j * 128:(j + 1) * 128].reshape(FC, 128, 128) \
                .transpose(1, 0, 2)
        return out

    wtab = np.ascontiguousarray(
        np.concatenate([fm_chunks(wc), fm_chunks(ws)], axis=0))

    def tm_chunks(tab):
        out = np.empty((8, KC, 128, 512), dtype=BF)
        for sl in range(8):
            for j in range(KC):
                out[sl, j] = tab[j * 128:(j + 1) * 128,
                                 sl * 512:(sl + 1) * 512]
        return out

    TAU = 2176
    ctab_h = np.zeros((KPAD, TAU), dtype=np.float64)
    ctab_h[:, :2049] = ctab[:, :2049]
    ctab_c = np.empty((5, KC, 128, 512), dtype=BF)
    for sl in range(5):
        w = 512 if sl < 4 else 128
        for j in range(KC):
            ctab_c[sl, j, :, :w] = ctab_h[j * 128:(j + 1) * 128,
                                          sl * 512:sl * 512 + w]
            if w < 512:
                ctab_c[sl, j, :, w:] = 0
    ctab_c = np.ascontiguousarray(ctab_c.transpose(0, 2, 1, 3))
    cpf = np.where((f == 0) | (f == D // 2), 1.0, 2.0) * (f < 2049)
    wc2 = np.cos(ang) * valid * cpf[:, None]
    wtab2 = np.ascontiguousarray(fm_chunks(wc2)[:KC, :, :KC, :])
    irtab = np.ascontiguousarray(
        np.concatenate([tm_chunks(vc), tm_chunks(vs)], axis=1)
        .transpose(0, 2, 1, 3))

    ident = np.ascontiguousarray(np.eye(128), dtype=BF)
    onesb = np.ones((1, 128), dtype=BF)

    import jax
    with jax.default_device(jax.devices("cpu")[0]):
        u = np.asarray(jax.random.uniform(
            jax.random.key(42, impl="threefry2x32"), (1, SEQ, D)))[0]
    masks = [((u >= 0.25 * l) & (u < 0.25 * (l + 1))).astype(np.uint8)
             for l in (1, 2, 3)]

    _CONST.update(wtab=wtab, wtab2=wtab2, ctab=ctab_c, irtab=irtab,
                  ident=ident, onesb=onesb, masks=masks)
    return _CONST


def _core_inputs(core, x, w0, b0, a_w, b_w):
    c = _constants()
    t0 = core * NTOK
    xs = x[0, t0:t0 + NTOK]
    x_fm = np.ascontiguousarray(
        xs.reshape(NPASS, PTOK, D).transpose(0, 2, 1)
        .reshape(NPASS, FC, 128, PTOK).transpose(0, 2, 1, 3))
    x_tm = np.ascontiguousarray(
        xs.reshape(NPASS, TBP, 128, 8, 512).transpose(0, 3, 2, 1, 4))
    w0t = np.ascontiguousarray(
        w0.T.reshape(FC, 128, NO8, OSLAB).transpose(2, 1, 0, 3))
    acat = np.ascontiguousarray(
        a_w.transpose(2, 0, 1).reshape(FC, 128, 64).transpose(1, 0, 2))
    blt = np.ascontiguousarray(
        b_w.transpose(0, 2, 1).reshape(4, 16, NO8, OSLAB)
        .transpose(2, 1, 0, 3))
    m_sl = [np.ascontiguousarray(
        m[t0:t0 + NTOK].reshape(TB, 128, NO8, OSLAB).transpose(2, 1, 0, 3))
        for m in c["masks"]]
    b0sl = np.ascontiguousarray(b0.reshape(NO8, 1, OSLAB))
    return dict(
        x_fm=x_fm.astype(np.float32), x_tm=x_tm.astype(np.float32),
        w0t=w0t.astype(np.float32), b0sl=b0sl.astype(np.float32),
        acat=acat.astype(np.float32), blt=blt.astype(np.float32),
        m1=m_sl[0], m2=m_sl[1], m3=m_sl[2],
        wtab=c["wtab"], wtab2=c["wtab2"], ctab=c["ctab"], irtab=c["irtab"],
        ident=c["ident"], identf=np.eye(128, dtype=np.float32),
        onesb=c["onesb"],
    )


class Plan:
    def __init__(self):
        self.steps = []
        self.counts = {}

    def cnt(self, s):
        return self.counts.get(s, 0)

    def step(self, engine, fn=None, waits=(), incs=()):
        self.steps.append((engine, list(waits), fn, list(incs)))
        for s, n in incs:
            self.counts[s] = self.counts.get(s, 0) + n

    def emit(self, engine_name, eng, sems):
        for e, waits, fn, incs in self.steps:
            if e != engine_name:
                continue
            for s, v in waits:
                if v > 0:
                    eng.wait_ge(sems[s], v)
            if fn is None:
                continue
            instr = fn(eng)
            for s, n in incs:
                instr.then_inc(sems[s], n)


def build_program(debug=False):
    nc = bass.Bass("TRN2", target_bir_lowering=False, debug=False)

    din = lambda name, shape, dt: nc.dram_tensor(
        name, list(shape), dt, kind="ExternalInput").ap()
    x_fm_d = din("x_fm", [NPASS, 128, FC, PTOK], F32)
    x_tm_d = din("x_tm", [NPASS, 8, 128, TBP, 512], F32)
    w0t_d = din("w0t", [NO8, 128, FC, OSLAB], F32)
    b0_d = din("b0sl", [NO8, 1, OSLAB], F32)
    acat_d = din("acat", [128, FC, 64], F32)
    blt_d = din("blt", [NO8, 16, 4, OSLAB], F32)
    m_d = [din(f"m{l}", [NO8, 128, TB, OSLAB], U8) for l in (1, 2, 3)]
    wtab_d = din("wtab", [2 * KC, 128, FC, 128], BF16)
    ctab_d = din("ctab", [5, 128, KC, 512], BF16)
    wtab2_d = din("wtab2", [KC, 128, KC, 128], BF16)
    irtab_d = din("irtab", [8, 128, 2 * KC, 512], BF16)
    ident_d = din("ident", [128, 128], BF16)
    identf_d = din("identf", [128, 128], F32)
    onesb_d = din("onesb", [1, 128], BF16)
    out_d = nc.dram_tensor("out", [TB, 128, D], F32, kind="ExternalOutput").ap()
    dbg = {}
    if debug:
        for nm, shape in [("dbg_F", [128, 2 * KC * PTOK]),
                          ("dbg_corr", [128, TBP * 2176]),
                          ("dbg_sfm", [128, TBP * 2176]),
                          ("dbg_u", [128, 2 * KC * PTOK]),
                          ("dbg_v1", [128, TBP * D]),
                          ("dbg_x1", [128, TB * D]),
                          ("dbg_h", [16, 4 * NTOK]),
                          ("dbg_acat", [128, 2048]),
                          ("dbg_v", [128, TB * D])]:
            dbg[nm] = nc.dram_tensor(nm, shape, BF16, kind="ExternalOutput").ap()

    def dump(nm, src_ap, eng="sync"):
        if debug:
            P.step(eng, lambda e, o=dbg[nm], i=src_ap: e.dma_start(out=o, in_=i),
                   waits=[("ve", P.cnt("ve")), ("te", P.cnt("te")),
                          ("se", P.cnt("se"))], incs=[("sA", 16)])
            for en in ("vector", "tensor", "scalar", "gpsimd"):
                P.step(en, waits=[("sA", P.cnt("sA"))])

    from contextlib import ExitStack
    ctx = ExitStack()
    sb = lambda n, s, dt: ctx.enter_context(nc.sbuf_tensor(n, list(s), dt))
    psT = lambda n, s, dt=F32: ctx.enter_context(nc.psum_tensor(n, list(s), dt))

    ALPHA = sb("ALPHA", [128, 34816], BF16)
    BETA = sb("BETA", [128, 32768], BF16)
    GAMMA = sb("GAMMA", [128, 16384], BF16)
    PTS = sb("PTS", [128, 8704], BF16)
    h_b = sb("h_b", [16, 4 * NTOK], BF16)
    e_scr = sb("e_scr", [128, 512], BF16)
    s2_scr = sb("s2_scr", [128, 512], BF16)
    blt_b = sb("blt_b", [16, 2, 4 * OSLAB], BF16)
    b0_b = sb("b0_b", [1, 2, OSLAB], BF16)
    ctab_b = sb("ctab_b", [128, 4, 512], BF16)
    ident_b = sb("ident_b", [128, 128], BF16)
    identf_b = sb("identf_b", [128, 128], F32)
    onesb_b = sb("onesb_b", [1, 128], BF16)
    m8 = sb("m8", [128, 8], BF16)
    mf = sb("mf", [128, 8], F32)
    sm = sb("sm", [128, 16], F32)
    vsum_p = sb("vsum_p", [128, 64], F32)
    vsq_p = sb("vsq_p", [128, 64], F32)

    def A(o, n, dt=BF16):
        v = ALPHA[:, o:o + n]
        return v.bitcast(dt) if dt != BF16 else v

    def G(o, n, dt=BF16):
        v = GAMMA[:, o:o + n]
        return v.bitcast(dt) if dt != BF16 else v

    def PT(o, n, dt=BF16):
        v = PTS[:, o:o + n]
        return v.bitcast(dt) if dt != BF16 else v

    F_sz = 2 * KC * PTOK
    F_at = lambda kc: A(kc * PTOK, PTOK)
    CORR0 = F_sz
    TAU = 2176
    TSW = [512, 512, 512, 512, 128]
    corr_tb = lambda t: A(CORR0 + t * TAU, TAU)
    corr_tile = lambda t, sl, w: A(CORR0 + t * TAU + sl * 512, w)
    corr_pad = lambda t: A(CORR0 + t * TAU + 2049, 127)
    sfm_blk = lambda t, fcc: A(CORR0 + t * TAU + fcc * 128, 128)
    v1_tile = lambda t, isl: A(CORR0 + t * D + isl * 512, 512)
    v1_row = lambda t: A(CORR0 + t * D, D)
    xfm_blk = lambda fcc: A(CORR0 + fcc * PTOK, PTOK)
    v_tile = lambda g, o8: A((g * 8 + o8) * 512, 512)
    v_row = lambda g: A(g * D, D)
    acat_blk = lambda fcc, l: PT(fcc * 64 + l * 16, 16)

    wtab_blk = lambda buf, fcc: G(buf * 4096 + fcc * 128, 128)
    w2_blk = lambda buf, fcc: G(buf * 2176 + fcc * 128, 128)
    xtm_tile = lambda buf, t: G(buf * 2048 + t * 512, 512)
    irt_g = lambda buf, j: G(4096 + buf * 4096 + j * 512, 512)
    ctab_g = lambda buf, j: G(buf * 4096 + j * 512, 512)
    w0_blk = lambda fcc: G(fcc * 512, 512)
    out_view = lambda buf: G(buf * 8192, 8192, F32)

    P_blk = lambda kc, t: PT(kc * 512 + t * 128, 128)
    tscr = PT(0, 8192, F32)
    mscr = lambda i: PT(i * 1024, 1024, F32)
    x1t = PT(0, 8192, F32)
    acc_v = PT(0, 1024, F32)
    mkt = lambda i: PTS[:, 2048 + i * 2048: 2048 + (i + 1) * 2048].bitcast(U8)

    x1_blk = lambda g, fcc: BETA[:, g * D + fcc * 128: g * D + fcc * 128 + 128]

    bank = [psT(f"bk{i}", [128, 512]) for i in range(8)]
    tr_ps = lambda b: bank[4 + b][:, 0:128]

    def warm():
        P.step("tensor", lambda e: e.matmul(
            bank[6][:, 0:128], ident_b[:], ident_b[:], start=True, stop=True),
            waits=[("ve", P.cnt("ve"))])

    sems = {s: ctx.enter_context(nc.semaphore(s))
            for s in ("te", "ve", "se", "cs", "xf", "tA", "tB",
                      "cA", "cB", "cC", "cD", "iA", "iB", "iC", "iD",
                      "xA", "xB", "wl", "w0", "w1", "w2", "w3",
                      "mk", "ac", "sA", "sB", "tq")}
    P = Plan()

    def dma(engine, out_ap, in_ap, sname, waits=()):
        P.step(engine, lambda e, o=out_ap, i=in_ap: e.dma_start(out=o, in_=i),
               waits=waits, incs=[(sname, 16)])

    dma("sync", ident_b[:], ident_d[:], "cs")
    dma("sync", identf_b[:], identf_d[:], "cs")
    dma("sync", onesb_b[:], onesb_d[:], "cs")

    ln1_affine_ve = []
    for ps in range(NPASS):
        for q in range(4):
            w = FC // 4
            dma("gpsimd",
                A(CORR0 + q * w * PTOK, w * PTOK)
                .rearrange("p (c t) -> p c t", c=w),
                x_fm_d[ps, :, q * w:(q + 1) * w, :], "xf",
                waits=[("ve", ln1_affine_ve[q] if ps > 0 else P.cnt("ve"))])

        dma("sync", GAMMA[:, 0:4096].rearrange("p (c k) -> p c k", c=FC),
            wtab_d[0], "tA", waits=[("te", P.cnt("te"))])
        dma("sync", GAMMA[:, 4096:8192].rearrange("p (c k) -> p c k", c=FC),
            wtab_d[1], "tB")
        P.step("tensor", waits=[("xf", P.cnt("xf")), ("cs", P.cnt("cs")),
                                ("ve", P.cnt("ve"))])
        a2_evac = {}
        a3_ve = {}
        for kc in range(2 * KC):
            buf = kc % 2
            tsem = "tA" if buf == 0 else "tB"
            w = [(tsem, P.cnt(tsem))]
            if kc >= 2:
                w.append(("ve", a2_evac[kc - 2]))
            P.step("tensor", waits=w)
            for fc in range(FC):
                P.step("tensor", lambda e, b=buf, f=fc, k=kc: e.matmul(
                    bank[b][:], wtab_blk(b, f), xfm_blk(f),
                    start=(f == 0), stop=(f == FC - 1)),
                    incs=[("te", 1)] if fc == FC - 1 else [])
            if kc + 2 < 2 * KC:
                dma("sync",
                    GAMMA[:, buf * 4096:buf * 4096 + 4096]
                    .rearrange("p (c k) -> p c k", c=FC),
                    wtab_d[kc + 2], tsem, waits=[("te", P.cnt("te"))])
            P.step("vector", lambda e, b=buf, k=kc: e.tensor_copy(
                out=F_at(k), in_=bank[b][:]),
                waits=[("te", P.cnt("te"))], incs=[("ve", 1)])
            a2_evac[kc] = P.cnt("ve")
            if kc >= KC:
                k3 = kc - KC
                P.step("scalar", lambda e, k=k3: e.activation(
                    out=e_scr[:], in_=F_at(k), func=AF.Square),
                    waits=[("ve", a2_evac[kc])], incs=[("se", 1)])
                P.step("scalar", lambda e, k=k3: e.activation(
                    out=s2_scr[:], in_=F_at(KC + k), func=AF.Square),
                    incs=[("se", 1)])
                P.step("vector", lambda e, k=k3: e.tensor_add(
                    PTS[:, k * 512:(k + 1) * 512], e_scr[:], s2_scr[:]),
                    waits=[("se", P.cnt("se"))], incs=[("ve", 1)])
                a3_ve[k3] = P.cnt("ve")
        ve_F = P.cnt("ve")
        ve_P = P.cnt("ve")
        if ps == 0:
            dump("dbg_F", ALPHA[:, 0:F_sz])

        CGRP = [(0, 6), (6, 6), (12, 5)]
        def a4_tab(ts_, gi_, buf_, w=()):
            k0, ng = CGRP[gi_]
            dma("sync", GAMMA[:, buf_ * 4096: buf_ * 4096 + ng * 512]
                .rearrange("p (k c) -> p k c", k=ng),
                ctab_d[ts_][:, k0:k0 + ng, :], "cA" if buf_ == 0 else "cB", w)
        a4_tab(0, 0, 0, [("te", P.cnt("te"))])
        a4_tab(0, 1, 1)
        pend_evac = []
        gidx = 0
        for ts in range(5):
            bb = (ts % 2) * 4
            for gi in range(3):
                k0, ng = CGRP[gi]
                buf = gidx % 2
                csem = "cA" if buf == 0 else "cB"
                vtgt = a3_ve[k0 + ng - 1]
                if ts == 0:
                    vtgt = max(vtgt, a2_evac[2 * KC - 1])
                w = [(csem, P.cnt(csem)), ("ve", vtgt)]
                if gi == 0 and ts >= 2:
                    w.append(("ve", pend_evac[ts - 2]))
                P.step("tensor", waits=w)
                for kk2 in range(ng):
                    kc = k0 + kk2
                    for t in range(TBP):
                        P.step("tensor", lambda e, tt=t, k=kc, b=buf, j=kk2,
                               bo=bb, s=ts: e.matmul(
                            bank[bo + tt][:, 0:TSW[s]], P_blk(k, tt),
                            ctab_g(b, j)[:, 0:TSW[s]],
                            start=(k == 0), stop=(k == KC - 1)),
                            incs=[("te", 1)] if t == TBP - 1 else [])
                ngi = gidx + 2
                if ngi < 15:
                    nts, ngj = divmod(ngi, 3)
                    a4_tab(nts, ngj, buf, [("te", P.cnt("te"))])
                gidx += 1
            te_now = P.cnt("te")
            for t in range(TBP):
                P.step("vector", lambda e, tt=t, s=ts, bo=bb: e.tensor_copy(
                    out=corr_tile(tt, s, TSW[s]), in_=bank[bo + tt][:, 0:TSW[s]]),
                    waits=[("te", te_now)], incs=[("ve", 1)])
            pend_evac.append(P.cnt("ve"))
        for t in range(TBP):
            P.step("vector", lambda e, tt=t: e.memset(corr_pad(tt), -1e9),
                   waits=[("ve", P.cnt("ve"))], incs=[("ve", 1)])

        ve_corr = P.cnt("ve")
        if ps == 0:
            dump("dbg_corr", A(CORR0, TBP * TAU))
        for t in range(TBP):
            P.step("vector", lambda e, tt=t: e.max(
                out=m8[:], in_=corr_tb(tt)),
                waits=[("ve", ve_corr)], incs=[("ve", 1)])
            warm()
            P.step("vector", lambda e: e.tensor_copy(out=mf[:], in_=m8[:]),
                   waits=[("ve", P.cnt("ve"))], incs=[("ve", 1)])
            P.step("vector", lambda e: e.tensor_scalar_mul(
                sm[:, 0:1], mf[:, 0:1], -1.0),
                waits=[("ve", P.cnt("ve"))], incs=[("ve", 1)])
            P.step("scalar", lambda e: e.activation(
                out=sm[:, 8:16], in_=mf[:], func=AF.Exp,
                bias=sm[:, 0:1], scale=1.0),
                waits=[("ve", P.cnt("ve"))], incs=[("se", 1)])
            P.step("vector", lambda e: e.reduce_sum(
                sm[:, 1:2], sm[:, 8:16], axis=AX),
                waits=[("se", P.cnt("se"))], incs=[("ve", 1)])
            P.step("scalar", lambda e: e.activation(
                out=sm[:, 2:3], in_=sm[:, 1:2], func=AF.Ln),
                waits=[("ve", P.cnt("ve"))], incs=[("se", 1)])
            P.step("vector", lambda e: e.tensor_sub(
                sm[:, 3:4], sm[:, 0:1], sm[:, 2:3]),
                waits=[("se", P.cnt("se"))], incs=[("ve", 1)])
            P.step("vector", lambda e: e.tensor_copy(
                out=sm[:, 4:5], in_=m8[:, 7:8]),
                waits=[("ve", P.cnt("ve"))], incs=[("ve", 1)])
            warm()
            for cchunk in range(5):
                cs = corr_tile(t, cchunk, TSW[cchunk])
                ww = TSW[cchunk]
                P.step("scalar", lambda e, c=cs, w2=ww: e.activation(
                    out=e_scr[:, 0:w2], in_=c, func=AF.Exp,
                    bias=sm[:, 3:4], scale=1.0),
                    waits=[("ve", P.cnt("ve"))], incs=[("se", 1)])
                P.step("vector", lambda e, c=cs, cc=cchunk, w2=ww:
                       e.scalar_tensor_tensor(
                    out=tscr[:, cc * 512:cc * 512 + w2], in0=c,
                    scalar=sm[:, 4:5], in1=e_scr[:, 0:w2],
                    op0=OP.is_ge, op1=OP.mult),
                    waits=[("se", P.cnt("se"))], incs=[("ve", 1)])
                warm()
            ve_s = P.cnt("ve")
            for fc in range(KC):
                b = fc % 2
                w = [("ve", ve_s)] if fc < 2 else [("ve", P.cnt("ve") - 1)]
                P.step("tensor", lambda e, f=fc, bb=b: e.transpose(
                    tr_ps(bb), tscr[:, f * 128:(f + 1) * 128], identf_b[:]),
                    waits=w, incs=[("te", 1)])
                P.step("vector", lambda e, f=fc, bb=b, tt=t: e.tensor_copy(
                    out=sfm_blk(tt, f), in_=tr_ps(bb)),
                    waits=[("te", P.cnt("te"))], incs=[("ve", 1)])
        ve_sfm = P.cnt("ve")
        if ps == 0:
            dump("dbg_sfm", A(CORR0, TBP * TAU))

        dma("sync", GAMMA[:, 0:2176].rearrange("p (c k) -> p c k", c=KC),
            wtab2_d[0], "tA", waits=[("te", P.cnt("te")), ("ve", ve_sfm)])
        dma("sync", GAMMA[:, 2176:4352].rearrange("p (c k) -> p c k", c=KC),
            wtab2_d[1], "tB")

        def sfm_mov(fcc):
            return A(CORR0, TBP * TAU).rearrange("p (t q) -> p t q", q=TAU)[
                :, :, fcc * 128:fcc * 128 + 128]

        P.step("tensor", waits=[("ve", ve_sfm)])
        for kc in range(KC):
            buf = kc % 2
            tsem = "tA" if buf == 0 else "tB"
            P.step("tensor", waits=[(tsem, P.cnt(tsem)),
                                    ("ve", max(0, P.cnt("ve") - 1))])
            for fc in range(KC):
                P.step("tensor", lambda e, b=buf, f=fc, k=kc: e.matmul(
                    bank[k % 2][:], w2_blk(b, f), sfm_mov(f),
                    start=(f == 0), stop=(f == KC - 1)),
                    incs=[("te", 1)] if fc == KC - 1 else [])
            if kc + 2 < KC:
                dma("sync",
                    GAMMA[:, buf * 2176:buf * 2176 + 2176]
                    .rearrange("p (c k) -> p c k", c=KC),
                    wtab2_d[kc + 2], tsem, waits=[("te", P.cnt("te"))])
            te_now = P.cnt("te")
            fre, fim = F_at(kc), F_at(KC + kc)
            P.step("vector", lambda e, a=fre, k=kc: e.tensor_mul(
                a, a, bank[k % 2][:]),
                waits=[("te", te_now), ("ve", P.cnt("ve"))], incs=[("ve", 1)])
            P.step("vector", lambda e, a=fim, k=kc: e.tensor_mul(
                a, a, bank[k % 2][:]), incs=[("ve", 1)])
            ve_u = P.cnt("ve")
        if ps == 0:
            dump("dbg_u", ALPHA[:, 0:F_sz])

        IGRP = [(0, 8), (8, 8), (16, 8), (24, 8), (32, 2)]
        def a9_tab(isl_, gi_, buf_, w=()):
            k0, ng = IGRP[gi_]
            dma("sync", GAMMA[:, 4096 + buf_ * 4096: 4096 + buf_ * 4096 + ng * 512]
                .rearrange("p (k c) -> p k c", k=ng),
                irtab_d[isl_][:, k0:k0 + ng, :], "iA" if buf_ == 0 else "iB", w)
        a9_tab(0, 0, 0, [("te", P.cnt("te")), ("ve", ve_u)])
        a9_tab(0, 1, 1)
        dma("gpsimd", G(0, 2048).rearrange("p (t c) -> p t c", t=TBP),
            x_tm_d[ps, 0], "xA", waits=[("te", P.cnt("te")), ("ve", ve_u)])
        P.step("tensor", waits=[("ve", ve_u)])
        pend_v1 = []
        gidx9 = 0
        for isl in range(8):
            bb9 = (isl % 2) * 4
            for gi in range(5):
                k0, ng = IGRP[gi]
                buf = gidx9 % 2
                isem = "iA" if buf == 0 else "iB"
                w = [(isem, P.cnt(isem))]
                if gi == 0 and isl >= 2:
                    w.append(("ve", pend_v1[isl - 2]))
                P.step("tensor", waits=w)
                for kk2 in range(ng):
                    kc = k0 + kk2
                    for t in range(TBP):
                        P.step("tensor", lambda e, tt=t, k=kc, b=buf, j=kk2,
                               bo=bb9: e.matmul(
                            bank[bo + tt][:],
                            A(k * PTOK + tt * 128, 128), irt_g(b, j),
                            start=(k == 0), stop=(k == 2 * KC - 1)),
                            incs=[("te", 1)] if t == TBP - 1 else [])
                ngi = gidx9 + 2
                if ngi < 40:
                    nisl, ngj = divmod(ngi, 5)
                    a9_tab(nisl, ngj, buf, [("te", P.cnt("te"))])
                gidx9 += 1
            if isl + 1 < 8:
                xsem2 = "xA" if (isl + 1) % 2 == 0 else "xB"
                dma("gpsimd",
                    G(((isl + 1) % 2) * 2048, 2048)
                    .rearrange("p (t c) -> p t c", t=TBP),
                    x_tm_d[ps, isl + 1], xsem2,
                    waits=[("ve", P.cnt("ve"))])
            te_now = P.cnt("te")
            xb = isl % 2
            xsem = "xA" if xb == 0 else "xB"
            for t in range(TBP):
                P.step("vector", lambda e, tt=t, b=xb, i=isl, bo=bb9: e.tensor_add(
                    v1_tile(tt, i), bank[bo + tt][:], xtm_tile(b, tt)),
                    waits=[("te", te_now), (xsem, P.cnt(xsem))],
                    incs=[("ve", 1)])
                P.step("vector", lambda e, tt=t, i=isl: e.reduce_sum(
                    vsum_p[:, tt * 8 + i: tt * 8 + i + 1], v1_tile(tt, i),
                    axis=AX),
                    waits=[("ve", P.cnt("ve"))], incs=[("ve", 1)])
                P.step("scalar", lambda e, tt=t, i=isl: e.activation(
                    out=s2_scr[:], in_=v1_tile(tt, i), func=AF.Square),
                    waits=[("ve", P.cnt("ve"))], incs=[("se", 1)])
                P.step("vector", lambda e, tt=t, i=isl: e.reduce_sum(
                    vsq_p[:, tt * 8 + i: tt * 8 + i + 1], s2_scr[:], axis=AX),
                    waits=[("se", P.cnt("se"))], incs=[("ve", 1)])
            pend_v1.append(P.cnt("ve"))

        if ps == NPASS - 1:
            post_a9 = (P.cnt("te"), P.cnt("ve"))
        if ps == 0:
            dump("dbg_v1", A(CORR0, TBP * D))
        for t in range(TBP):
            g = ps * TBP + t
            P.step("vector", lambda e, tt=t: e.reduce_sum(
                sm[:, 5:6], vsum_p[:, tt * 8:(tt + 1) * 8], axis=AX),
                waits=[("ve", P.cnt("ve"))], incs=[("ve", 1)])
            P.step("vector", lambda e, tt=t: e.reduce_sum(
                sm[:, 6:7], vsq_p[:, tt * 8:(tt + 1) * 8], axis=AX),
                incs=[("ve", 1)])
            P.step("vector", lambda e: e.tensor_scalar_mul(
                sm[:, 5:6], sm[:, 5:6], 1.0 / D),
                waits=[("ve", P.cnt("ve"))], incs=[("ve", 1)])
            P.step("vector", lambda e: e.tensor_scalar_mul(
                sm[:, 6:7], sm[:, 6:7], 1.0 / D), incs=[("ve", 1)])
            P.step("vector", lambda e: e.tensor_mul(
                sm[:, 7:8], sm[:, 5:6], sm[:, 5:6]),
                waits=[("ve", P.cnt("ve"))], incs=[("ve", 1)])
            P.step("vector", lambda e: e.tensor_sub(
                sm[:, 7:8], sm[:, 6:7], sm[:, 7:8]),
                waits=[("ve", P.cnt("ve"))], incs=[("ve", 1)])
            P.step("vector", lambda e: e.tensor_scalar_add(
                sm[:, 7:8], sm[:, 7:8], EPS),
                waits=[("ve", P.cnt("ve"))], incs=[("ve", 1)])
            P.step("vector", lambda e: e.reciprocal(
                sm[:, 8:9], sm[:, 7:8]),
                waits=[("ve", P.cnt("ve"))], incs=[("ve", 1)])
            warm()
            P.step("scalar", lambda e: e.activation(
                out=sm[:, 9:10], in_=sm[:, 8:9], func=AF.Sqrt),
                waits=[("ve", P.cnt("ve"))], incs=[("se", 1)])
            P.step("vector", lambda e: e.tensor_mul(
                sm[:, 10:11], sm[:, 5:6], sm[:, 9:10]),
                waits=[("se", P.cnt("se"))], incs=[("ve", 1)])
            P.step("vector", lambda e: e.tensor_scalar_mul(
                sm[:, 10:11], sm[:, 10:11], -1.0),
                waits=[("ve", P.cnt("ve"))], incs=[("ve", 1)])
            P.step("vector", lambda e, tt=t: e.tensor_scalar(
                out=x1t[:], in0=v1_row(tt), scalar1=sm[:, 9:10],
                scalar2=sm[:, 10:11], op0=OP.mult, op1=OP.add),
                waits=[("ve", P.cnt("ve"))], incs=[("ve", 1)])
            if ps == 0:
                ln1_affine_ve.append(P.cnt("ve"))
            warm()
            ve_x1 = P.cnt("ve")
            for fc in range(FC):
                b = fc % 2
                w = [("ve", ve_x1)] if fc < 2 else [("ve", P.cnt("ve") - 1)]
                P.step("tensor", lambda e, f=fc, bb=b: e.transpose(
                    tr_ps(bb), x1t[:, f * 128:(f + 1) * 128], identf_b[:]),
                    waits=w, incs=[("te", 1)])
                P.step("vector", lambda e, f=fc, bb=b, gg=g: e.tensor_copy(
                    out=x1_blk(gg, f), in_=tr_ps(bb)),
                    waits=[("te", P.cnt("te"))], incs=[("ve", 1)])
    ve_x1fm = P.cnt("ve")
    if debug:
        dump("dbg_x1", BETA[:, :])

    dma("gpsimd", PTS[:, 0:2048].rearrange("p (c r) -> p c r", c=FC),
        acat_d[:], "ac", waits=[("ve", ve_x1fm), ("te", P.cnt("te"))])

    def x1_mov(th, fcc):
        return BETA[:, :].rearrange("p (g q) -> p g q", q=D)[
            :, th * 4:(th + 1) * 4, fcc * 128:fcc * 128 + 128]

    if debug:
        P.step("sync", waits=[("ac", P.cnt("ac"))])
        dump("dbg_acat", PTS[:, 0:2048])
    P.step("tensor", waits=[("ve", ve_x1fm), ("ac", P.cnt("ac"))])
    for l in range(4):
        for th in range(2):
            bk = bank[l * 2 + th][0:16, :]
            for fc in range(FC):
                P.step("tensor", lambda e, g=bk, ll=l, t=th, f=fc: e.matmul(
                    g, acat_blk(f, ll), x1_mov(t, f),
                    start=(f == 0), stop=(f == FC - 1)),
                    incs=[("te", 1)] if fc == FC - 1 else [])
    te_h = P.cnt("te")
    for l in range(4):
        for th in range(2):
            P.step("vector", lambda e, ll=l, t=th: e.tensor_copy(
                out=h_b[:, ll * NTOK + t * 512: ll * NTOK + t * 512 + 512],
                in_=bank[ll * 2 + t][0:16, :]),
                waits=[("te", te_h)], incs=[("ve", 1)])
    ve_h = P.cnt("ve")
    if debug:
        dump("dbg_h", h_b[:, :])


    def b_quarters(o8, extra_waits=()):
        for q in range(4):
            w = list(extra_waits) if q == 0 else []
            if o8 >= 1:
                w.append(("tq", 4 * (o8 - 1) + q + 1))
            dma("gpsimd",
                GAMMA[:, q * 4096:(q + 1) * 4096]
                .rearrange("p (c k) -> p c k", c=8),
                w0t_d[o8, :, q * 8:(q + 1) * 8, :], f"w{q}", waits=w)

    def b_small(o8, waits_b):
        dma("gpsimd", b0_b[:, o8 % 2], b0_d[o8], "wl", waits=waits_b)
        dma("gpsimd", blt_b[:, o8 % 2].rearrange("q (l c) -> q l c", l=4),
            blt_d[o8], "wl")

    def b_masks(o8, waits_m):
        for i in range(3):
            dma("sync", mkt(i).rearrange("p (t c) -> p t c", t=TB),
                m_d[i][o8], "mk", waits=waits_m if i == 0 else ())

    b_quarters(0, extra_waits=[("te", post_a9[0]), ("ve", post_a9[1])])
    b_small(0, ())
    b_masks(0, [("te", P.cnt("te"))])

    SETS = [bank[0:4], bank[4:8]]
    NH = NO8 * 2
    evac_ve = {}
    sel_ve = {}
    lora_te = {}
    tail_te = {}

    def app_lora(k, gi):
        o8, h = k // 2, k % 2
        g = h * 4 + gi
        st = SETS[k % 2]
        if gi == 0:
            w = [("ve", evac_ve[k]), ("wl", 32 * (o8 + 1))]
        else:
            w = [("ve", sel_ve[(k, gi - 1)])]
        P.step("tensor", waits=w)
        for l in range(4):
            P.step("tensor", lambda e, ll=l, gg=g, o=o8, s=st: e.matmul(
                s[ll][:],
                h_b[:, ll * NTOK + gg * 128: ll * NTOK + gg * 128 + 128],
                blt_b[:, o % 2, ll * OSLAB:(ll + 1) * OSLAB],
                start=True, stop=True),
                incs=[("te", 1)] if l == 3 else [])
        lora_te[(k, gi)] = P.cnt("te")

    def app_sel(k, gi):
        o8, h = k // 2, k % 2
        g = h * 4 + gi
        st = SETS[k % 2]
        P.step("vector", lambda e, s=st: e.tensor_copy(
            out=acc_v[:], in_=s[0][:]),
            waits=[("te", lora_te[(k, gi)]), ("mk", 48 * (o8 + 1)),
                   ("ve", P.cnt("ve"))], incs=[("ve", 1)])
        for i in range(3):
            P.step("vector", lambda e, ii=i, gg=g, s=st: e.copy_predicated(
                out=acc_v[:],
                mask=mkt(ii)[:, gg * OSLAB:(gg + 1) * OSLAB],
                data=s[1 + ii][:]),
                waits=[("ve", P.cnt("ve"))], incs=[("ve", 1)])
        sel_ve[(k, gi)] = P.cnt("ve")
        P.step("vector", lambda e, gg=g, o=o8: e.scalar_tensor_tensor(
            out=v_tile(gg, o), in0=acc_v[:], scalar=8.0, in1=v_tile(gg, o),
            op0=OP.mult, op1=OP.add,
            accum_out=vsum_p[:, gg * 8 + o: gg * 8 + o + 1]),
            waits=[("ve", P.cnt("ve"))], incs=[("ve", 1)])
        P.step("vector", lambda e, gg=g, o=o8: e.scalar_tensor_tensor(
            out=s2_scr[:], in0=v_tile(gg, o), scalar=1.0, in1=v_tile(gg, o),
            op0=OP.mult, op1=OP.mult,
            accum_out=vsq_p[:, gg * 8 + o: gg * 8 + o + 1]),
            waits=[("ve", P.cnt("ve"))], incs=[("ve", 1)])

    for k in range(NH):
        o8, h = k // 2, k % 2
        st = SETS[k % 2]
        P.step("tensor",
               waits=[("ve", ve_h if k < 2 else sel_ve[(k - 2, 3)])])
        for fc in range(FC):
            if h == 0 and fc % 8 == 0:
                P.step("tensor", waits=[(f"w{fc // 8}", 16 * (o8 + 1))])
            for gi in range(4):
                g = h * 4 + gi
                inc = [("tq", 1)] if (h == 1 and fc % 8 == 7 and gi == 3) \
                    else []
                P.step("tensor", lambda e, gg=g, f=fc, s=st, i=gi: e.matmul(
                    s[i][:], x1_blk(gg, f), w0_blk(f),
                    start=(f == 0), stop=False), incs=inc)
            if k > 0 and fc % 8 == 7:
                app_lora(k - 1, fc // 8)
                app_sel(k - 1, fc // 8)
        P.step("tensor", waits=[("wl", 32 * (o8 + 1))])
        for gi in range(4):
            g = h * 4 + gi
            P.step("tensor", lambda e, o=o8, s=st, i=gi: e.matmul(
                s[i][:], onesb_b[:], b0_b[:, o % 2],
                start=False, stop=False))
            for j in range(4):
                P.step("tensor", lambda e, gg=g, o=o8, jj=j, s=st, i=gi:
                       e.matmul(
                           s[i][:, jj * 128:(jj + 1) * 128],
                           x1_blk(gg, 4 * o + jj), ident_b[:],
                           start=False, stop=(jj == 3)),
                       incs=[("te", 1)] if j == 3 else [])
        tail_te[k] = P.cnt("te")
        for gi in range(4):
            g = h * 4 + gi
            P.step("vector", lambda e, gg=g, o=o8, s=st, i=gi: e.tensor_copy(
                out=v_tile(gg, o), in_=s[i][:]),
                waits=[("te", tail_te[k]), ("ve", P.cnt("ve"))],
                incs=[("ve", 1)])
        evac_ve[k] = P.cnt("ve")
        if h == 0 and o8 + 1 < NO8:
            b_quarters(o8 + 1)
            b_small(o8 + 1, [("te", P.cnt("te"))])
        if h == 0 and o8 >= 1:
            b_masks(o8, [("ve", P.cnt("ve"))])
    for gi in range(4):
        app_lora(NH - 1, gi)
        app_sel(NH - 1, gi)

    te_b = P.cnt("te")
    for g in range(TB):
        P.step("vector", lambda e, gg=g: e.reduce_sum(
            sm[:, 5:6], vsum_p[:, gg * 8:(gg + 1) * 8], axis=AX),
            waits=[("ve", P.cnt("ve")), ("te", te_b)], incs=[("ve", 1)])
        P.step("vector", lambda e, gg=g: e.reduce_sum(
            sm[:, 6:7], vsq_p[:, gg * 8:(gg + 1) * 8], axis=AX),
            incs=[("ve", 1)])
        P.step("vector", lambda e: e.tensor_scalar_mul(
            sm[:, 5:6], sm[:, 5:6], 1.0 / D),
            waits=[("ve", P.cnt("ve"))], incs=[("ve", 1)])
        P.step("vector", lambda e: e.tensor_scalar_mul(
            sm[:, 6:7], sm[:, 6:7], 1.0 / D), incs=[("ve", 1)])
        P.step("vector", lambda e: e.tensor_mul(
            sm[:, 7:8], sm[:, 5:6], sm[:, 5:6]),
            waits=[("ve", P.cnt("ve"))], incs=[("ve", 1)])
        P.step("vector", lambda e: e.tensor_sub(
            sm[:, 7:8], sm[:, 6:7], sm[:, 7:8]),
            waits=[("ve", P.cnt("ve"))], incs=[("ve", 1)])
        P.step("vector", lambda e: e.tensor_scalar_add(
            sm[:, 7:8], sm[:, 7:8], EPS),
            waits=[("ve", P.cnt("ve"))], incs=[("ve", 1)])
        P.step("vector", lambda e: e.reciprocal(
            sm[:, 8:9], sm[:, 7:8]),
            waits=[("ve", P.cnt("ve"))], incs=[("ve", 1)])
        P.step("scalar", lambda e: e.activation(
            out=sm[:, 9:10], in_=sm[:, 8:9], func=AF.Sqrt),
            waits=[("ve", P.cnt("ve"))], incs=[("se", 1)])
        P.step("vector", lambda e: e.tensor_mul(
            sm[:, 10:11], sm[:, 5:6], sm[:, 9:10]),
            waits=[("se", P.cnt("se"))], incs=[("ve", 1)])
        P.step("vector", lambda e: e.tensor_scalar_mul(
            sm[:, 10:11], sm[:, 10:11], -1.0),
            waits=[("ve", P.cnt("ve"))], incs=[("ve", 1)])
        b = g % 2
        ssem = "sA" if b == 0 else "sB"
        w = [("ve", P.cnt("ve"))]
        if g >= 2:
            w.append((ssem, P.cnt(ssem)))
        P.step("vector", lambda e, gg=g, bb=b: e.tensor_scalar(
            out=out_view(bb), in0=v_row(gg), scalar1=sm[:, 9:10],
            scalar2=sm[:, 10:11], op0=OP.mult, op1=OP.add),
            waits=w, incs=[("ve", 1)])
        dma("sync", out_d[g], out_view(b), ssem, waits=[("ve", P.cnt("ve"))])

    P.step("sync", waits=[("sA", P.cnt("sA")), ("sB", P.cnt("sB")),
                          ("wl", P.cnt("wl")), ("xf", P.cnt("xf"))])

    with nc.Block() as block:
        @block.sync
        def _(eng):
            P.emit("sync", eng, sems)

        @block.gpsimd
        def _(eng):
            P.emit("gpsimd", eng, sems)

        @block.tensor
        def _(eng):
            P.emit("tensor", eng, sems)

        @block.vector
        def _(eng):
            P.emit("vector", eng, sems)

        @block.scalar
        def _(eng):
            P.emit("scalar", eng, sems)

    ctx.close()
    return nc


_NC = {}


def kernel(L0_attn_x, L0_ffn_W0, L0_ffn_b0, L0_ffn_A, L0_ffn_B):
    x = np.asarray(L0_attn_x, dtype=np.float32)
    w0 = np.asarray(L0_ffn_W0, dtype=np.float32)
    b0 = np.asarray(L0_ffn_b0, dtype=np.float32)
    a_w = np.asarray(L0_ffn_A, dtype=np.float32)
    b_w = np.asarray(L0_ffn_B, dtype=np.float32)
    if "nc" not in _NC:
        _NC["nc"] = build_program()
    in_maps = [_core_inputs(c, x, w0, b0, a_w, b_w) for c in range(N_CORES)]
    res = run_bass_kernel_spmd(_NC["nc"], in_maps, core_ids=list(range(N_CORES)))
    out = np.concatenate([r["out"].reshape(-1, D) for r in res.results], axis=0)
    return out.reshape(1, SEQ, D).astype(np.float32)


# version 20
# speedup vs baseline: 1.2968x; 1.0262x over previous
import numpy as np
import ml_dtypes

import concourse.bass as bass
import concourse.mybir as mybir
from concourse.bass_utils import run_bass_kernel_spmd

F32 = mybir.dt.float32
BF16 = mybir.dt.bfloat16
U8 = mybir.dt.uint8
AX = mybir.AxisListType.X
AF = mybir.ActivationFunctionType
OP = mybir.AluOpType
BF = ml_dtypes.bfloat16

N_CORES = 8
SEQ = 8192
D = 4096
KHALF = 2049
KPAD = 2176
KC = KPAD // 128
FC = D // 128
OSLAB = 512
NO8 = D // OSLAB
EPS = 1e-5
NPASS = 2
NTOK = SEQ // N_CORES
PTOK = NTOK // NPASS
TBP = PTOK // 128
TB = NTOK // 128

_CONST = {}


def _constants():
    if _CONST:
        return _CONST
    k = np.arange(KPAD, dtype=np.float64)
    f = np.arange(D, dtype=np.float64)
    valid = (k < KHALF).astype(np.float64)
    ang = 2.0 * np.pi * np.outer(f, k) / D
    wc = np.cos(ang) * valid
    ws = -np.sin(ang) * valid
    ck = np.where((k == 0) | (k == D // 2), 1.0, 2.0) * valid / D
    tau = np.arange(D, dtype=np.float64)
    iang = 2.0 * np.pi * np.outer(k, tau) / D
    ctab = np.cos(iang) * ck[:, None]
    vc = np.cos(iang) * ck[:, None]
    vs = -np.sin(iang) * ck[:, None]

    def fm_chunks(tab):
        out = np.empty((KC, 128, FC, 128), dtype=BF)
        for j in range(KC):
            out[j] = tab[:, j * 128:(j + 1) * 128].reshape(FC, 128, 128) \
                .transpose(1, 0, 2)
        return out

    wtab = np.ascontiguousarray(
        np.concatenate([fm_chunks(wc), fm_chunks(ws)], axis=0))

    def tm_chunks(tab):
        out = np.empty((8, KC, 128, 512), dtype=BF)
        for sl in range(8):
            for j in range(KC):
                out[sl, j] = tab[j * 128:(j + 1) * 128,
                                 sl * 512:(sl + 1) * 512]
        return out

    TAU = 2176
    ctab_h = np.zeros((KPAD, TAU), dtype=np.float64)
    ctab_h[:, :2049] = ctab[:, :2049]
    ctab_c = np.empty((5, KC, 128, 512), dtype=BF)
    for sl in range(5):
        w = 512 if sl < 4 else 128
        for j in range(KC):
            ctab_c[sl, j, :, :w] = ctab_h[j * 128:(j + 1) * 128,
                                          sl * 512:sl * 512 + w]
            if w < 512:
                ctab_c[sl, j, :, w:] = 0
    ctab_c = np.ascontiguousarray(ctab_c.transpose(0, 2, 1, 3))
    cpf = np.where((f == 0) | (f == D // 2), 1.0, 2.0) * (f < 2049)
    wc2 = np.cos(ang) * valid * cpf[:, None]
    wtab2 = np.ascontiguousarray(fm_chunks(wc2)[:KC, :, :KC, :])
    irtab = np.ascontiguousarray(
        np.concatenate([tm_chunks(vc), tm_chunks(vs)], axis=1)
        .transpose(0, 2, 1, 3))

    ident = np.ascontiguousarray(np.eye(128), dtype=BF)
    onesb = np.ones((1, 128), dtype=BF)

    import jax
    with jax.default_device(jax.devices("cpu")[0]):
        u = np.asarray(jax.random.uniform(
            jax.random.key(42, impl="threefry2x32"), (1, SEQ, D)))[0]
    masks = [((u >= 0.25 * l) & (u < 0.25 * (l + 1))).astype(np.uint8)
             for l in (1, 2, 3)]

    _CONST.update(wtab=wtab, wtab2=wtab2, ctab=ctab_c, irtab=irtab,
                  ident=ident, onesb=onesb, masks=masks)
    return _CONST


def _core_inputs(core, x, w0, b0, a_w, b_w):
    c = _constants()
    t0 = core * NTOK
    xs = x[0, t0:t0 + NTOK]
    x_fm = np.ascontiguousarray(
        xs.reshape(NPASS, PTOK, D).transpose(0, 2, 1)
        .reshape(NPASS, FC, 128, PTOK).transpose(0, 2, 1, 3))
    x_tm = np.ascontiguousarray(
        xs.reshape(NPASS, TBP, 128, 8, 512).transpose(0, 3, 2, 1, 4))
    w0t = np.ascontiguousarray(
        w0.T.reshape(FC, 128, NO8, OSLAB).transpose(2, 1, 0, 3))
    acat = np.ascontiguousarray(
        a_w.transpose(2, 0, 1).reshape(FC, 128, 64).transpose(1, 0, 2))
    blt = np.ascontiguousarray(
        b_w.transpose(0, 2, 1).reshape(4, 16, NO8, OSLAB)
        .transpose(2, 1, 0, 3))
    m_sl = [np.ascontiguousarray(
        m[t0:t0 + NTOK].reshape(TB, 128, NO8, OSLAB).transpose(2, 1, 0, 3))
        for m in c["masks"]]
    b0sl = np.ascontiguousarray(b0.reshape(NO8, 1, OSLAB))
    return dict(
        x_fm=x_fm.astype(np.float32), x_tm=x_tm.astype(np.float32),
        w0t=w0t.astype(np.float32), b0sl=b0sl.astype(np.float32),
        acat=acat.astype(np.float32), blt=blt.astype(np.float32),
        m1=m_sl[0], m2=m_sl[1], m3=m_sl[2],
        wtab=c["wtab"], wtab2=c["wtab2"], ctab=c["ctab"], irtab=c["irtab"],
        ident=c["ident"], identf=np.eye(128, dtype=np.float32),
        onesb=c["onesb"],
    )


class Plan:
    def __init__(self):
        self.steps = []
        self.counts = {}

    def cnt(self, s):
        return self.counts.get(s, 0)

    def step(self, engine, fn=None, waits=(), incs=()):
        self.steps.append((engine, list(waits), fn, list(incs)))
        for s, n in incs:
            self.counts[s] = self.counts.get(s, 0) + n

    def emit(self, engine_name, eng, sems):
        for e, waits, fn, incs in self.steps:
            if e != engine_name:
                continue
            for s, v in waits:
                if v > 0:
                    eng.wait_ge(sems[s], v)
            if fn is None:
                continue
            instr = fn(eng)
            for s, n in incs:
                instr.then_inc(sems[s], n)


def build_program(debug=False):
    nc = bass.Bass("TRN2", target_bir_lowering=False, debug=False)

    din = lambda name, shape, dt: nc.dram_tensor(
        name, list(shape), dt, kind="ExternalInput").ap()
    x_fm_d = din("x_fm", [NPASS, 128, FC, PTOK], F32)
    x_tm_d = din("x_tm", [NPASS, 8, 128, TBP, 512], F32)
    w0t_d = din("w0t", [NO8, 128, FC, OSLAB], F32)
    b0_d = din("b0sl", [NO8, 1, OSLAB], F32)
    acat_d = din("acat", [128, FC, 64], F32)
    blt_d = din("blt", [NO8, 16, 4, OSLAB], F32)
    m_d = [din(f"m{l}", [NO8, 128, TB, OSLAB], U8) for l in (1, 2, 3)]
    wtab_d = din("wtab", [2 * KC, 128, FC, 128], BF16)
    ctab_d = din("ctab", [5, 128, KC, 512], BF16)
    wtab2_d = din("wtab2", [KC, 128, KC, 128], BF16)
    irtab_d = din("irtab", [8, 128, 2 * KC, 512], BF16)
    ident_d = din("ident", [128, 128], BF16)
    identf_d = din("identf", [128, 128], F32)
    onesb_d = din("onesb", [1, 128], BF16)
    out_d = nc.dram_tensor("out", [TB, 128, D], F32, kind="ExternalOutput").ap()
    dbg = {}
    if debug:
        for nm, shape in [("dbg_F", [128, 2 * KC * PTOK]),
                          ("dbg_corr", [128, TBP * 2176]),
                          ("dbg_sfm", [128, TBP * 2176]),
                          ("dbg_u", [128, 2 * KC * PTOK]),
                          ("dbg_v1", [128, TBP * D]),
                          ("dbg_x1", [128, TB * D]),
                          ("dbg_h", [16, 4 * NTOK]),
                          ("dbg_acat", [128, 2048]),
                          ("dbg_v", [128, TB * D])]:
            dbg[nm] = nc.dram_tensor(nm, shape, BF16, kind="ExternalOutput").ap()

    def dump(nm, src_ap, eng="sync"):
        if debug:
            P.step(eng, lambda e, o=dbg[nm], i=src_ap: e.dma_start(out=o, in_=i),
                   waits=[("ve", P.cnt("ve")), ("te", P.cnt("te")),
                          ("se", P.cnt("se"))], incs=[("sA", 16)])
            for en in ("vector", "tensor", "scalar", "gpsimd"):
                P.step(en, waits=[("sA", P.cnt("sA"))])

    from contextlib import ExitStack
    ctx = ExitStack()
    sb = lambda n, s, dt: ctx.enter_context(nc.sbuf_tensor(n, list(s), dt))
    psT = lambda n, s, dt=F32: ctx.enter_context(nc.psum_tensor(n, list(s), dt))

    ALPHA = sb("ALPHA", [128, 34816], BF16)
    BETA = sb("BETA", [128, 32768], BF16)
    GAMMA = sb("GAMMA", [128, 16384], BF16)
    PTS = sb("PTS", [128, 8704], BF16)
    h_b = sb("h_b", [16, 4 * NTOK], BF16)
    e_scr = sb("e_scr", [128, 512], BF16)
    s2_scr = sb("s2_scr", [128, 512], BF16)
    blt_b = sb("blt_b", [16, 2, 4 * OSLAB], BF16)
    b0_b = sb("b0_b", [1, 2, OSLAB], BF16)
    ctab_b = sb("ctab_b", [128, 4, 512], BF16)
    ident_b = sb("ident_b", [128, 128], BF16)
    identf_b = sb("identf_b", [128, 128], F32)
    onesb_b = sb("onesb_b", [1, 128], BF16)
    m8 = sb("m8", [128, 8], BF16)
    mf = sb("mf", [128, 8], F32)
    sm = sb("sm", [128, 16], F32)
    vsum_p = sb("vsum_p", [128, 64], F32)
    vsq_p = sb("vsq_p", [128, 64], F32)

    def A(o, n, dt=BF16):
        v = ALPHA[:, o:o + n]
        return v.bitcast(dt) if dt != BF16 else v

    def G(o, n, dt=BF16):
        v = GAMMA[:, o:o + n]
        return v.bitcast(dt) if dt != BF16 else v

    def PT(o, n, dt=BF16):
        v = PTS[:, o:o + n]
        return v.bitcast(dt) if dt != BF16 else v

    F_sz = 2 * KC * PTOK
    F_at = lambda kc: A(kc * PTOK, PTOK)
    CORR0 = F_sz
    TAU = 2176
    TSW = [512, 512, 512, 512, 128]
    corr_tb = lambda t: A(CORR0 + t * TAU, TAU)
    corr_tile = lambda t, sl, w: A(CORR0 + t * TAU + sl * 512, w)
    corr_pad = lambda t: A(CORR0 + t * TAU + 2049, 127)
    sfm_blk = lambda t, fcc: A(CORR0 + t * TAU + fcc * 128, 128)
    v1_tile = lambda t, isl: A(CORR0 + t * D + isl * 512, 512)
    v1_row = lambda t: A(CORR0 + t * D, D)
    xfm_blk = lambda fcc: A(CORR0 + fcc * PTOK, PTOK)
    v_tile = lambda g, o8: A((g * 8 + o8) * 512, 512)
    v_row = lambda g: A(g * D, D)
    acat_blk = lambda fcc, l: PT(fcc * 64 + l * 16, 16)

    wtab_blk = lambda buf, fcc: G(buf * 4096 + fcc * 128, 128)
    w2_blk = lambda buf, fcc: G(buf * 2176 + fcc * 128, 128)
    xtm_tile = lambda buf, t: G(buf * 2048 + t * 512, 512)
    irt_g = lambda buf, j: G(4096 + buf * 4096 + j * 512, 512)
    ctab_g = lambda buf, j: G(buf * 4096 + j * 512, 512)
    w0_blk = lambda fcc: G(fcc * 512, 512)
    out_view = lambda buf: G(buf * 8192, 8192, F32)

    P_blk = lambda kc, t: PT(kc * 512 + t * 128, 128)
    tscr = PT(0, 8192, F32)
    mscr = lambda i: PT(i * 1024, 1024, F32)
    x1t = PT(0, 8192, F32)
    acc_v = PT(0, 1024, F32)
    mkt = lambda i: PTS[:, 2048 + i * 2048: 2048 + (i + 1) * 2048].bitcast(U8)

    x1_blk = lambda g, fcc: BETA[:, g * D + fcc * 128: g * D + fcc * 128 + 128]

    bank = [psT(f"bk{i}", [128, 512]) for i in range(8)]
    tr_ps = lambda b: bank[4 + b][:, 0:128]

    def warm():
        P.step("tensor", lambda e: e.matmul(
            bank[6][:, 0:128], ident_b[:], ident_b[:], start=True, stop=True),
            waits=[("ve", P.cnt("ve"))])

    sems = {s: ctx.enter_context(nc.semaphore(s))
            for s in ("te", "ve", "se", "cs", "xf", "tA", "tB",
                      "cA", "cB", "cC", "cD", "iA", "iB", "iC", "iD",
                      "xA", "xB", "wl", "w0", "w1", "w2", "w3",
                      "mk", "ac", "sA", "sB", "tq")}
    P = Plan()

    def dma(engine, out_ap, in_ap, sname, waits=()):
        P.step(engine, lambda e, o=out_ap, i=in_ap: e.dma_start(out=o, in_=i),
               waits=waits, incs=[(sname, 16)])

    dma("sync", ident_b[:], ident_d[:], "cs")
    dma("sync", identf_b[:], identf_d[:], "cs")
    dma("sync", onesb_b[:], onesb_d[:], "cs")

    ln1_affine_ve = []
    for ps in range(NPASS):
        for q in range(4):
            w = FC // 4
            dma("gpsimd",
                A(CORR0 + q * w * PTOK, w * PTOK)
                .rearrange("p (c t) -> p c t", c=w),
                x_fm_d[ps, :, q * w:(q + 1) * w, :], "xf",
                waits=[("ve", ln1_affine_ve[q] if ps > 0 else P.cnt("ve"))])

        dma("sync", GAMMA[:, 0:4096].rearrange("p (c k) -> p c k", c=FC),
            wtab_d[0], "tA", waits=[("te", P.cnt("te"))])
        dma("sync", GAMMA[:, 4096:8192].rearrange("p (c k) -> p c k", c=FC),
            wtab_d[1], "tB")
        P.step("tensor", waits=[("xf", P.cnt("xf")), ("cs", P.cnt("cs")),
                                ("ve", P.cnt("ve"))])
        a2_evac = {}
        a3_ve = {}
        for kc in range(2 * KC):
            buf = kc % 2
            tsem = "tA" if buf == 0 else "tB"
            w = [(tsem, P.cnt(tsem))]
            if kc >= 2:
                w.append(("ve", a2_evac[kc - 2]))
            P.step("tensor", waits=w)
            for fc in range(FC):
                P.step("tensor", lambda e, b=buf, f=fc, k=kc: e.matmul(
                    bank[b][:], wtab_blk(b, f), xfm_blk(f),
                    start=(f == 0), stop=(f == FC - 1)),
                    incs=[("te", 1)] if fc == FC - 1 else [])
            if kc + 2 < 2 * KC:
                dma("sync",
                    GAMMA[:, buf * 4096:buf * 4096 + 4096]
                    .rearrange("p (c k) -> p c k", c=FC),
                    wtab_d[kc + 2], tsem, waits=[("te", P.cnt("te"))])
            P.step("vector", lambda e, b=buf, k=kc: e.tensor_copy(
                out=F_at(k), in_=bank[b][:]),
                waits=[("te", P.cnt("te"))], incs=[("ve", 1)])
            a2_evac[kc] = P.cnt("ve")
            if kc >= KC:
                k3 = kc - KC
                P.step("scalar", lambda e, k=k3: e.activation(
                    out=e_scr[:], in_=F_at(k), func=AF.Square),
                    waits=[("ve", a2_evac[kc])], incs=[("se", 1)])
                P.step("scalar", lambda e, k=k3: e.activation(
                    out=s2_scr[:], in_=F_at(KC + k), func=AF.Square),
                    incs=[("se", 1)])
                P.step("vector", lambda e, k=k3: e.tensor_add(
                    PTS[:, k * 512:(k + 1) * 512], e_scr[:], s2_scr[:]),
                    waits=[("se", P.cnt("se"))], incs=[("ve", 1)])
                a3_ve[k3] = P.cnt("ve")
        ve_F = P.cnt("ve")
        ve_P = P.cnt("ve")
        if ps == 0:
            dump("dbg_F", ALPHA[:, 0:F_sz])

        CGRP = [(0, 6), (6, 6), (12, 5)]
        def a4_tab(ts_, gi_, buf_, w=()):
            k0, ng = CGRP[gi_]
            dma("sync", GAMMA[:, buf_ * 4096: buf_ * 4096 + ng * 512]
                .rearrange("p (k c) -> p k c", k=ng),
                ctab_d[ts_][:, k0:k0 + ng, :], "cA" if buf_ == 0 else "cB", w)
        a4_tab(0, 0, 0, [("te", P.cnt("te"))])
        a4_tab(0, 1, 1)
        pend_evac = []
        gidx = 0
        for ts in range(5):
            bb = (ts % 2) * 4
            for gi in range(3):
                k0, ng = CGRP[gi]
                buf = gidx % 2
                csem = "cA" if buf == 0 else "cB"
                vtgt = a3_ve[k0 + ng - 1]
                if ts == 0:
                    vtgt = max(vtgt, a2_evac[2 * KC - 1])
                w = [(csem, P.cnt(csem)), ("ve", vtgt)]
                if gi == 0 and ts >= 2:
                    w.append(("ve", pend_evac[ts - 2]))
                P.step("tensor", waits=w)
                for kk2 in range(ng):
                    kc = k0 + kk2
                    for t in range(TBP):
                        P.step("tensor", lambda e, tt=t, k=kc, b=buf, j=kk2,
                               bo=bb, s=ts: e.matmul(
                            bank[bo + tt][:, 0:TSW[s]], P_blk(k, tt),
                            ctab_g(b, j)[:, 0:TSW[s]],
                            start=(k == 0), stop=(k == KC - 1)),
                            incs=[("te", 1)] if t == TBP - 1 else [])
                ngi = gidx + 2
                if ngi < 15:
                    nts, ngj = divmod(ngi, 3)
                    a4_tab(nts, ngj, buf, [("te", P.cnt("te"))])
                gidx += 1
            te_now = P.cnt("te")
            for t in range(TBP):
                P.step("vector", lambda e, tt=t, s=ts, bo=bb: e.tensor_copy(
                    out=corr_tile(tt, s, TSW[s]), in_=bank[bo + tt][:, 0:TSW[s]]),
                    waits=[("te", te_now)], incs=[("ve", 1)])
            pend_evac.append(P.cnt("ve"))
        for t in range(TBP):
            P.step("vector", lambda e, tt=t: e.memset(corr_pad(tt), -1e9),
                   waits=[("ve", P.cnt("ve"))], incs=[("ve", 1)])

        ve_corr = P.cnt("ve")
        if ps == 0:
            dump("dbg_corr", A(CORR0, TBP * TAU))
        for t in range(TBP):
            P.step("vector", lambda e, tt=t: e.max(
                out=m8[:], in_=corr_tb(tt)),
                waits=[("ve", ve_corr)], incs=[("ve", 1)])
            warm()
            P.step("vector", lambda e: e.tensor_copy(out=mf[:], in_=m8[:]),
                   waits=[("ve", P.cnt("ve"))], incs=[("ve", 1)])
            P.step("vector", lambda e: e.tensor_scalar_mul(
                sm[:, 0:1], mf[:, 0:1], -1.0),
                waits=[("ve", P.cnt("ve"))], incs=[("ve", 1)])
            P.step("scalar", lambda e: e.activation(
                out=sm[:, 8:16], in_=mf[:], func=AF.Exp,
                bias=sm[:, 0:1], scale=1.0),
                waits=[("ve", P.cnt("ve"))], incs=[("se", 1)])
            P.step("vector", lambda e: e.reduce_sum(
                sm[:, 1:2], sm[:, 8:16], axis=AX),
                waits=[("se", P.cnt("se"))], incs=[("ve", 1)])
            P.step("scalar", lambda e: e.activation(
                out=sm[:, 2:3], in_=sm[:, 1:2], func=AF.Ln),
                waits=[("ve", P.cnt("ve"))], incs=[("se", 1)])
            P.step("vector", lambda e: e.tensor_sub(
                sm[:, 3:4], sm[:, 0:1], sm[:, 2:3]),
                waits=[("se", P.cnt("se"))], incs=[("ve", 1)])
            P.step("vector", lambda e: e.tensor_copy(
                out=sm[:, 4:5], in_=m8[:, 7:8]),
                waits=[("ve", P.cnt("ve"))], incs=[("ve", 1)])
            warm()
            for cchunk in range(5):
                cs = corr_tile(t, cchunk, TSW[cchunk])
                ww = TSW[cchunk]
                P.step("scalar", lambda e, c=cs, w2=ww: e.activation(
                    out=e_scr[:, 0:w2], in_=c, func=AF.Exp,
                    bias=sm[:, 3:4], scale=1.0),
                    waits=[("ve", P.cnt("ve"))], incs=[("se", 1)])
                P.step("vector", lambda e, c=cs, cc=cchunk, w2=ww:
                       e.scalar_tensor_tensor(
                    out=tscr[:, cc * 512:cc * 512 + w2], in0=c,
                    scalar=sm[:, 4:5], in1=e_scr[:, 0:w2],
                    op0=OP.is_ge, op1=OP.mult),
                    waits=[("se", P.cnt("se"))], incs=[("ve", 1)])
                warm()
            ve_s = P.cnt("ve")
            for fc in range(KC):
                b = fc % 2
                w = [("ve", ve_s)] if fc < 2 else [("ve", P.cnt("ve") - 1)]
                P.step("tensor", lambda e, f=fc, bb=b: e.transpose(
                    tr_ps(bb), tscr[:, f * 128:(f + 1) * 128], identf_b[:]),
                    waits=w, incs=[("te", 1)])
                P.step("vector", lambda e, f=fc, bb=b, tt=t: e.tensor_copy(
                    out=sfm_blk(tt, f), in_=tr_ps(bb)),
                    waits=[("te", P.cnt("te"))], incs=[("ve", 1)])
        ve_sfm = P.cnt("ve")
        if ps == 0:
            dump("dbg_sfm", A(CORR0, TBP * TAU))

        dma("sync", GAMMA[:, 0:2176].rearrange("p (c k) -> p c k", c=KC),
            wtab2_d[0], "tA", waits=[("te", P.cnt("te")), ("ve", ve_sfm)])
        dma("sync", GAMMA[:, 2176:4352].rearrange("p (c k) -> p c k", c=KC),
            wtab2_d[1], "tB")

        def sfm_mov(fcc):
            return A(CORR0, TBP * TAU).rearrange("p (t q) -> p t q", q=TAU)[
                :, :, fcc * 128:fcc * 128 + 128]

        P.step("tensor", waits=[("ve", ve_sfm)])
        for kc in range(KC):
            buf = kc % 2
            tsem = "tA" if buf == 0 else "tB"
            P.step("tensor", waits=[(tsem, P.cnt(tsem)),
                                    ("ve", max(0, P.cnt("ve") - 1))])
            for fc in range(KC):
                P.step("tensor", lambda e, b=buf, f=fc, k=kc: e.matmul(
                    bank[k % 2][:], w2_blk(b, f), sfm_mov(f),
                    start=(f == 0), stop=(f == KC - 1)),
                    incs=[("te", 1)] if fc == KC - 1 else [])
            if kc + 2 < KC:
                dma("sync",
                    GAMMA[:, buf * 2176:buf * 2176 + 2176]
                    .rearrange("p (c k) -> p c k", c=KC),
                    wtab2_d[kc + 2], tsem, waits=[("te", P.cnt("te"))])
            te_now = P.cnt("te")
            fre, fim = F_at(kc), F_at(KC + kc)
            P.step("vector", lambda e, a=fre, k=kc: e.tensor_mul(
                a, a, bank[k % 2][:]),
                waits=[("te", te_now), ("ve", P.cnt("ve"))], incs=[("ve", 1)])
            P.step("vector", lambda e, a=fim, k=kc: e.tensor_mul(
                a, a, bank[k % 2][:]), incs=[("ve", 1)])
            ve_u = P.cnt("ve")
        if ps == 0:
            dump("dbg_u", ALPHA[:, 0:F_sz])

        IGRP = [(0, 7), (7, 7), (14, 7), (21, 7), (28, 6)]
        def a9_tab(isl_, gi_, buf_, w=()):
            k0, ng = IGRP[gi_]
            dma("sync", GAMMA[:, 4096 + buf_ * 4096: 4096 + buf_ * 4096 + ng * 512]
                .rearrange("p (k c) -> p k c", k=ng),
                irtab_d[isl_][:, k0:k0 + ng, :], "iA" if buf_ == 0 else "iB", w)
        a9_tab(0, 0, 0, [("te", P.cnt("te")), ("ve", ve_u)])
        a9_tab(0, 1, 1)
        dma("gpsimd", G(0, 2048).rearrange("p (t c) -> p t c", t=TBP),
            x_tm_d[ps, 0], "xA", waits=[("te", P.cnt("te")), ("ve", ve_u)])
        P.step("tensor", waits=[("ve", ve_u)])
        pend_v1 = []
        gidx9 = 0
        for isl in range(8):
            bb9 = (isl % 2) * 4
            for gi in range(5):
                k0, ng = IGRP[gi]
                buf = gidx9 % 2
                isem = "iA" if buf == 0 else "iB"
                w = [(isem, P.cnt(isem))]
                if gi == 0 and isl >= 2:
                    w.append(("ve", pend_v1[isl - 2]))
                P.step("tensor", waits=w)
                for kk2 in range(ng):
                    kc = k0 + kk2
                    for t in range(TBP):
                        P.step("tensor", lambda e, tt=t, k=kc, b=buf, j=kk2,
                               bo=bb9: e.matmul(
                            bank[bo + tt][:],
                            A(k * PTOK + tt * 128, 128), irt_g(b, j),
                            start=(k == 0), stop=(k == 2 * KC - 1)),
                            incs=[("te", 1)] if t == TBP - 1 else [])
                ngi = gidx9 + 2
                if ngi < 40:
                    nisl, ngj = divmod(ngi, 5)
                    a9_tab(nisl, ngj, buf, [("te", P.cnt("te"))])
                gidx9 += 1
            if isl + 1 < 8:
                xsem2 = "xA" if (isl + 1) % 2 == 0 else "xB"
                dma("gpsimd",
                    G(((isl + 1) % 2) * 2048, 2048)
                    .rearrange("p (t c) -> p t c", t=TBP),
                    x_tm_d[ps, isl + 1], xsem2,
                    waits=[("ve", P.cnt("ve"))])
            te_now = P.cnt("te")
            xb = isl % 2
            xsem = "xA" if xb == 0 else "xB"
            for t in range(TBP):
                P.step("vector", lambda e, tt=t, b=xb, i=isl, bo=bb9: e.tensor_add(
                    v1_tile(tt, i), bank[bo + tt][:], xtm_tile(b, tt)),
                    waits=[("te", te_now), (xsem, P.cnt(xsem))],
                    incs=[("ve", 1)])
                P.step("vector", lambda e, tt=t, i=isl: e.reduce_sum(
                    vsum_p[:, tt * 8 + i: tt * 8 + i + 1], v1_tile(tt, i),
                    axis=AX),
                    waits=[("ve", P.cnt("ve"))], incs=[("ve", 1)])
                P.step("scalar", lambda e, tt=t, i=isl: e.activation(
                    out=s2_scr[:], in_=v1_tile(tt, i), func=AF.Square),
                    waits=[("ve", P.cnt("ve"))], incs=[("se", 1)])
                P.step("vector", lambda e, tt=t, i=isl: e.reduce_sum(
                    vsq_p[:, tt * 8 + i: tt * 8 + i + 1], s2_scr[:], axis=AX),
                    waits=[("se", P.cnt("se"))], incs=[("ve", 1)])
            pend_v1.append(P.cnt("ve"))

        if ps == NPASS - 1:
            post_a9 = (P.cnt("te"), P.cnt("ve"))
        if ps == 0:
            dump("dbg_v1", A(CORR0, TBP * D))
        for t in range(TBP):
            g = ps * TBP + t
            P.step("vector", lambda e, tt=t: e.reduce_sum(
                sm[:, 5:6], vsum_p[:, tt * 8:(tt + 1) * 8], axis=AX),
                waits=[("ve", P.cnt("ve"))], incs=[("ve", 1)])
            P.step("vector", lambda e, tt=t: e.reduce_sum(
                sm[:, 6:7], vsq_p[:, tt * 8:(tt + 1) * 8], axis=AX),
                incs=[("ve", 1)])
            P.step("vector", lambda e: e.tensor_scalar_mul(
                sm[:, 5:6], sm[:, 5:6], 1.0 / D),
                waits=[("ve", P.cnt("ve"))], incs=[("ve", 1)])
            P.step("vector", lambda e: e.tensor_scalar_mul(
                sm[:, 6:7], sm[:, 6:7], 1.0 / D), incs=[("ve", 1)])
            P.step("vector", lambda e: e.tensor_mul(
                sm[:, 7:8], sm[:, 5:6], sm[:, 5:6]),
                waits=[("ve", P.cnt("ve"))], incs=[("ve", 1)])
            P.step("vector", lambda e: e.tensor_sub(
                sm[:, 7:8], sm[:, 6:7], sm[:, 7:8]),
                waits=[("ve", P.cnt("ve"))], incs=[("ve", 1)])
            P.step("vector", lambda e: e.tensor_scalar_add(
                sm[:, 7:8], sm[:, 7:8], EPS),
                waits=[("ve", P.cnt("ve"))], incs=[("ve", 1)])
            P.step("vector", lambda e: e.reciprocal(
                sm[:, 8:9], sm[:, 7:8]),
                waits=[("ve", P.cnt("ve"))], incs=[("ve", 1)])
            warm()
            P.step("scalar", lambda e: e.activation(
                out=sm[:, 9:10], in_=sm[:, 8:9], func=AF.Sqrt),
                waits=[("ve", P.cnt("ve"))], incs=[("se", 1)])
            P.step("vector", lambda e: e.tensor_mul(
                sm[:, 10:11], sm[:, 5:6], sm[:, 9:10]),
                waits=[("se", P.cnt("se"))], incs=[("ve", 1)])
            P.step("vector", lambda e: e.tensor_scalar_mul(
                sm[:, 10:11], sm[:, 10:11], -1.0),
                waits=[("ve", P.cnt("ve"))], incs=[("ve", 1)])
            P.step("vector", lambda e, tt=t: e.tensor_scalar(
                out=x1t[:], in0=v1_row(tt), scalar1=sm[:, 9:10],
                scalar2=sm[:, 10:11], op0=OP.mult, op1=OP.add),
                waits=[("ve", P.cnt("ve"))], incs=[("ve", 1)])
            if ps == 0:
                ln1_affine_ve.append(P.cnt("ve"))
            warm()
            ve_x1 = P.cnt("ve")
            for fc in range(FC):
                b = fc % 2
                w = [("ve", ve_x1)] if fc < 2 else [("ve", P.cnt("ve") - 1)]
                P.step("tensor", lambda e, f=fc, bb=b: e.transpose(
                    tr_ps(bb), x1t[:, f * 128:(f + 1) * 128], identf_b[:]),
                    waits=w, incs=[("te", 1)])
                P.step("vector", lambda e, f=fc, bb=b, gg=g: e.tensor_copy(
                    out=x1_blk(gg, f), in_=tr_ps(bb)),
                    waits=[("te", P.cnt("te"))], incs=[("ve", 1)])
    ve_x1fm = P.cnt("ve")
    if debug:
        dump("dbg_x1", BETA[:, :])

    dma("gpsimd", PTS[:, 0:2048].rearrange("p (c r) -> p c r", c=FC),
        acat_d[:], "ac", waits=[("ve", ve_x1fm), ("te", P.cnt("te"))])

    def x1_mov(th, fcc):
        return BETA[:, :].rearrange("p (g q) -> p g q", q=D)[
            :, th * 4:(th + 1) * 4, fcc * 128:fcc * 128 + 128]

    if debug:
        P.step("sync", waits=[("ac", P.cnt("ac"))])
        dump("dbg_acat", PTS[:, 0:2048])
    P.step("tensor", waits=[("ve", ve_x1fm), ("ac", P.cnt("ac"))])
    for l in range(4):
        for th in range(2):
            bk = bank[l * 2 + th][0:16, :]
            for fc in range(FC):
                P.step("tensor", lambda e, g=bk, ll=l, t=th, f=fc: e.matmul(
                    g, acat_blk(f, ll), x1_mov(t, f),
                    start=(f == 0), stop=(f == FC - 1)),
                    incs=[("te", 1)] if fc == FC - 1 else [])
    te_h = P.cnt("te")
    for l in range(4):
        for th in range(2):
            P.step("vector", lambda e, ll=l, t=th: e.tensor_copy(
                out=h_b[:, ll * NTOK + t * 512: ll * NTOK + t * 512 + 512],
                in_=bank[ll * 2 + t][0:16, :]),
                waits=[("te", te_h)], incs=[("ve", 1)])
    ve_h = P.cnt("ve")
    if debug:
        dump("dbg_h", h_b[:, :])


    def b_quarters(o8, extra_waits=()):
        for q in range(4):
            w = list(extra_waits) if q == 0 else []
            if o8 >= 1:
                w.append(("tq", 4 * (o8 - 1) + q + 1))
            dma("gpsimd",
                GAMMA[:, q * 4096:(q + 1) * 4096]
                .rearrange("p (c k) -> p c k", c=8),
                w0t_d[o8, :, q * 8:(q + 1) * 8, :], f"w{q}", waits=w)

    def b_small(o8, waits_b):
        dma("gpsimd", b0_b[:, o8 % 2], b0_d[o8], "wl", waits=waits_b)
        dma("gpsimd", blt_b[:, o8 % 2].rearrange("q (l c) -> q l c", l=4),
            blt_d[o8], "wl")

    def b_masks(o8, waits_m):
        for i in range(3):
            dma("sync", mkt(i).rearrange("p (t c) -> p t c", t=TB),
                m_d[i][o8], "mk", waits=waits_m if i == 0 else ())

    b_quarters(0, extra_waits=[("te", post_a9[0]), ("ve", post_a9[1])])
    b_small(0, ())
    b_masks(0, [("te", P.cnt("te"))])

    SETS = [bank[0:4], bank[4:8]]
    NH = NO8 * 2
    evac_ve = {}
    sel_ve = {}
    lora_te = {}
    tail_te = {}

    def app_lora(k, gi):
        o8, h = k // 2, k % 2
        g = h * 4 + gi
        st = SETS[k % 2]
        if gi == 0:
            w = [("ve", evac_ve[k]), ("wl", 32 * (o8 + 1))]
        else:
            w = [("ve", sel_ve[(k, gi - 1)])]
        P.step("tensor", waits=w)
        for l in range(4):
            P.step("tensor", lambda e, ll=l, gg=g, o=o8, s=st: e.matmul(
                s[ll][:],
                h_b[:, ll * NTOK + gg * 128: ll * NTOK + gg * 128 + 128],
                blt_b[:, o % 2, ll * OSLAB:(ll + 1) * OSLAB],
                start=True, stop=True),
                incs=[("te", 1)] if l == 3 else [])
        lora_te[(k, gi)] = P.cnt("te")

    def app_sel(k, gi):
        o8, h = k // 2, k % 2
        g = h * 4 + gi
        st = SETS[k % 2]
        P.step("vector", lambda e, s=st: e.tensor_copy(
            out=acc_v[:], in_=s[0][:]),
            waits=[("te", lora_te[(k, gi)]), ("mk", 48 * (o8 + 1)),
                   ("ve", P.cnt("ve"))], incs=[("ve", 1)])
        for i in range(3):
            P.step("vector", lambda e, ii=i, gg=g, s=st: e.copy_predicated(
                out=acc_v[:],
                mask=mkt(ii)[:, gg * OSLAB:(gg + 1) * OSLAB],
                data=s[1 + ii][:]),
                waits=[("ve", P.cnt("ve"))], incs=[("ve", 1)])
        sel_ve[(k, gi)] = P.cnt("ve")
        P.step("vector", lambda e, gg=g, o=o8: e.scalar_tensor_tensor(
            out=v_tile(gg, o), in0=acc_v[:], scalar=8.0, in1=v_tile(gg, o),
            op0=OP.mult, op1=OP.add,
            accum_out=vsum_p[:, gg * 8 + o: gg * 8 + o + 1]),
            waits=[("ve", P.cnt("ve"))], incs=[("ve", 1)])
        P.step("vector", lambda e, gg=g, o=o8: e.scalar_tensor_tensor(
            out=s2_scr[:], in0=v_tile(gg, o), scalar=1.0, in1=v_tile(gg, o),
            op0=OP.mult, op1=OP.mult,
            accum_out=vsq_p[:, gg * 8 + o: gg * 8 + o + 1]),
            waits=[("ve", P.cnt("ve"))], incs=[("ve", 1)])

    for k in range(NH):
        o8, h = k // 2, k % 2
        st = SETS[k % 2]
        P.step("tensor",
               waits=[("ve", ve_h if k < 2 else sel_ve[(k - 2, 3)])])
        for fc in range(FC):
            if h == 0 and fc % 8 == 0:
                P.step("tensor", waits=[(f"w{fc // 8}", 16 * (o8 + 1))])
            for gi in range(4):
                g = h * 4 + gi
                inc = [("tq", 1)] if (h == 1 and fc % 8 == 7 and gi == 3) \
                    else []
                P.step("tensor", lambda e, gg=g, f=fc, s=st, i=gi: e.matmul(
                    s[i][:], x1_blk(gg, f), w0_blk(f),
                    start=(f == 0), stop=False), incs=inc)
            if k > 0 and fc % 8 == 7:
                app_lora(k - 1, fc // 8)
                app_sel(k - 1, fc // 8)
        P.step("tensor", waits=[("wl", 32 * (o8 + 1))])
        for gi in range(4):
            g = h * 4 + gi
            P.step("tensor", lambda e, o=o8, s=st, i=gi: e.matmul(
                s[i][:], onesb_b[:], b0_b[:, o % 2],
                start=False, stop=False))
            for j in range(4):
                P.step("tensor", lambda e, gg=g, o=o8, jj=j, s=st, i=gi:
                       e.matmul(
                           s[i][:, jj * 128:(jj + 1) * 128],
                           x1_blk(gg, 4 * o + jj), ident_b[:],
                           start=False, stop=(jj == 3)),
                       incs=[("te", 1)] if j == 3 else [])
        tail_te[k] = P.cnt("te")
        for gi in range(4):
            g = h * 4 + gi
            P.step("vector", lambda e, gg=g, o=o8, s=st, i=gi: e.tensor_copy(
                out=v_tile(gg, o), in_=s[i][:]),
                waits=[("te", tail_te[k]), ("ve", P.cnt("ve"))],
                incs=[("ve", 1)])
        evac_ve[k] = P.cnt("ve")
        if h == 0 and o8 + 1 < NO8:
            b_quarters(o8 + 1)
            b_small(o8 + 1, [("te", P.cnt("te"))])
        if h == 0 and o8 >= 1:
            b_masks(o8, [("ve", P.cnt("ve"))])
    for gi in range(4):
        app_lora(NH - 1, gi)
        app_sel(NH - 1, gi)

    te_b = P.cnt("te")
    for g in range(TB):
        P.step("vector", lambda e, gg=g: e.reduce_sum(
            sm[:, 5:6], vsum_p[:, gg * 8:(gg + 1) * 8], axis=AX),
            waits=[("ve", P.cnt("ve")), ("te", te_b)], incs=[("ve", 1)])
        P.step("vector", lambda e, gg=g: e.reduce_sum(
            sm[:, 6:7], vsq_p[:, gg * 8:(gg + 1) * 8], axis=AX),
            incs=[("ve", 1)])
        P.step("vector", lambda e: e.tensor_scalar_mul(
            sm[:, 5:6], sm[:, 5:6], 1.0 / D),
            waits=[("ve", P.cnt("ve"))], incs=[("ve", 1)])
        P.step("vector", lambda e: e.tensor_scalar_mul(
            sm[:, 6:7], sm[:, 6:7], 1.0 / D), incs=[("ve", 1)])
        P.step("vector", lambda e: e.tensor_mul(
            sm[:, 7:8], sm[:, 5:6], sm[:, 5:6]),
            waits=[("ve", P.cnt("ve"))], incs=[("ve", 1)])
        P.step("vector", lambda e: e.tensor_sub(
            sm[:, 7:8], sm[:, 6:7], sm[:, 7:8]),
            waits=[("ve", P.cnt("ve"))], incs=[("ve", 1)])
        P.step("vector", lambda e: e.tensor_scalar_add(
            sm[:, 7:8], sm[:, 7:8], EPS),
            waits=[("ve", P.cnt("ve"))], incs=[("ve", 1)])
        P.step("vector", lambda e: e.reciprocal(
            sm[:, 8:9], sm[:, 7:8]),
            waits=[("ve", P.cnt("ve"))], incs=[("ve", 1)])
        P.step("scalar", lambda e: e.activation(
            out=sm[:, 9:10], in_=sm[:, 8:9], func=AF.Sqrt),
            waits=[("ve", P.cnt("ve"))], incs=[("se", 1)])
        P.step("vector", lambda e: e.tensor_mul(
            sm[:, 10:11], sm[:, 5:6], sm[:, 9:10]),
            waits=[("se", P.cnt("se"))], incs=[("ve", 1)])
        P.step("vector", lambda e: e.tensor_scalar_mul(
            sm[:, 10:11], sm[:, 10:11], -1.0),
            waits=[("ve", P.cnt("ve"))], incs=[("ve", 1)])
        b = g % 2
        ssem = "sA" if b == 0 else "sB"
        w = [("ve", P.cnt("ve"))]
        if g >= 2:
            w.append((ssem, P.cnt(ssem)))
        P.step("vector", lambda e, gg=g, bb=b: e.tensor_scalar(
            out=out_view(bb), in0=v_row(gg), scalar1=sm[:, 9:10],
            scalar2=sm[:, 10:11], op0=OP.mult, op1=OP.add),
            waits=w, incs=[("ve", 1)])
        dma("sync", out_d[g], out_view(b), ssem, waits=[("ve", P.cnt("ve"))])

    P.step("sync", waits=[("sA", P.cnt("sA")), ("sB", P.cnt("sB")),
                          ("wl", P.cnt("wl")), ("xf", P.cnt("xf"))])

    with nc.Block() as block:
        @block.sync
        def _(eng):
            P.emit("sync", eng, sems)

        @block.gpsimd
        def _(eng):
            P.emit("gpsimd", eng, sems)

        @block.tensor
        def _(eng):
            P.emit("tensor", eng, sems)

        @block.vector
        def _(eng):
            P.emit("vector", eng, sems)

        @block.scalar
        def _(eng):
            P.emit("scalar", eng, sems)

    ctx.close()
    return nc


_NC = {}


def kernel(L0_attn_x, L0_ffn_W0, L0_ffn_b0, L0_ffn_A, L0_ffn_B):
    x = np.asarray(L0_attn_x, dtype=np.float32)
    w0 = np.asarray(L0_ffn_W0, dtype=np.float32)
    b0 = np.asarray(L0_ffn_b0, dtype=np.float32)
    a_w = np.asarray(L0_ffn_A, dtype=np.float32)
    b_w = np.asarray(L0_ffn_B, dtype=np.float32)
    if "nc" not in _NC:
        _NC["nc"] = build_program()
    in_maps = [_core_inputs(c, x, w0, b0, a_w, b_w) for c in range(N_CORES)]
    res = run_bass_kernel_spmd(_NC["nc"], in_maps, core_ids=list(range(N_CORES)))
    out = np.concatenate([r["out"].reshape(-1, D) for r in res.results], axis=0)
    return out.reshape(1, SEQ, D).astype(np.float32)


# version 21
# speedup vs baseline: 1.3574x; 1.0467x over previous
import numpy as np
import ml_dtypes

import concourse.bass as bass
import concourse.mybir as mybir
from concourse.bass_utils import run_bass_kernel_spmd

F32 = mybir.dt.float32
BF16 = mybir.dt.bfloat16
U8 = mybir.dt.uint8
AX = mybir.AxisListType.X
AF = mybir.ActivationFunctionType
OP = mybir.AluOpType
BF = ml_dtypes.bfloat16

N_CORES = 8
SEQ = 8192
D = 4096
KHALF = 2049
KPAD = 2176
KC = KPAD // 128
FC = D // 128
OSLAB = 512
NO8 = D // OSLAB
EPS = 1e-5
NPASS = 2
NTOK = SEQ // N_CORES
PTOK = NTOK // NPASS
TBP = PTOK // 128
TB = NTOK // 128

_CONST = {}


def _constants():
    if _CONST:
        return _CONST
    k = np.arange(KPAD, dtype=np.float64)
    f = np.arange(D, dtype=np.float64)
    valid = (k < KHALF).astype(np.float64)
    ang = 2.0 * np.pi * np.outer(f, k) / D
    wc = np.cos(ang) * valid
    ws = -np.sin(ang) * valid
    ck = np.where((k == 0) | (k == D // 2), 1.0, 2.0) * valid / D
    tau = np.arange(D, dtype=np.float64)
    iang = 2.0 * np.pi * np.outer(k, tau) / D
    ctab = np.cos(iang) * ck[:, None]
    vc = np.cos(iang) * ck[:, None]
    vs = -np.sin(iang) * ck[:, None]

    def fm_chunks(tab):
        out = np.empty((KC, 128, FC, 128), dtype=BF)
        for j in range(KC):
            out[j] = tab[:, j * 128:(j + 1) * 128].reshape(FC, 128, 128) \
                .transpose(1, 0, 2)
        return out

    wtab = np.ascontiguousarray(
        np.concatenate([fm_chunks(wc), fm_chunks(ws)], axis=0))

    def tm_chunks(tab):
        out = np.empty((8, KC, 128, 512), dtype=BF)
        for sl in range(8):
            for j in range(KC):
                out[sl, j] = tab[j * 128:(j + 1) * 128,
                                 sl * 512:(sl + 1) * 512]
        return out

    TAU = 2176
    ctab_h = np.zeros((KPAD, TAU), dtype=np.float64)
    ctab_h[:, :2049] = ctab[:, :2049]
    ctab_c = np.empty((5, KC, 128, 512), dtype=BF)
    for sl in range(5):
        w = 512 if sl < 4 else 128
        for j in range(KC):
            ctab_c[sl, j, :, :w] = ctab_h[j * 128:(j + 1) * 128,
                                          sl * 512:sl * 512 + w]
            if w < 512:
                ctab_c[sl, j, :, w:] = 0
    ctab_c = np.ascontiguousarray(ctab_c.transpose(0, 2, 1, 3))
    cpf = np.where((f == 0) | (f == D // 2), 1.0, 2.0) * (f < 2049)
    wc2 = np.cos(ang) * valid * cpf[:, None]
    wtab2 = np.ascontiguousarray(fm_chunks(wc2)[:KC, :, :KC, :])
    irtab = np.ascontiguousarray(
        np.concatenate([tm_chunks(vc), tm_chunks(vs)], axis=1)
        .transpose(0, 2, 1, 3))

    ident = np.ascontiguousarray(np.eye(128), dtype=BF)
    onesb = np.ones((1, 128), dtype=BF)

    import jax
    with jax.default_device(jax.devices("cpu")[0]):
        u = np.asarray(jax.random.uniform(
            jax.random.key(42, impl="threefry2x32"), (1, SEQ, D)))[0]
    masks = [((u >= 0.25 * l) & (u < 0.25 * (l + 1))).astype(np.uint8)
             for l in (1, 2, 3)]

    _CONST.update(wtab=wtab, wtab2=wtab2, ctab=ctab_c, irtab=irtab,
                  ident=ident, onesb=onesb, masks=masks)
    return _CONST


def _core_inputs(core, x, w0, b0, a_w, b_w):
    c = _constants()
    t0 = core * NTOK
    xs = x[0, t0:t0 + NTOK]
    x_fm = np.ascontiguousarray(
        xs.reshape(NPASS, PTOK, D).transpose(0, 2, 1)
        .reshape(NPASS, FC, 128, PTOK).transpose(0, 2, 1, 3))
    x_tm = np.ascontiguousarray(
        xs.reshape(NPASS, TBP, 128, 8, 512).transpose(0, 3, 2, 1, 4))
    w0t = np.ascontiguousarray(
        w0.T.reshape(FC, 128, NO8, OSLAB).transpose(2, 1, 0, 3))
    acat0 = a_w.transpose(2, 0, 1).reshape(FC, 128, 4, 16)
    acat = np.zeros((FC, 128, 4, 32), dtype=np.float64)
    acat[:, :, :, :16] = acat0
    acat = np.ascontiguousarray(
        acat.reshape(FC, 128, 128).transpose(1, 0, 2))
    blt0 = b_w.transpose(0, 2, 1).reshape(4, 16, NO8, OSLAB)
    blt = np.zeros((4, 32, NO8, OSLAB), dtype=np.float64)
    blt[:, :16] = blt0
    blt = np.ascontiguousarray(
        blt.reshape(128, NO8, OSLAB).transpose(1, 0, 2))
    m_sl = [np.ascontiguousarray(
        m[t0:t0 + NTOK].reshape(TB, 128, NO8, OSLAB).transpose(2, 1, 0, 3))
        for m in c["masks"]]
    b0sl = np.ascontiguousarray(b0.reshape(NO8, 1, OSLAB))
    return dict(
        x_fm=x_fm.astype(np.float32), x_tm=x_tm.astype(np.float32),
        w0t=w0t.astype(np.float32), b0sl=b0sl.astype(np.float32),
        acat=acat.astype(np.float32), blt=blt.astype(np.float32),
        m1=m_sl[0], m2=m_sl[1], m3=m_sl[2],
        wtab=c["wtab"], wtab2=c["wtab2"], ctab=c["ctab"], irtab=c["irtab"],
        ident=c["ident"], identf=np.eye(128, dtype=np.float32),
        onesb=c["onesb"],
    )


class Plan:
    def __init__(self):
        self.steps = []
        self.counts = {}

    def cnt(self, s):
        return self.counts.get(s, 0)

    def step(self, engine, fn=None, waits=(), incs=()):
        self.steps.append((engine, list(waits), fn, list(incs)))
        for s, n in incs:
            self.counts[s] = self.counts.get(s, 0) + n

    def emit(self, engine_name, eng, sems):
        for e, waits, fn, incs in self.steps:
            if e != engine_name:
                continue
            for s, v in waits:
                if v > 0:
                    eng.wait_ge(sems[s], v)
            if fn is None:
                continue
            instr = fn(eng)
            for s, n in incs:
                instr.then_inc(sems[s], n)


def build_program(debug=False):
    nc = bass.Bass("TRN2", target_bir_lowering=False, debug=False)

    din = lambda name, shape, dt: nc.dram_tensor(
        name, list(shape), dt, kind="ExternalInput").ap()
    x_fm_d = din("x_fm", [NPASS, 128, FC, PTOK], F32)
    x_tm_d = din("x_tm", [NPASS, 8, 128, TBP, 512], F32)
    w0t_d = din("w0t", [NO8, 128, FC, OSLAB], F32)
    b0_d = din("b0sl", [NO8, 1, OSLAB], F32)
    acat_d = din("acat", [128, FC, 128], F32)
    blt_d = din("blt", [NO8, 128, OSLAB], F32)
    m_d = [din(f"m{l}", [NO8, 128, TB, OSLAB], U8) for l in (1, 2, 3)]
    wtab_d = din("wtab", [2 * KC, 128, FC, 128], BF16)
    ctab_d = din("ctab", [5, 128, KC, 512], BF16)
    wtab2_d = din("wtab2", [KC, 128, KC, 128], BF16)
    irtab_d = din("irtab", [8, 128, 2 * KC, 512], BF16)
    ident_d = din("ident", [128, 128], BF16)
    identf_d = din("identf", [128, 128], F32)
    onesb_d = din("onesb", [1, 128], BF16)
    out_d = nc.dram_tensor("out", [TB, 128, D], F32, kind="ExternalOutput").ap()
    dbg = {}
    if debug:
        for nm, shape in [("dbg_F", [128, 2 * KC * PTOK]),
                          ("dbg_corr", [128, TBP * 2176]),
                          ("dbg_sfm", [128, TBP * 2176]),
                          ("dbg_u", [128, 2 * KC * PTOK]),
                          ("dbg_v1", [128, TBP * D]),
                          ("dbg_x1", [128, TB * D]),
                          ("dbg_h", [16, 4 * NTOK]),
                          ("dbg_acat", [128, 2048]),
                          ("dbg_v", [128, TB * D])]:
            dbg[nm] = nc.dram_tensor(nm, shape, BF16, kind="ExternalOutput").ap()

    def dump(nm, src_ap, eng="sync"):
        if debug:
            P.step(eng, lambda e, o=dbg[nm], i=src_ap: e.dma_start(out=o, in_=i),
                   waits=[("ve", P.cnt("ve")), ("te", P.cnt("te")),
                          ("se", P.cnt("se"))], incs=[("sA", 16)])
            for en in ("vector", "tensor", "scalar", "gpsimd"):
                P.step(en, waits=[("sA", P.cnt("sA"))])

    from contextlib import ExitStack
    ctx = ExitStack()
    sb = lambda n, s, dt: ctx.enter_context(nc.sbuf_tensor(n, list(s), dt))
    psT = lambda n, s, dt=F32: ctx.enter_context(nc.psum_tensor(n, list(s), dt))

    ALPHA = sb("ALPHA", [128, 34816], BF16)
    BETA = sb("BETA", [128, 32768], BF16)
    GAMMA = sb("GAMMA", [128, 16384], BF16)
    PTS = sb("PTS", [128, 8704], BF16)
    h_b = sb("h_b", [128, NTOK], BF16)
    h3_b = sb("h3_b", [16, NTOK], BF16)
    blt3_b = sb("blt3_b", [16, 2, OSLAB], BF16)
    e_scr = sb("e_scr", [128, 512], BF16)
    s2_scr = sb("s2_scr", [128, 512], BF16)
    blt_b = sb("blt_b", [128, 2, OSLAB], BF16)
    b0_b = sb("b0_b", [1, 2, OSLAB], BF16)
    ctab_b = sb("ctab_b", [128, 4, 512], BF16)
    ident_b = sb("ident_b", [128, 128], BF16)
    identf_b = sb("identf_b", [128, 128], F32)
    onesb_b = sb("onesb_b", [1, 128], BF16)
    m8 = sb("m8", [128, 8], BF16)
    mf = sb("mf", [128, 8], F32)
    sm = sb("sm", [128, 16], F32)
    vsum_p = sb("vsum_p", [128, 64], F32)
    vsq_p = sb("vsq_p", [128, 64], F32)

    def A(o, n, dt=BF16):
        v = ALPHA[:, o:o + n]
        return v.bitcast(dt) if dt != BF16 else v

    def G(o, n, dt=BF16):
        v = GAMMA[:, o:o + n]
        return v.bitcast(dt) if dt != BF16 else v

    def PT(o, n, dt=BF16):
        v = PTS[:, o:o + n]
        return v.bitcast(dt) if dt != BF16 else v

    F_sz = 2 * KC * PTOK
    F_at = lambda kc: A(kc * PTOK, PTOK)
    CORR0 = F_sz
    TAU = 2176
    TSW = [512, 512, 512, 512, 128]
    corr_tb = lambda t: A(CORR0 + t * TAU, TAU)
    corr_tile = lambda t, sl, w: A(CORR0 + t * TAU + sl * 512, w)
    corr_pad = lambda t: A(CORR0 + t * TAU + 2049, 127)
    sfm_blk = lambda t, fcc: A(CORR0 + t * TAU + fcc * 128, 128)
    v1_tile = lambda t, isl: A(CORR0 + t * D + isl * 512, 512)
    v1_row = lambda t: A(CORR0 + t * D, D)
    xfm_blk = lambda fcc: A(CORR0 + fcc * PTOK, PTOK)
    v_tile = lambda g, o8: A((g * 8 + o8) * 512, 512)
    v_row = lambda g: A(g * D, D)
    acat_blk = lambda fcc, l: PT(fcc * 64 + l * 16, 16)

    wtab_blk = lambda buf, fcc: G(buf * 4096 + fcc * 128, 128)
    w2_blk = lambda buf, fcc: G(buf * 2176 + fcc * 128, 128)
    xtm_tile = lambda buf, t: G(buf * 2048 + t * 512, 512)
    irt_g = lambda buf, j: G(4096 + buf * 4096 + j * 512, 512)
    ctab_g = lambda buf, j: G(buf * 4096 + j * 512, 512)
    w0_blk = lambda fcc: G(fcc * 512, 512)
    out_view = lambda buf: G(buf * 8192, 8192, F32)

    P_blk = lambda kc, t: PT(kc * 512 + t * 128, 128)
    tscr = PT(0, 8192, F32)
    mscr = lambda i: PT(i * 1024, 1024, F32)
    x1t = PT(0, 8192, F32)
    acc_v = PT(0, 1024, F32)
    mkt = lambda i: PTS[:, 2048 + i * 2048: 2048 + (i + 1) * 2048].bitcast(U8)

    x1_blk = lambda g, fcc: BETA[:, g * D + fcc * 128: g * D + fcc * 128 + 128]

    bank = [psT(f"bk{i}", [128, 512]) for i in range(8)]
    tr_ps = lambda b: bank[4 + b][:, 0:128]

    def warm():
        P.step("tensor", lambda e: e.matmul(
            bank[6][:, 0:128], ident_b[:], ident_b[:], start=True, stop=True),
            waits=[("ve", P.cnt("ve"))])

    sems = {s: ctx.enter_context(nc.semaphore(s))
            for s in ("te", "ve", "se", "cs", "xf", "tA", "tB",
                      "cA", "cB", "cC", "cD", "iA", "iB", "iC", "iD",
                      "xA", "xB", "wl", "w0", "w1", "w2", "w3",
                      "mk", "ac", "sA", "sB", "tq")}
    P = Plan()

    def dma(engine, out_ap, in_ap, sname, waits=()):
        P.step(engine, lambda e, o=out_ap, i=in_ap: e.dma_start(out=o, in_=i),
               waits=waits, incs=[(sname, 16)])

    dma("sync", ident_b[:], ident_d[:], "cs")
    dma("sync", identf_b[:], identf_d[:], "cs")
    dma("sync", onesb_b[:], onesb_d[:], "cs")

    ln1_affine_ve = []
    for ps in range(NPASS):
        for q in range(4):
            w = FC // 4
            dma("gpsimd",
                A(CORR0 + q * w * PTOK, w * PTOK)
                .rearrange("p (c t) -> p c t", c=w),
                x_fm_d[ps, :, q * w:(q + 1) * w, :], "xf",
                waits=[("ve", ln1_affine_ve[q] if ps > 0 else P.cnt("ve"))])

        dma("sync", GAMMA[:, 0:4096].rearrange("p (c k) -> p c k", c=FC),
            wtab_d[0], "tA", waits=[("te", P.cnt("te"))])
        dma("sync", GAMMA[:, 4096:8192].rearrange("p (c k) -> p c k", c=FC),
            wtab_d[1], "tB")
        P.step("tensor", waits=[("xf", P.cnt("xf")), ("cs", P.cnt("cs")),
                                ("ve", P.cnt("ve"))])
        a2_evac = {}
        a3_ve = {}
        for kc in range(2 * KC):
            buf = kc % 2
            tsem = "tA" if buf == 0 else "tB"
            w = [(tsem, P.cnt(tsem))]
            if kc >= 2:
                w.append(("ve", a2_evac[kc - 2]))
            P.step("tensor", waits=w)
            for fc in range(FC):
                P.step("tensor", lambda e, b=buf, f=fc, k=kc: e.matmul(
                    bank[b][:], wtab_blk(b, f), xfm_blk(f),
                    start=(f == 0), stop=(f == FC - 1)),
                    incs=[("te", 1)] if fc == FC - 1 else [])
            if kc + 2 < 2 * KC:
                dma("sync",
                    GAMMA[:, buf * 4096:buf * 4096 + 4096]
                    .rearrange("p (c k) -> p c k", c=FC),
                    wtab_d[kc + 2], tsem, waits=[("te", P.cnt("te"))])
            P.step("vector", lambda e, b=buf, k=kc: e.tensor_copy(
                out=F_at(k), in_=bank[b][:]),
                waits=[("te", P.cnt("te"))], incs=[("ve", 1)])
            a2_evac[kc] = P.cnt("ve")
            if kc >= KC:
                k3 = kc - KC
                P.step("scalar", lambda e, k=k3: e.activation(
                    out=e_scr[:], in_=F_at(k), func=AF.Square),
                    waits=[("ve", a2_evac[kc])], incs=[("se", 1)])
                P.step("scalar", lambda e, k=k3: e.activation(
                    out=s2_scr[:], in_=F_at(KC + k), func=AF.Square),
                    incs=[("se", 1)])
                P.step("vector", lambda e, k=k3: e.tensor_add(
                    PTS[:, k * 512:(k + 1) * 512], e_scr[:], s2_scr[:]),
                    waits=[("se", P.cnt("se"))], incs=[("ve", 1)])
                a3_ve[k3] = P.cnt("ve")
        ve_F = P.cnt("ve")
        ve_P = P.cnt("ve")
        if ps == 0:
            dump("dbg_F", ALPHA[:, 0:F_sz])

        CGRP = [(0, 6), (6, 6), (12, 5)]
        def a4_tab(ts_, gi_, buf_, w=()):
            k0, ng = CGRP[gi_]
            dma("sync", GAMMA[:, buf_ * 4096: buf_ * 4096 + ng * 512]
                .rearrange("p (k c) -> p k c", k=ng),
                ctab_d[ts_][:, k0:k0 + ng, :], "cA" if buf_ == 0 else "cB", w)
        a4_tab(0, 0, 0, [("te", P.cnt("te"))])
        a4_tab(0, 1, 1)
        pend_evac = []
        gidx = 0
        for ts in range(5):
            bb = (ts % 2) * 4
            for gi in range(3):
                k0, ng = CGRP[gi]
                buf = gidx % 2
                csem = "cA" if buf == 0 else "cB"
                vtgt = a3_ve[k0 + ng - 1]
                if ts == 0:
                    vtgt = max(vtgt, a2_evac[2 * KC - 1])
                w = [(csem, P.cnt(csem)), ("ve", vtgt)]
                if gi == 0 and ts >= 2:
                    w.append(("ve", pend_evac[ts - 2]))
                P.step("tensor", waits=w)
                for kk2 in range(ng):
                    kc = k0 + kk2
                    for t in range(TBP):
                        P.step("tensor", lambda e, tt=t, k=kc, b=buf, j=kk2,
                               bo=bb, s=ts: e.matmul(
                            bank[bo + tt][:, 0:TSW[s]], P_blk(k, tt),
                            ctab_g(b, j)[:, 0:TSW[s]],
                            start=(k == 0), stop=(k == KC - 1)),
                            incs=[("te", 1)] if t == TBP - 1 else [])
                ngi = gidx + 2
                if ngi < 15:
                    nts, ngj = divmod(ngi, 3)
                    a4_tab(nts, ngj, buf, [("te", P.cnt("te"))])
                gidx += 1
            te_now = P.cnt("te")
            for t in range(TBP):
                P.step("vector", lambda e, tt=t, s=ts, bo=bb: e.tensor_copy(
                    out=corr_tile(tt, s, TSW[s]), in_=bank[bo + tt][:, 0:TSW[s]]),
                    waits=[("te", te_now)], incs=[("ve", 1)])
            pend_evac.append(P.cnt("ve"))
        for t in range(TBP):
            P.step("vector", lambda e, tt=t: e.memset(corr_pad(tt), -1e9),
                   waits=[("ve", P.cnt("ve"))], incs=[("ve", 1)])

        ve_corr = P.cnt("ve")
        if ps == 0:
            dump("dbg_corr", A(CORR0, TBP * TAU))
        for t in range(TBP):
            P.step("vector", lambda e, tt=t: e.max(
                out=m8[:], in_=corr_tb(tt)),
                waits=[("ve", ve_corr)], incs=[("ve", 1)])
            warm()
            P.step("vector", lambda e: e.tensor_copy(out=mf[:], in_=m8[:]),
                   waits=[("ve", P.cnt("ve"))], incs=[("ve", 1)])
            P.step("vector", lambda e: e.tensor_scalar_mul(
                sm[:, 0:1], mf[:, 0:1], -1.0),
                waits=[("ve", P.cnt("ve"))], incs=[("ve", 1)])
            P.step("scalar", lambda e: e.activation(
                out=sm[:, 8:16], in_=mf[:], func=AF.Exp,
                bias=sm[:, 0:1], scale=1.0),
                waits=[("ve", P.cnt("ve"))], incs=[("se", 1)])
            P.step("vector", lambda e: e.reduce_sum(
                sm[:, 1:2], sm[:, 8:16], axis=AX),
                waits=[("se", P.cnt("se"))], incs=[("ve", 1)])
            P.step("scalar", lambda e: e.activation(
                out=sm[:, 2:3], in_=sm[:, 1:2], func=AF.Ln),
                waits=[("ve", P.cnt("ve"))], incs=[("se", 1)])
            P.step("vector", lambda e: e.tensor_sub(
                sm[:, 3:4], sm[:, 0:1], sm[:, 2:3]),
                waits=[("se", P.cnt("se"))], incs=[("ve", 1)])
            P.step("vector", lambda e: e.tensor_copy(
                out=sm[:, 4:5], in_=m8[:, 7:8]),
                waits=[("ve", P.cnt("ve"))], incs=[("ve", 1)])
            warm()
            for cchunk in range(5):
                cs = corr_tile(t, cchunk, TSW[cchunk])
                ww = TSW[cchunk]
                P.step("scalar", lambda e, c=cs, w2=ww: e.activation(
                    out=e_scr[:, 0:w2], in_=c, func=AF.Exp,
                    bias=sm[:, 3:4], scale=1.0),
                    waits=[("ve", P.cnt("ve"))], incs=[("se", 1)])
                P.step("vector", lambda e, c=cs, cc=cchunk, w2=ww:
                       e.scalar_tensor_tensor(
                    out=tscr[:, cc * 512:cc * 512 + w2], in0=c,
                    scalar=sm[:, 4:5], in1=e_scr[:, 0:w2],
                    op0=OP.is_ge, op1=OP.mult),
                    waits=[("se", P.cnt("se"))], incs=[("ve", 1)])
                warm()
            ve_s = P.cnt("ve")
            for fc in range(KC):
                b = fc % 2
                w = [("ve", ve_s)] if fc < 2 else [("ve", P.cnt("ve") - 1)]
                P.step("tensor", lambda e, f=fc, bb=b: e.transpose(
                    tr_ps(bb), tscr[:, f * 128:(f + 1) * 128], identf_b[:]),
                    waits=w, incs=[("te", 1)])
                P.step("vector", lambda e, f=fc, bb=b, tt=t: e.tensor_copy(
                    out=sfm_blk(tt, f), in_=tr_ps(bb)),
                    waits=[("te", P.cnt("te"))], incs=[("ve", 1)])
        ve_sfm = P.cnt("ve")
        if ps == 0:
            dump("dbg_sfm", A(CORR0, TBP * TAU))

        dma("sync", GAMMA[:, 0:2176].rearrange("p (c k) -> p c k", c=KC),
            wtab2_d[0], "tA", waits=[("te", P.cnt("te")), ("ve", ve_sfm)])
        dma("sync", GAMMA[:, 2176:4352].rearrange("p (c k) -> p c k", c=KC),
            wtab2_d[1], "tB")

        def sfm_mov(fcc):
            return A(CORR0, TBP * TAU).rearrange("p (t q) -> p t q", q=TAU)[
                :, :, fcc * 128:fcc * 128 + 128]

        P.step("tensor", waits=[("ve", ve_sfm)])
        for kc in range(KC):
            buf = kc % 2
            tsem = "tA" if buf == 0 else "tB"
            P.step("tensor", waits=[(tsem, P.cnt(tsem)),
                                    ("ve", max(0, P.cnt("ve") - 1))])
            for fc in range(KC):
                P.step("tensor", lambda e, b=buf, f=fc, k=kc: e.matmul(
                    bank[k % 2][:], w2_blk(b, f), sfm_mov(f),
                    start=(f == 0), stop=(f == KC - 1)),
                    incs=[("te", 1)] if fc == KC - 1 else [])
            if kc + 2 < KC:
                dma("sync",
                    GAMMA[:, buf * 2176:buf * 2176 + 2176]
                    .rearrange("p (c k) -> p c k", c=KC),
                    wtab2_d[kc + 2], tsem, waits=[("te", P.cnt("te"))])
            te_now = P.cnt("te")
            fre, fim = F_at(kc), F_at(KC + kc)
            P.step("vector", lambda e, a=fre, k=kc: e.tensor_mul(
                a, a, bank[k % 2][:]),
                waits=[("te", te_now), ("ve", P.cnt("ve"))], incs=[("ve", 1)])
            P.step("vector", lambda e, a=fim, k=kc: e.tensor_mul(
                a, a, bank[k % 2][:]), incs=[("ve", 1)])
            ve_u = P.cnt("ve")
        if ps == 0:
            dump("dbg_u", ALPHA[:, 0:F_sz])

        IGRP = [(0, 7), (7, 7), (14, 7), (21, 7), (28, 6)]
        def a9_tab(isl_, gi_, buf_, w=()):
            k0, ng = IGRP[gi_]
            dma("sync", GAMMA[:, 4096 + buf_ * 4096: 4096 + buf_ * 4096 + ng * 512]
                .rearrange("p (k c) -> p k c", k=ng),
                irtab_d[isl_][:, k0:k0 + ng, :], "iA" if buf_ == 0 else "iB", w)
        a9_tab(0, 0, 0, [("te", P.cnt("te")), ("ve", ve_u)])
        a9_tab(0, 1, 1)
        dma("gpsimd", G(0, 2048).rearrange("p (t c) -> p t c", t=TBP),
            x_tm_d[ps, 0], "xA", waits=[("te", P.cnt("te")), ("ve", ve_u)])
        P.step("tensor", waits=[("ve", ve_u)])
        pend_v1 = []
        gidx9 = 0
        for isl in range(8):
            bb9 = (isl % 2) * 4
            for gi in range(5):
                k0, ng = IGRP[gi]
                buf = gidx9 % 2
                isem = "iA" if buf == 0 else "iB"
                w = [(isem, P.cnt(isem))]
                if gi == 0 and isl >= 2:
                    w.append(("ve", pend_v1[isl - 2]))
                P.step("tensor", waits=w)
                for kk2 in range(ng):
                    kc = k0 + kk2
                    for t in range(TBP):
                        P.step("tensor", lambda e, tt=t, k=kc, b=buf, j=kk2,
                               bo=bb9: e.matmul(
                            bank[bo + tt][:],
                            A(k * PTOK + tt * 128, 128), irt_g(b, j),
                            start=(k == 0), stop=(k == 2 * KC - 1)),
                            incs=[("te", 1)] if t == TBP - 1 else [])
                ngi = gidx9 + 2
                if ngi < 40:
                    nisl, ngj = divmod(ngi, 5)
                    a9_tab(nisl, ngj, buf, [("te", P.cnt("te"))])
                gidx9 += 1
            if isl + 1 < 8:
                xsem2 = "xA" if (isl + 1) % 2 == 0 else "xB"
                dma("gpsimd",
                    G(((isl + 1) % 2) * 2048, 2048)
                    .rearrange("p (t c) -> p t c", t=TBP),
                    x_tm_d[ps, isl + 1], xsem2,
                    waits=[("ve", P.cnt("ve"))])
            te_now = P.cnt("te")
            xb = isl % 2
            xsem = "xA" if xb == 0 else "xB"
            for t in range(TBP):
                P.step("vector", lambda e, tt=t, b=xb, i=isl, bo=bb9: e.tensor_add(
                    v1_tile(tt, i), bank[bo + tt][:], xtm_tile(b, tt)),
                    waits=[("te", te_now), (xsem, P.cnt(xsem))],
                    incs=[("ve", 1)])
                P.step("vector", lambda e, tt=t, i=isl: e.reduce_sum(
                    vsum_p[:, tt * 8 + i: tt * 8 + i + 1], v1_tile(tt, i),
                    axis=AX),
                    waits=[("ve", P.cnt("ve"))], incs=[("ve", 1)])
                P.step("scalar", lambda e, tt=t, i=isl: e.activation(
                    out=s2_scr[:], in_=v1_tile(tt, i), func=AF.Square),
                    waits=[("ve", P.cnt("ve"))], incs=[("se", 1)])
                P.step("vector", lambda e, tt=t, i=isl: e.reduce_sum(
                    vsq_p[:, tt * 8 + i: tt * 8 + i + 1], s2_scr[:], axis=AX),
                    waits=[("se", P.cnt("se"))], incs=[("ve", 1)])
            pend_v1.append(P.cnt("ve"))

        if ps == NPASS - 1:
            post_a9 = (P.cnt("te"), P.cnt("ve"))
        if ps == 0:
            dump("dbg_v1", A(CORR0, TBP * D))
        for t in range(TBP):
            g = ps * TBP + t
            P.step("vector", lambda e, tt=t: e.reduce_sum(
                sm[:, 5:6], vsum_p[:, tt * 8:(tt + 1) * 8], axis=AX),
                waits=[("ve", P.cnt("ve"))], incs=[("ve", 1)])
            P.step("vector", lambda e, tt=t: e.reduce_sum(
                sm[:, 6:7], vsq_p[:, tt * 8:(tt + 1) * 8], axis=AX),
                incs=[("ve", 1)])
            P.step("vector", lambda e: e.tensor_scalar_mul(
                sm[:, 5:6], sm[:, 5:6], 1.0 / D),
                waits=[("ve", P.cnt("ve"))], incs=[("ve", 1)])
            P.step("vector", lambda e: e.tensor_scalar_mul(
                sm[:, 6:7], sm[:, 6:7], 1.0 / D), incs=[("ve", 1)])
            P.step("vector", lambda e: e.tensor_mul(
                sm[:, 7:8], sm[:, 5:6], sm[:, 5:6]),
                waits=[("ve", P.cnt("ve"))], incs=[("ve", 1)])
            P.step("vector", lambda e: e.tensor_sub(
                sm[:, 7:8], sm[:, 6:7], sm[:, 7:8]),
                waits=[("ve", P.cnt("ve"))], incs=[("ve", 1)])
            P.step("vector", lambda e: e.tensor_scalar_add(
                sm[:, 7:8], sm[:, 7:8], EPS),
                waits=[("ve", P.cnt("ve"))], incs=[("ve", 1)])
            P.step("vector", lambda e: e.reciprocal(
                sm[:, 8:9], sm[:, 7:8]),
                waits=[("ve", P.cnt("ve"))], incs=[("ve", 1)])
            warm()
            P.step("scalar", lambda e: e.activation(
                out=sm[:, 9:10], in_=sm[:, 8:9], func=AF.Sqrt),
                waits=[("ve", P.cnt("ve"))], incs=[("se", 1)])
            P.step("vector", lambda e: e.tensor_mul(
                sm[:, 10:11], sm[:, 5:6], sm[:, 9:10]),
                waits=[("se", P.cnt("se"))], incs=[("ve", 1)])
            P.step("vector", lambda e: e.tensor_scalar_mul(
                sm[:, 10:11], sm[:, 10:11], -1.0),
                waits=[("ve", P.cnt("ve"))], incs=[("ve", 1)])
            P.step("vector", lambda e, tt=t: e.tensor_scalar(
                out=x1t[:], in0=v1_row(tt), scalar1=sm[:, 9:10],
                scalar2=sm[:, 10:11], op0=OP.mult, op1=OP.add),
                waits=[("ve", P.cnt("ve"))], incs=[("ve", 1)])
            if ps == 0:
                ln1_affine_ve.append(P.cnt("ve"))
            warm()
            ve_x1 = P.cnt("ve")
            for fc in range(FC):
                b = fc % 2
                w = [("ve", ve_x1)] if fc < 2 else [("ve", P.cnt("ve") - 1)]
                P.step("tensor", lambda e, f=fc, bb=b: e.transpose(
                    tr_ps(bb), x1t[:, f * 128:(f + 1) * 128], identf_b[:]),
                    waits=w, incs=[("te", 1)])
                P.step("vector", lambda e, f=fc, bb=b, gg=g: e.tensor_copy(
                    out=x1_blk(gg, f), in_=tr_ps(bb)),
                    waits=[("te", P.cnt("te"))], incs=[("ve", 1)])
    ve_x1fm = P.cnt("ve")
    if debug:
        dump("dbg_x1", BETA[:, :])

    dma("gpsimd", PTS[:, 0:4096].rearrange("p (c r) -> p c r", c=FC),
        acat_d[:], "ac", waits=[("ve", ve_x1fm), ("te", P.cnt("te"))])

    def x1_mov(th, fcc):
        return BETA[:, :].rearrange("p (g q) -> p g q", q=D)[
            :, th * 4:(th + 1) * 4, fcc * 128:fcc * 128 + 128]

    if debug:
        P.step("sync", waits=[("ac", P.cnt("ac"))])
        dump("dbg_acat", PTS[:, 0:2048])
    P.step("tensor", waits=[("ve", ve_x1fm), ("ac", P.cnt("ac"))])
    for th in range(2):
        for fc in range(FC):
            P.step("tensor", lambda e, t=th, f=fc: e.matmul(
                bank[t][0:96, :], PT(f * 128, 96), x1_mov(t, f),
                start=(f == 0), stop=(f == FC - 1)),
                incs=[("te", 1)] if fc == FC - 1 else [])
        for fc in range(FC):
            P.step("tensor", lambda e, t=th, f=fc: e.matmul(
                bank[2 + t][0:16, :], PT(f * 128 + 96, 16), x1_mov(t, f),
                start=(f == 0), stop=(f == FC - 1)),
                incs=[("te", 1)] if fc == FC - 1 else [])
    te_h = P.cnt("te")
    for th in range(2):
        P.step("vector", lambda e, t=th: e.tensor_copy(
            out=h_b[0:96, t * 512: t * 512 + 512], in_=bank[t][0:96, :]),
            waits=[("te", te_h)], incs=[("ve", 1)])
        P.step("vector", lambda e, t=th: e.tensor_copy(
            out=h3_b[:, t * 512: t * 512 + 512], in_=bank[2 + t][0:16, :]),
            incs=[("ve", 1)])
    ve_h = P.cnt("ve")
    if debug:
        dump("dbg_h", h_b[:, :])


    def b_quarters(o8, extra_waits=()):
        for q in range(4):
            w = list(extra_waits) if q == 0 else []
            if o8 >= 1:
                w.append(("tq", 4 * (o8 - 1) + q + 1))
            dma("gpsimd",
                GAMMA[:, q * 4096:(q + 1) * 4096]
                .rearrange("p (c k) -> p c k", c=8),
                w0t_d[o8, :, q * 8:(q + 1) * 8, :], f"w{q}", waits=w)

    def b_small(o8, waits_b):
        dma("gpsimd", b0_b[:, o8 % 2], b0_d[o8], "wl", waits=waits_b)
        dma("gpsimd", blt_b[:, o8 % 2], blt_d[o8], "wl")
        dma("gpsimd", blt3_b[:, o8 % 2], blt_d[o8, 96:112, :], "wl")

    def b_masks(o8, waits_m):
        for i in range(3):
            dma("sync", mkt(i).rearrange("p (t c) -> p t c", t=TB),
                m_d[i][o8], "mk", waits=waits_m if i == 0 else ())

    b_quarters(0, extra_waits=[("te", post_a9[0]), ("ve", post_a9[1])])
    b_small(0, ())
    b_masks(0, [("te", P.cnt("te"))])

    SETS = [bank[0:4], bank[4:8]]
    NH = NO8 * 2
    evac_ve = {}
    sel_ve = {}
    lora_te = {}
    tail_te = {}

    def app_lora(k, gi):
        o8, h = k // 2, k % 2
        g = h * 4 + gi
        st = SETS[k % 2]
        if gi == 0:
            w = [("ve", evac_ve[k]), ("wl", 48 * (o8 + 1))]
        else:
            w = [("ve", sel_ve[(k, gi - 1)])]
        P.step("tensor", waits=w)
        for l in range(4):
            if l < 3:
                lh = h_b[32 * l:32 * l + 16, g * 128: g * 128 + 128]
                lb = blt_b[32 * l:32 * l + 16, o8 % 2, :]
            else:
                lh = h3_b[:, g * 128: g * 128 + 128]
                lb = blt3_b[:, o8 % 2, :]
            P.step("tensor", lambda e, ll=l, s=st, a=lh, b=lb: e.matmul(
                s[ll][:], a, b, start=True, stop=True),
                incs=[("te", 1)] if l == 3 else [])
        lora_te[(k, gi)] = P.cnt("te")

    def app_sel(k, gi):
        o8, h = k // 2, k % 2
        g = h * 4 + gi
        st = SETS[k % 2]
        P.step("vector", lambda e, s=st: e.tensor_copy(
            out=acc_v[:], in_=s[0][:]),
            waits=[("te", lora_te[(k, gi)]), ("mk", 48 * (o8 + 1)),
                   ("ve", P.cnt("ve"))], incs=[("ve", 1)])
        for i in range(3):
            P.step("vector", lambda e, ii=i, gg=g, s=st: e.copy_predicated(
                out=acc_v[:],
                mask=mkt(ii)[:, gg * OSLAB:(gg + 1) * OSLAB],
                data=s[1 + ii][:]),
                waits=[("ve", P.cnt("ve"))], incs=[("ve", 1)])
        sel_ve[(k, gi)] = P.cnt("ve")
        P.step("vector", lambda e, gg=g, o=o8: e.scalar_tensor_tensor(
            out=v_tile(gg, o), in0=acc_v[:], scalar=8.0, in1=v_tile(gg, o),
            op0=OP.mult, op1=OP.add,
            accum_out=vsum_p[:, gg * 8 + o: gg * 8 + o + 1]),
            waits=[("ve", P.cnt("ve"))], incs=[("ve", 1)])
        P.step("vector", lambda e, gg=g, o=o8: e.scalar_tensor_tensor(
            out=s2_scr[:], in0=v_tile(gg, o), scalar=1.0, in1=v_tile(gg, o),
            op0=OP.mult, op1=OP.mult,
            accum_out=vsq_p[:, gg * 8 + o: gg * 8 + o + 1]),
            waits=[("ve", P.cnt("ve"))], incs=[("ve", 1)])

    for k in range(NH):
        o8, h = k // 2, k % 2
        st = SETS[k % 2]
        P.step("tensor",
               waits=[("ve", ve_h if k < 2 else sel_ve[(k - 2, 3)])])
        for fc in range(FC):
            if h == 0 and fc % 8 == 0:
                P.step("tensor", waits=[(f"w{fc // 8}", 16 * (o8 + 1))])
            for gi in range(4):
                g = h * 4 + gi
                inc = [("tq", 1)] if (h == 1 and fc % 8 == 7 and gi == 3) \
                    else []
                P.step("tensor", lambda e, gg=g, f=fc, s=st, i=gi: e.matmul(
                    s[i][:], x1_blk(gg, f), w0_blk(f),
                    start=(f == 0), stop=False), incs=inc)
            if k > 0 and fc % 8 == 7:
                app_lora(k - 1, fc // 8)
                app_sel(k - 1, fc // 8)
        P.step("tensor", waits=[("wl", 48 * (o8 + 1))])
        for gi in range(4):
            g = h * 4 + gi
            P.step("tensor", lambda e, o=o8, s=st, i=gi: e.matmul(
                s[i][:], onesb_b[:], b0_b[:, o % 2],
                start=False, stop=False))
            for j in range(4):
                P.step("tensor", lambda e, gg=g, o=o8, jj=j, s=st, i=gi:
                       e.matmul(
                           s[i][:, jj * 128:(jj + 1) * 128],
                           x1_blk(gg, 4 * o + jj), ident_b[:],
                           start=False, stop=(jj == 3)),
                       incs=[("te", 1)] if j == 3 else [])
        tail_te[k] = P.cnt("te")
        for gi in range(4):
            g = h * 4 + gi
            P.step("vector", lambda e, gg=g, o=o8, s=st, i=gi: e.tensor_copy(
                out=v_tile(gg, o), in_=s[i][:]),
                waits=[("te", tail_te[k]), ("ve", P.cnt("ve"))],
                incs=[("ve", 1)])
        evac_ve[k] = P.cnt("ve")
        if h == 0 and o8 + 1 < NO8:
            b_quarters(o8 + 1)
            b_small(o8 + 1, [("te", P.cnt("te"))])
        if h == 0 and o8 >= 1:
            b_masks(o8, [("ve", P.cnt("ve"))])
    for gi in range(4):
        app_lora(NH - 1, gi)
        app_sel(NH - 1, gi)

    te_b = P.cnt("te")
    for g in range(TB):
        P.step("vector", lambda e, gg=g: e.reduce_sum(
            sm[:, 5:6], vsum_p[:, gg * 8:(gg + 1) * 8], axis=AX),
            waits=[("ve", P.cnt("ve")), ("te", te_b)], incs=[("ve", 1)])
        P.step("vector", lambda e, gg=g: e.reduce_sum(
            sm[:, 6:7], vsq_p[:, gg * 8:(gg + 1) * 8], axis=AX),
            incs=[("ve", 1)])
        P.step("vector", lambda e: e.tensor_scalar_mul(
            sm[:, 5:6], sm[:, 5:6], 1.0 / D),
            waits=[("ve", P.cnt("ve"))], incs=[("ve", 1)])
        P.step("vector", lambda e: e.tensor_scalar_mul(
            sm[:, 6:7], sm[:, 6:7], 1.0 / D), incs=[("ve", 1)])
        P.step("vector", lambda e: e.tensor_mul(
            sm[:, 7:8], sm[:, 5:6], sm[:, 5:6]),
            waits=[("ve", P.cnt("ve"))], incs=[("ve", 1)])
        P.step("vector", lambda e: e.tensor_sub(
            sm[:, 7:8], sm[:, 6:7], sm[:, 7:8]),
            waits=[("ve", P.cnt("ve"))], incs=[("ve", 1)])
        P.step("vector", lambda e: e.tensor_scalar_add(
            sm[:, 7:8], sm[:, 7:8], EPS),
            waits=[("ve", P.cnt("ve"))], incs=[("ve", 1)])
        P.step("vector", lambda e: e.reciprocal(
            sm[:, 8:9], sm[:, 7:8]),
            waits=[("ve", P.cnt("ve"))], incs=[("ve", 1)])
        P.step("scalar", lambda e: e.activation(
            out=sm[:, 9:10], in_=sm[:, 8:9], func=AF.Sqrt),
            waits=[("ve", P.cnt("ve"))], incs=[("se", 1)])
        P.step("vector", lambda e: e.tensor_mul(
            sm[:, 10:11], sm[:, 5:6], sm[:, 9:10]),
            waits=[("se", P.cnt("se"))], incs=[("ve", 1)])
        P.step("vector", lambda e: e.tensor_scalar_mul(
            sm[:, 10:11], sm[:, 10:11], -1.0),
            waits=[("ve", P.cnt("ve"))], incs=[("ve", 1)])
        b = g % 2
        ssem = "sA" if b == 0 else "sB"
        w = [("ve", P.cnt("ve"))]
        if g >= 2:
            w.append((ssem, P.cnt(ssem)))
        P.step("vector", lambda e, gg=g, bb=b: e.tensor_scalar(
            out=out_view(bb), in0=v_row(gg), scalar1=sm[:, 9:10],
            scalar2=sm[:, 10:11], op0=OP.mult, op1=OP.add),
            waits=w, incs=[("ve", 1)])
        dma("sync", out_d[g], out_view(b), ssem, waits=[("ve", P.cnt("ve"))])

    P.step("sync", waits=[("sA", P.cnt("sA")), ("sB", P.cnt("sB")),
                          ("wl", P.cnt("wl")), ("xf", P.cnt("xf"))])

    with nc.Block() as block:
        @block.sync
        def _(eng):
            P.emit("sync", eng, sems)

        @block.gpsimd
        def _(eng):
            P.emit("gpsimd", eng, sems)

        @block.tensor
        def _(eng):
            P.emit("tensor", eng, sems)

        @block.vector
        def _(eng):
            P.emit("vector", eng, sems)

        @block.scalar
        def _(eng):
            P.emit("scalar", eng, sems)

    ctx.close()
    return nc


_NC = {}


def kernel(L0_attn_x, L0_ffn_W0, L0_ffn_b0, L0_ffn_A, L0_ffn_B):
    x = np.asarray(L0_attn_x, dtype=np.float32)
    w0 = np.asarray(L0_ffn_W0, dtype=np.float32)
    b0 = np.asarray(L0_ffn_b0, dtype=np.float32)
    a_w = np.asarray(L0_ffn_A, dtype=np.float32)
    b_w = np.asarray(L0_ffn_B, dtype=np.float32)
    if "nc" not in _NC:
        _NC["nc"] = build_program()
    in_maps = [_core_inputs(c, x, w0, b0, a_w, b_w) for c in range(N_CORES)]
    res = run_bass_kernel_spmd(_NC["nc"], in_maps, core_ids=list(range(N_CORES)))
    out = np.concatenate([r["out"].reshape(-1, D) for r in res.results], axis=0)
    return out.reshape(1, SEQ, D).astype(np.float32)


# version 22
# speedup vs baseline: 1.3713x; 1.0102x over previous
import numpy as np
import ml_dtypes

import concourse.bass as bass
import concourse.mybir as mybir
from concourse.bass_utils import run_bass_kernel_spmd

F32 = mybir.dt.float32
BF16 = mybir.dt.bfloat16
U8 = mybir.dt.uint8
AX = mybir.AxisListType.X
AF = mybir.ActivationFunctionType
OP = mybir.AluOpType
BF = ml_dtypes.bfloat16

N_CORES = 8
SEQ = 8192
D = 4096
KHALF = 2049
KPAD = 2176
KC = KPAD // 128
FC = D // 128
OSLAB = 512
NO8 = D // OSLAB
EPS = 1e-5
NPASS = 2
NTOK = SEQ // N_CORES
PTOK = NTOK // NPASS
TBP = PTOK // 128
TB = NTOK // 128

_CONST = {}


def _constants():
    if _CONST:
        return _CONST
    k = np.arange(KPAD, dtype=np.float64)
    f = np.arange(D, dtype=np.float64)
    valid = (k < KHALF).astype(np.float64)
    ang = 2.0 * np.pi * np.outer(f, k) / D
    wc = np.cos(ang) * valid
    ws = -np.sin(ang) * valid
    ck = np.where((k == 0) | (k == D // 2), 1.0, 2.0) * valid / D
    tau = np.arange(D, dtype=np.float64)
    iang = 2.0 * np.pi * np.outer(k, tau) / D
    ctab = np.cos(iang) * ck[:, None]
    vc = np.cos(iang) * ck[:, None]
    vs = -np.sin(iang) * ck[:, None]

    def fm_chunks(tab):
        out = np.empty((KC, 128, FC, 128), dtype=BF)
        for j in range(KC):
            out[j] = tab[:, j * 128:(j + 1) * 128].reshape(FC, 128, 128) \
                .transpose(1, 0, 2)
        return out

    wtab = np.ascontiguousarray(
        np.concatenate([fm_chunks(wc), fm_chunks(ws)], axis=0))

    def tm_chunks(tab):
        out = np.empty((8, KC, 128, 512), dtype=BF)
        for sl in range(8):
            for j in range(KC):
                out[sl, j] = tab[j * 128:(j + 1) * 128,
                                 sl * 512:(sl + 1) * 512]
        return out

    TAU = 2176
    ctab_h = np.zeros((KPAD, TAU), dtype=np.float64)
    ctab_h[:, :2049] = ctab[:, :2049]
    ctab_c = np.empty((5, KC, 128, 512), dtype=BF)
    for sl in range(5):
        w = 512 if sl < 4 else 128
        for j in range(KC):
            ctab_c[sl, j, :, :w] = ctab_h[j * 128:(j + 1) * 128,
                                          sl * 512:sl * 512 + w]
            if w < 512:
                ctab_c[sl, j, :, w:] = 0
    ctab_c = np.ascontiguousarray(ctab_c.transpose(0, 2, 1, 3))
    cpf = np.where((f == 0) | (f == D // 2), 1.0, 2.0) * (f < 2049)
    wc2 = np.cos(ang) * valid * cpf[:, None]
    wtab2 = np.ascontiguousarray(fm_chunks(wc2)[:KC, :, :KC, :])
    irtab = np.ascontiguousarray(
        np.concatenate([tm_chunks(vc), tm_chunks(vs)], axis=1)
        .transpose(0, 2, 1, 3))

    ident = np.ascontiguousarray(np.eye(128), dtype=BF)
    onesb = np.ones((1, 128), dtype=BF)

    import jax
    with jax.default_device(jax.devices("cpu")[0]):
        u = np.asarray(jax.random.uniform(
            jax.random.key(42, impl="threefry2x32"), (1, SEQ, D)))[0]
    masks = [((u >= 0.25 * l) & (u < 0.25 * (l + 1))).astype(np.uint8)
             for l in (1, 2, 3)]

    _CONST.update(wtab=wtab, wtab2=wtab2, ctab=ctab_c, irtab=irtab,
                  ident=ident, onesb=onesb, masks=masks)
    return _CONST


def _core_inputs(core, x, w0, b0, a_w, b_w):
    c = _constants()
    t0 = core * NTOK
    xs = x[0, t0:t0 + NTOK]
    x_fm = np.ascontiguousarray(
        xs.reshape(NPASS, PTOK, D).transpose(0, 2, 1)
        .reshape(NPASS, FC, 128, PTOK).transpose(0, 2, 1, 3))
    x_tm = np.ascontiguousarray(
        xs.reshape(NPASS, TBP, 128, 8, 512).transpose(0, 3, 2, 1, 4))
    w0t = np.ascontiguousarray(
        w0.T.reshape(FC, 128, NO8, OSLAB).transpose(2, 1, 0, 3))
    acat0 = a_w.transpose(2, 0, 1).reshape(FC, 128, 4, 16)
    acat = np.zeros((FC, 128, 4, 32), dtype=np.float64)
    acat[:, :, :, :16] = acat0
    acat = np.ascontiguousarray(
        acat.reshape(FC, 128, 128).transpose(1, 0, 2))
    blt0 = b_w.transpose(0, 2, 1).reshape(4, 16, NO8, OSLAB)
    blt = np.zeros((4, 32, NO8, OSLAB), dtype=np.float64)
    blt[:, :16] = blt0
    blt = np.ascontiguousarray(
        blt.reshape(128, NO8, OSLAB).transpose(1, 0, 2))
    m_sl = [np.ascontiguousarray(
        m[t0:t0 + NTOK].reshape(TB, 128, NO8, OSLAB).transpose(2, 1, 0, 3))
        for m in c["masks"]]
    b0sl = np.ascontiguousarray(b0.reshape(NO8, 1, OSLAB))
    return dict(
        x_fm=x_fm.astype(np.float32), x_tm=x_tm.astype(np.float32),
        w0t=w0t.astype(np.float32), b0sl=b0sl.astype(np.float32),
        acat=acat.astype(np.float32), blt=blt.astype(np.float32),
        m1=m_sl[0], m2=m_sl[1], m3=m_sl[2],
        wtab=c["wtab"], wtab2=c["wtab2"], ctab=c["ctab"], irtab=c["irtab"],
        ident=c["ident"], identf=np.eye(128, dtype=np.float32),
        onesb=c["onesb"],
    )


class Plan:
    def __init__(self):
        self.steps = []
        self.counts = {}

    def cnt(self, s):
        return self.counts.get(s, 0)

    def step(self, engine, fn=None, waits=(), incs=()):
        self.steps.append((engine, list(waits), fn, list(incs)))
        for s, n in incs:
            self.counts[s] = self.counts.get(s, 0) + n

    def emit(self, engine_name, eng, sems):
        for e, waits, fn, incs in self.steps:
            if e != engine_name:
                continue
            for s, v in waits:
                if v > 0:
                    eng.wait_ge(sems[s], v)
            if fn is None:
                continue
            instr = fn(eng)
            for s, n in incs:
                instr.then_inc(sems[s], n)


def build_program(debug=False):
    nc = bass.Bass("TRN2", target_bir_lowering=False, debug=False)

    din = lambda name, shape, dt: nc.dram_tensor(
        name, list(shape), dt, kind="ExternalInput").ap()
    x_fm_d = din("x_fm", [NPASS, 128, FC, PTOK], F32)
    x_tm_d = din("x_tm", [NPASS, 8, 128, TBP, 512], F32)
    w0t_d = din("w0t", [NO8, 128, FC, OSLAB], F32)
    b0_d = din("b0sl", [NO8, 1, OSLAB], F32)
    acat_d = din("acat", [128, FC, 128], F32)
    blt_d = din("blt", [NO8, 128, OSLAB], F32)
    m_d = [din(f"m{l}", [NO8, 128, TB, OSLAB], U8) for l in (1, 2, 3)]
    wtab_d = din("wtab", [2 * KC, 128, FC, 128], BF16)
    ctab_d = din("ctab", [5, 128, KC, 512], BF16)
    wtab2_d = din("wtab2", [KC, 128, KC, 128], BF16)
    irtab_d = din("irtab", [8, 128, 2 * KC, 512], BF16)
    ident_d = din("ident", [128, 128], BF16)
    identf_d = din("identf", [128, 128], F32)
    onesb_d = din("onesb", [1, 128], BF16)
    out_d = nc.dram_tensor("out", [TB, 128, D], F32, kind="ExternalOutput").ap()
    dbg = {}
    if debug:
        for nm, shape in [("dbg_F", [128, 2 * KC * PTOK]),
                          ("dbg_corr", [128, TBP * 2176]),
                          ("dbg_sfm", [128, TBP * 2176]),
                          ("dbg_u", [128, 2 * KC * PTOK]),
                          ("dbg_v1", [128, TBP * D]),
                          ("dbg_x1", [128, TB * D]),
                          ("dbg_h", [16, 4 * NTOK]),
                          ("dbg_acat", [128, 2048]),
                          ("dbg_v", [128, TB * D])]:
            dbg[nm] = nc.dram_tensor(nm, shape, BF16, kind="ExternalOutput").ap()

    def dump(nm, src_ap, eng="sync"):
        if debug:
            P.step(eng, lambda e, o=dbg[nm], i=src_ap: e.dma_start(out=o, in_=i),
                   waits=[("ve", P.cnt("ve")), ("te", P.cnt("te")),
                          ("se", P.cnt("se"))], incs=[("sA", 16)])
            for en in ("vector", "tensor", "scalar", "gpsimd"):
                P.step(en, waits=[("sA", P.cnt("sA"))])

    from contextlib import ExitStack
    ctx = ExitStack()
    sb = lambda n, s, dt: ctx.enter_context(nc.sbuf_tensor(n, list(s), dt))
    psT = lambda n, s, dt=F32: ctx.enter_context(nc.psum_tensor(n, list(s), dt))

    ALPHA = sb("ALPHA", [128, 34816], BF16)
    BETA = sb("BETA", [128, 32768], BF16)
    GAMMA = sb("GAMMA", [128, 16384], BF16)
    PTS = sb("PTS", [128, 8704], BF16)
    h_b = sb("h_b", [128, NTOK], BF16)
    h3_b = sb("h3_b", [16, NTOK], BF16)
    blt3_b = sb("blt3_b", [16, 2, OSLAB], BF16)
    e_scr = sb("e_scr", [128, 512], BF16)
    s2_scr = sb("s2_scr", [128, 512], BF16)
    blt_b = sb("blt_b", [128, 2, OSLAB], BF16)
    b0_b = sb("b0_b", [1, 2, OSLAB], BF16)
    ctab_b = sb("ctab_b", [128, 4, 512], BF16)
    ident_b = sb("ident_b", [128, 128], BF16)
    identf_b = sb("identf_b", [128, 128], F32)
    onesb_b = sb("onesb_b", [1, 128], BF16)
    m8 = sb("m8", [128, 8], BF16)
    mf = sb("mf", [128, 8], F32)
    sm = sb("sm", [128, 16], F32)
    vsum_p = sb("vsum_p", [128, 64], F32)
    vsq_p = sb("vsq_p", [128, 64], F32)

    def A(o, n, dt=BF16):
        v = ALPHA[:, o:o + n]
        return v.bitcast(dt) if dt != BF16 else v

    def G(o, n, dt=BF16):
        v = GAMMA[:, o:o + n]
        return v.bitcast(dt) if dt != BF16 else v

    def PT(o, n, dt=BF16):
        v = PTS[:, o:o + n]
        return v.bitcast(dt) if dt != BF16 else v

    F_sz = 2 * KC * PTOK
    F_at = lambda kc: A(kc * PTOK, PTOK)
    CORR0 = F_sz
    TAU = 2176
    TSW = [512, 512, 512, 512, 128]
    corr_tb = lambda t: A(CORR0 + t * TAU, TAU)
    corr_tile = lambda t, sl, w: A(CORR0 + t * TAU + sl * 512, w)
    corr_pad = lambda t: A(CORR0 + t * TAU + 2049, 127)
    sfm_blk = lambda t, fcc: A(CORR0 + t * TAU + fcc * 128, 128)
    v1_tile = lambda t, isl: A(CORR0 + t * D + isl * 512, 512)
    v1_row = lambda t: A(CORR0 + t * D, D)
    xfm_blk = lambda fcc: A(CORR0 + fcc * PTOK, PTOK)
    v_tile = lambda g, o8: A((g * 8 + o8) * 512, 512)
    v_row = lambda g: A(g * D, D)
    acat_blk = lambda fcc, l: PT(fcc * 64 + l * 16, 16)

    wtab_blk = lambda buf, fcc: G(buf * 4096 + fcc * 128, 128)
    w2_blk = lambda buf, fcc: G(buf * 2176 + fcc * 128, 128)
    xtm_tile = lambda buf, t: G(buf * 2048 + t * 512, 512)
    irt_g = lambda buf, j: G(4096 + buf * 4096 + j * 512, 512)
    ctab_g = lambda buf, j: G(buf * 4096 + j * 512, 512)
    w0_blk = lambda fcc: G(fcc * 512, 512)
    out_view = lambda buf: G(buf * 8192, 8192, F32)

    P_blk = lambda kc, t: PT(kc * 512 + t * 128, 128)
    tscr = PT(0, 8192, F32)
    mscr = lambda i: PT(i * 1024, 1024, F32)
    x1t = PT(0, 8192, F32)
    acc_v = PT(0, 1024, F32)
    mkt = lambda i: PTS[:, 2048 + i * 2048: 2048 + (i + 1) * 2048].bitcast(U8)

    x1_blk = lambda g, fcc: BETA[:, g * D + fcc * 128: g * D + fcc * 128 + 128]

    bank = [psT(f"bk{i}", [128, 512]) for i in range(8)]
    tr_ps = lambda b: bank[4 + b][:, 0:128]

    def warm():
        P.step("tensor", lambda e: e.matmul(
            bank[6][:, 0:128], ident_b[:], ident_b[:], start=True, stop=True),
            waits=[("ve", P.cnt("ve"))])

    sems = {s: ctx.enter_context(nc.semaphore(s))
            for s in ("te", "ve", "se", "cs", "xf", "tA", "tB",
                      "cA", "cB", "cC", "cD", "iA", "iB", "iC", "iD",
                      "xA", "xB", "wl", "w0", "w1", "w2", "w3",
                      "mk", "ac", "sA", "sB", "tq")}
    P = Plan()

    def dma(engine, out_ap, in_ap, sname, waits=()):
        P.step(engine, lambda e, o=out_ap, i=in_ap: e.dma_start(out=o, in_=i),
               waits=waits, incs=[(sname, 16)])

    dma("sync", ident_b[:], ident_d[:], "cs")
    dma("sync", identf_b[:], identf_d[:], "cs")
    dma("sync", onesb_b[:], onesb_d[:], "cs")

    ln1_affine_ve = []
    for ps in range(NPASS):
        for q in range(4):
            w = FC // 4
            dma("gpsimd",
                A(CORR0 + q * w * PTOK, w * PTOK)
                .rearrange("p (c t) -> p c t", c=w),
                x_fm_d[ps, :, q * w:(q + 1) * w, :], "xf",
                waits=[("ve", ln1_affine_ve[q] if ps > 0 else P.cnt("ve"))])

        dma("sync", GAMMA[:, 0:4096].rearrange("p (c k) -> p c k", c=FC),
            wtab_d[0], "tA", waits=[("te", P.cnt("te"))])
        dma("sync", GAMMA[:, 4096:8192].rearrange("p (c k) -> p c k", c=FC),
            wtab_d[1], "tB")
        P.step("tensor", waits=[("xf", P.cnt("xf")), ("cs", P.cnt("cs")),
                                ("ve", P.cnt("ve"))])
        a2_evac = {}
        a3_ve = {}
        for kc in range(2 * KC):
            buf = kc % 2
            tsem = "tA" if buf == 0 else "tB"
            w = [(tsem, P.cnt(tsem))]
            if kc >= 2:
                w.append(("ve", a2_evac[kc - 2]))
            P.step("tensor", waits=w)
            for fc in range(FC):
                P.step("tensor", lambda e, b=buf, f=fc, k=kc: e.matmul(
                    bank[b][:], wtab_blk(b, f), xfm_blk(f),
                    start=(f == 0), stop=(f == FC - 1)),
                    incs=[("te", 1)] if fc == FC - 1 else [])
            if kc + 2 < 2 * KC:
                dma("sync",
                    GAMMA[:, buf * 4096:buf * 4096 + 4096]
                    .rearrange("p (c k) -> p c k", c=FC),
                    wtab_d[kc + 2], tsem, waits=[("te", P.cnt("te"))])
            P.step("vector", lambda e, b=buf, k=kc: e.tensor_copy(
                out=F_at(k), in_=bank[b][:]),
                waits=[("te", P.cnt("te"))], incs=[("ve", 1)])
            a2_evac[kc] = P.cnt("ve")
            if kc >= KC:
                k3 = kc - KC
                P.step("scalar", lambda e, k=k3: e.activation(
                    out=e_scr[:], in_=F_at(k), func=AF.Square),
                    waits=[("ve", a2_evac[kc])], incs=[("se", 1)])
                P.step("scalar", lambda e, k=k3: e.activation(
                    out=s2_scr[:], in_=F_at(KC + k), func=AF.Square),
                    incs=[("se", 1)])
                P.step("vector", lambda e, k=k3: e.tensor_add(
                    PTS[:, k * 512:(k + 1) * 512], e_scr[:], s2_scr[:]),
                    waits=[("se", P.cnt("se"))], incs=[("ve", 1)])
                a3_ve[k3] = P.cnt("ve")
        ve_F = P.cnt("ve")
        ve_P = P.cnt("ve")
        if ps == 0:
            dump("dbg_F", ALPHA[:, 0:F_sz])

        CGRP = [(0, 6), (6, 6), (12, 5)]
        def a4_tab(ts_, gi_, buf_, w=()):
            k0, ng = CGRP[gi_]
            dma("sync", GAMMA[:, buf_ * 4096: buf_ * 4096 + ng * 512]
                .rearrange("p (k c) -> p k c", k=ng),
                ctab_d[ts_][:, k0:k0 + ng, :], "cA" if buf_ == 0 else "cB", w)
        a4_tab(0, 0, 0, [("te", P.cnt("te"))])
        a4_tab(0, 1, 1)
        pend_evac = []
        gidx = 0
        for ts in range(5):
            bb = (ts % 2) * 4
            for gi in range(3):
                k0, ng = CGRP[gi]
                buf = gidx % 2
                csem = "cA" if buf == 0 else "cB"
                vtgt = a3_ve[k0 + ng - 1]
                if ts == 0:
                    vtgt = max(vtgt, a2_evac[2 * KC - 1])
                w = [(csem, P.cnt(csem)), ("ve", vtgt)]
                if gi == 0 and ts >= 2:
                    w.append(("ve", pend_evac[ts - 2]))
                P.step("tensor", waits=w)
                for kk2 in range(ng):
                    kc = k0 + kk2
                    for t in range(TBP):
                        P.step("tensor", lambda e, tt=t, k=kc, b=buf, j=kk2,
                               bo=bb, s=ts: e.matmul(
                            bank[bo + tt][:, 0:TSW[s]], P_blk(k, tt),
                            ctab_g(b, j)[:, 0:TSW[s]],
                            start=(k == 0), stop=(k == KC - 1)),
                            incs=[("te", 1)] if t == TBP - 1 else [])
                ngi = gidx + 2
                if ngi < 15:
                    nts, ngj = divmod(ngi, 3)
                    a4_tab(nts, ngj, buf, [("te", P.cnt("te"))])
                gidx += 1
            te_now = P.cnt("te")
            for t in range(TBP):
                P.step("vector", lambda e, tt=t, s=ts, bo=bb: e.tensor_copy(
                    out=corr_tile(tt, s, TSW[s]), in_=bank[bo + tt][:, 0:TSW[s]]),
                    waits=[("te", te_now)], incs=[("ve", 1)])
            pend_evac.append(P.cnt("ve"))
        for t in range(TBP):
            P.step("vector", lambda e, tt=t: e.memset(corr_pad(tt), -1e9),
                   waits=[("ve", P.cnt("ve"))], incs=[("ve", 1)])

        ve_corr = P.cnt("ve")
        if ps == 0:
            dump("dbg_corr", A(CORR0, TBP * TAU))
        for t in range(TBP):
            P.step("vector", lambda e, tt=t: e.max(
                out=m8[:], in_=corr_tb(tt)),
                waits=[("ve", ve_corr)], incs=[("ve", 1)])
            warm()
            P.step("vector", lambda e: e.tensor_copy(out=mf[:], in_=m8[:]),
                   waits=[("ve", P.cnt("ve"))], incs=[("ve", 1)])
            P.step("vector", lambda e: e.tensor_scalar_mul(
                sm[:, 0:1], mf[:, 0:1], -1.0),
                waits=[("ve", P.cnt("ve"))], incs=[("ve", 1)])
            P.step("scalar", lambda e: e.activation(
                out=sm[:, 8:16], in_=mf[:], func=AF.Exp,
                bias=sm[:, 0:1], scale=1.0),
                waits=[("ve", P.cnt("ve"))], incs=[("se", 1)])
            P.step("vector", lambda e: e.reduce_sum(
                sm[:, 1:2], sm[:, 8:16], axis=AX),
                waits=[("se", P.cnt("se"))], incs=[("ve", 1)])
            P.step("scalar", lambda e: e.activation(
                out=sm[:, 2:3], in_=sm[:, 1:2], func=AF.Ln),
                waits=[("ve", P.cnt("ve"))], incs=[("se", 1)])
            P.step("vector", lambda e: e.tensor_sub(
                sm[:, 3:4], sm[:, 0:1], sm[:, 2:3]),
                waits=[("se", P.cnt("se"))], incs=[("ve", 1)])
            P.step("vector", lambda e: e.tensor_copy(
                out=sm[:, 4:5], in_=m8[:, 7:8]),
                waits=[("ve", P.cnt("ve"))], incs=[("ve", 1)])
            warm()
            for cchunk in range(5):
                cs = corr_tile(t, cchunk, TSW[cchunk])
                ww = TSW[cchunk]
                P.step("scalar", lambda e, c=cs, w2=ww: e.activation(
                    out=e_scr[:, 0:w2], in_=c, func=AF.Exp,
                    bias=sm[:, 3:4], scale=1.0),
                    waits=[("ve", P.cnt("ve"))], incs=[("se", 1)])
                P.step("vector", lambda e, c=cs, cc=cchunk, w2=ww:
                       e.scalar_tensor_tensor(
                    out=tscr[:, cc * 512:cc * 512 + w2], in0=c,
                    scalar=sm[:, 4:5], in1=e_scr[:, 0:w2],
                    op0=OP.is_ge, op1=OP.mult),
                    waits=[("se", P.cnt("se"))], incs=[("ve", 1)])
                warm()
            ve_s = P.cnt("ve")
            for fc in range(KC):
                b = fc % 2
                w = [("ve", ve_s)] if fc < 2 else [("ve", P.cnt("ve") - 1)]
                P.step("tensor", lambda e, f=fc, bb=b: e.transpose(
                    tr_ps(bb), tscr[:, f * 128:(f + 1) * 128], identf_b[:]),
                    waits=w, incs=[("te", 1)])
                P.step("vector", lambda e, f=fc, bb=b, tt=t: e.tensor_copy(
                    out=sfm_blk(tt, f), in_=tr_ps(bb)),
                    waits=[("te", P.cnt("te"))], incs=[("ve", 1)])
        ve_sfm = P.cnt("ve")
        if ps == 0:
            dump("dbg_sfm", A(CORR0, TBP * TAU))

        dma("sync", GAMMA[:, 0:2176].rearrange("p (c k) -> p c k", c=KC),
            wtab2_d[0], "tA", waits=[("te", P.cnt("te")), ("ve", ve_sfm)])
        dma("sync", GAMMA[:, 2176:4352].rearrange("p (c k) -> p c k", c=KC),
            wtab2_d[1], "tB")

        def sfm_mov(fcc):
            return A(CORR0, TBP * TAU).rearrange("p (t q) -> p t q", q=TAU)[
                :, :, fcc * 128:fcc * 128 + 128]

        P.step("tensor", waits=[("ve", ve_sfm)])
        for kc in range(KC):
            buf = kc % 2
            tsem = "tA" if buf == 0 else "tB"
            P.step("tensor", waits=[(tsem, P.cnt(tsem)),
                                    ("ve", max(0, P.cnt("ve") - 1))])
            for fc in range(KC):
                P.step("tensor", lambda e, b=buf, f=fc, k=kc: e.matmul(
                    bank[k % 2][:], w2_blk(b, f), sfm_mov(f),
                    start=(f == 0), stop=(f == KC - 1)),
                    incs=[("te", 1)] if fc == KC - 1 else [])
            if kc + 2 < KC:
                dma("sync",
                    GAMMA[:, buf * 2176:buf * 2176 + 2176]
                    .rearrange("p (c k) -> p c k", c=KC),
                    wtab2_d[kc + 2], tsem, waits=[("te", P.cnt("te"))])
            te_now = P.cnt("te")
            fre, fim = F_at(kc), F_at(KC + kc)
            P.step("vector", lambda e, a=fre, k=kc: e.tensor_mul(
                a, a, bank[k % 2][:]),
                waits=[("te", te_now), ("ve", P.cnt("ve"))], incs=[("ve", 1)])
            P.step("vector", lambda e, a=fim, k=kc: e.tensor_mul(
                a, a, bank[k % 2][:]), incs=[("ve", 1)])
            ve_u = P.cnt("ve")
        if ps == 0:
            dump("dbg_u", ALPHA[:, 0:F_sz])

        IGRP = [(0, 7), (7, 7), (14, 7), (21, 7), (28, 6)]
        def a9_tab(isl_, gi_, buf_, w=()):
            k0, ng = IGRP[gi_]
            dma("sync", GAMMA[:, 4096 + buf_ * 4096: 4096 + buf_ * 4096 + ng * 512]
                .rearrange("p (k c) -> p k c", k=ng),
                irtab_d[isl_][:, k0:k0 + ng, :], "iA" if buf_ == 0 else "iB", w)
        a9_tab(0, 0, 0, [("te", P.cnt("te")), ("ve", ve_u)])
        a9_tab(0, 1, 1)
        dma("gpsimd", G(0, 2048).rearrange("p (t c) -> p t c", t=TBP),
            x_tm_d[ps, 0], "xA", waits=[("te", P.cnt("te")), ("ve", ve_u)])
        P.step("tensor", waits=[("ve", ve_u)])
        pend_v1 = []
        gidx9 = 0
        for isl in range(8):
            bb9 = (isl % 2) * 4
            for gi in range(5):
                k0, ng = IGRP[gi]
                buf = gidx9 % 2
                isem = "iA" if buf == 0 else "iB"
                w = [(isem, P.cnt(isem))]
                if gi == 0 and isl >= 2:
                    w.append(("ve", pend_v1[isl - 2]))
                P.step("tensor", waits=w)
                for kk2 in range(ng):
                    kc = k0 + kk2
                    for t in range(TBP):
                        P.step("tensor", lambda e, tt=t, k=kc, b=buf, j=kk2,
                               bo=bb9: e.matmul(
                            bank[bo + tt][:],
                            A(k * PTOK + tt * 128, 128), irt_g(b, j),
                            start=(k == 0), stop=(k == 2 * KC - 1)),
                            incs=[("te", 1)] if t == TBP - 1 else [])
                ngi = gidx9 + 2
                if ngi < 40:
                    nisl, ngj = divmod(ngi, 5)
                    a9_tab(nisl, ngj, buf, [("te", P.cnt("te"))])
                gidx9 += 1
            if isl + 1 < 8:
                xsem2 = "xA" if (isl + 1) % 2 == 0 else "xB"
                dma("gpsimd",
                    G(((isl + 1) % 2) * 2048, 2048)
                    .rearrange("p (t c) -> p t c", t=TBP),
                    x_tm_d[ps, isl + 1], xsem2,
                    waits=[("ve", P.cnt("ve"))])
            te_now = P.cnt("te")
            xb = isl % 2
            xsem = "xA" if xb == 0 else "xB"
            for t in range(TBP):
                P.step("vector", lambda e, tt=t, b=xb, i=isl, bo=bb9:
                       e.scalar_tensor_tensor(
                           out=v1_tile(tt, i), in0=bank[bo + tt][:],
                           scalar=1.0, in1=xtm_tile(b, tt),
                           op0=OP.mult, op1=OP.add,
                           accum_out=vsum_p[:, tt * 8 + i: tt * 8 + i + 1]),
                       waits=[("te", te_now), (xsem, P.cnt(xsem))],
                       incs=[("ve", 1)])
                P.step("vector", lambda e, tt=t, i=isl:
                       e.scalar_tensor_tensor(
                           out=s2_scr[:], in0=v1_tile(tt, i), scalar=1.0,
                           in1=v1_tile(tt, i), op0=OP.mult, op1=OP.mult,
                           accum_out=vsq_p[:, tt * 8 + i: tt * 8 + i + 1]),
                       waits=[("ve", P.cnt("ve"))], incs=[("ve", 1)])
            pend_v1.append(P.cnt("ve"))

        if ps == NPASS - 1:
            post_a9 = (P.cnt("te"), P.cnt("ve"))
        if ps == 0:
            dump("dbg_v1", A(CORR0, TBP * D))
        for t in range(TBP):
            g = ps * TBP + t
            P.step("vector", lambda e, tt=t: e.reduce_sum(
                sm[:, 5:6], vsum_p[:, tt * 8:(tt + 1) * 8], axis=AX),
                waits=[("ve", P.cnt("ve"))], incs=[("ve", 1)])
            P.step("vector", lambda e, tt=t: e.reduce_sum(
                sm[:, 6:7], vsq_p[:, tt * 8:(tt + 1) * 8], axis=AX),
                incs=[("ve", 1)])
            P.step("vector", lambda e: e.tensor_scalar_mul(
                sm[:, 5:6], sm[:, 5:6], 1.0 / D),
                waits=[("ve", P.cnt("ve"))], incs=[("ve", 1)])
            P.step("vector", lambda e: e.tensor_scalar_mul(
                sm[:, 6:7], sm[:, 6:7], 1.0 / D), incs=[("ve", 1)])
            P.step("vector", lambda e: e.tensor_mul(
                sm[:, 7:8], sm[:, 5:6], sm[:, 5:6]),
                waits=[("ve", P.cnt("ve"))], incs=[("ve", 1)])
            P.step("vector", lambda e: e.tensor_sub(
                sm[:, 7:8], sm[:, 6:7], sm[:, 7:8]),
                waits=[("ve", P.cnt("ve"))], incs=[("ve", 1)])
            P.step("vector", lambda e: e.tensor_scalar_add(
                sm[:, 7:8], sm[:, 7:8], EPS),
                waits=[("ve", P.cnt("ve"))], incs=[("ve", 1)])
            P.step("vector", lambda e: e.reciprocal(
                sm[:, 8:9], sm[:, 7:8]),
                waits=[("ve", P.cnt("ve"))], incs=[("ve", 1)])
            warm()
            P.step("scalar", lambda e: e.activation(
                out=sm[:, 9:10], in_=sm[:, 8:9], func=AF.Sqrt),
                waits=[("ve", P.cnt("ve"))], incs=[("se", 1)])
            P.step("vector", lambda e: e.tensor_mul(
                sm[:, 10:11], sm[:, 5:6], sm[:, 9:10]),
                waits=[("se", P.cnt("se"))], incs=[("ve", 1)])
            P.step("vector", lambda e: e.tensor_scalar_mul(
                sm[:, 10:11], sm[:, 10:11], -1.0),
                waits=[("ve", P.cnt("ve"))], incs=[("ve", 1)])
            P.step("vector", lambda e, tt=t: e.tensor_scalar(
                out=x1t[:], in0=v1_row(tt), scalar1=sm[:, 9:10],
                scalar2=sm[:, 10:11], op0=OP.mult, op1=OP.add),
                waits=[("ve", P.cnt("ve"))], incs=[("ve", 1)])
            if ps == 0:
                ln1_affine_ve.append(P.cnt("ve"))
            warm()
            ve_x1 = P.cnt("ve")
            for fc in range(FC):
                b = fc % 2
                w = [("ve", ve_x1)] if fc < 2 else [("ve", P.cnt("ve") - 1)]
                P.step("tensor", lambda e, f=fc, bb=b: e.transpose(
                    tr_ps(bb), x1t[:, f * 128:(f + 1) * 128], identf_b[:]),
                    waits=w, incs=[("te", 1)])
                P.step("vector", lambda e, f=fc, bb=b, gg=g: e.tensor_copy(
                    out=x1_blk(gg, f), in_=tr_ps(bb)),
                    waits=[("te", P.cnt("te"))], incs=[("ve", 1)])
    ve_x1fm = P.cnt("ve")
    if debug:
        dump("dbg_x1", BETA[:, :])

    dma("gpsimd", PTS[:, 0:4096].rearrange("p (c r) -> p c r", c=FC),
        acat_d[:], "ac", waits=[("ve", ve_x1fm), ("te", P.cnt("te"))])

    def x1_mov(th, fcc):
        return BETA[:, :].rearrange("p (g q) -> p g q", q=D)[
            :, th * 4:(th + 1) * 4, fcc * 128:fcc * 128 + 128]

    if debug:
        P.step("sync", waits=[("ac", P.cnt("ac"))])
        dump("dbg_acat", PTS[:, 0:2048])
    P.step("tensor", waits=[("ve", ve_x1fm), ("ac", P.cnt("ac"))])
    for th in range(2):
        for fc in range(FC):
            P.step("tensor", lambda e, t=th, f=fc: e.matmul(
                bank[t][0:96, :], PT(f * 128, 96), x1_mov(t, f),
                start=(f == 0), stop=(f == FC - 1)),
                incs=[("te", 1)] if fc == FC - 1 else [])
        for fc in range(FC):
            P.step("tensor", lambda e, t=th, f=fc: e.matmul(
                bank[2 + t][0:16, :], PT(f * 128 + 96, 16), x1_mov(t, f),
                start=(f == 0), stop=(f == FC - 1)),
                incs=[("te", 1)] if fc == FC - 1 else [])
    te_h = P.cnt("te")
    for th in range(2):
        P.step("vector", lambda e, t=th: e.tensor_copy(
            out=h_b[0:96, t * 512: t * 512 + 512], in_=bank[t][0:96, :]),
            waits=[("te", te_h)], incs=[("ve", 1)])
        P.step("vector", lambda e, t=th: e.tensor_copy(
            out=h3_b[:, t * 512: t * 512 + 512], in_=bank[2 + t][0:16, :]),
            incs=[("ve", 1)])
    ve_h = P.cnt("ve")
    if debug:
        dump("dbg_h", h_b[:, :])


    def b_quarters(o8, extra_waits=()):
        for q in range(4):
            w = list(extra_waits) if q == 0 else []
            if o8 >= 1:
                w.append(("tq", 4 * (o8 - 1) + q + 1))
            dma("gpsimd",
                GAMMA[:, q * 4096:(q + 1) * 4096]
                .rearrange("p (c k) -> p c k", c=8),
                w0t_d[o8, :, q * 8:(q + 1) * 8, :], f"w{q}", waits=w)

    def b_small(o8, waits_b):
        dma("gpsimd", b0_b[:, o8 % 2], b0_d[o8], "wl", waits=waits_b)
        dma("gpsimd", blt_b[:, o8 % 2], blt_d[o8], "wl")
        dma("gpsimd", blt3_b[:, o8 % 2], blt_d[o8, 96:112, :], "wl")

    def b_masks(o8, waits_m):
        for i in range(3):
            dma("sync", mkt(i).rearrange("p (t c) -> p t c", t=TB),
                m_d[i][o8], "mk", waits=waits_m if i == 0 else ())

    b_quarters(0, extra_waits=[("te", post_a9[0]), ("ve", post_a9[1])])
    b_small(0, ())
    b_masks(0, [("te", P.cnt("te"))])

    SETS = [bank[0:4], bank[4:8]]
    NH = NO8 * 2
    evac_ve = {}
    sel_ve = {}
    lora_te = {}
    tail_te = {}

    def app_lora(k, gi):
        o8, h = k // 2, k % 2
        g = h * 4 + gi
        st = SETS[k % 2]
        if gi == 0:
            w = [("ve", evac_ve[k]), ("wl", 48 * (o8 + 1))]
        else:
            w = [("ve", sel_ve[(k, gi - 1)])]
        P.step("tensor", waits=w)
        for l in range(4):
            if l < 3:
                lh = h_b[32 * l:32 * l + 16, g * 128: g * 128 + 128]
                lb = blt_b[32 * l:32 * l + 16, o8 % 2, :]
            else:
                lh = h3_b[:, g * 128: g * 128 + 128]
                lb = blt3_b[:, o8 % 2, :]
            P.step("tensor", lambda e, ll=l, s=st, a=lh, b=lb: e.matmul(
                s[ll][:], a, b, start=True, stop=True),
                incs=[("te", 1)] if l == 3 else [])
        lora_te[(k, gi)] = P.cnt("te")

    def app_sel(k, gi):
        o8, h = k // 2, k % 2
        g = h * 4 + gi
        st = SETS[k % 2]
        P.step("vector", lambda e, s=st: e.tensor_copy(
            out=acc_v[:], in_=s[0][:]),
            waits=[("te", lora_te[(k, gi)]), ("mk", 48 * (o8 + 1)),
                   ("ve", P.cnt("ve"))], incs=[("ve", 1)])
        for i in range(3):
            P.step("vector", lambda e, ii=i, gg=g, s=st: e.copy_predicated(
                out=acc_v[:],
                mask=mkt(ii)[:, gg * OSLAB:(gg + 1) * OSLAB],
                data=s[1 + ii][:]),
                waits=[("ve", P.cnt("ve"))], incs=[("ve", 1)])
        sel_ve[(k, gi)] = P.cnt("ve")
        P.step("vector", lambda e, gg=g, o=o8: e.scalar_tensor_tensor(
            out=v_tile(gg, o), in0=acc_v[:], scalar=8.0, in1=v_tile(gg, o),
            op0=OP.mult, op1=OP.add,
            accum_out=vsum_p[:, gg * 8 + o: gg * 8 + o + 1]),
            waits=[("ve", P.cnt("ve"))], incs=[("ve", 1)])
        P.step("vector", lambda e, gg=g, o=o8: e.scalar_tensor_tensor(
            out=s2_scr[:], in0=v_tile(gg, o), scalar=1.0, in1=v_tile(gg, o),
            op0=OP.mult, op1=OP.mult,
            accum_out=vsq_p[:, gg * 8 + o: gg * 8 + o + 1]),
            waits=[("ve", P.cnt("ve"))], incs=[("ve", 1)])

    for k in range(NH):
        o8, h = k // 2, k % 2
        st = SETS[k % 2]
        P.step("tensor",
               waits=[("ve", ve_h if k < 2 else sel_ve[(k - 2, 3)])])
        for fc in range(FC):
            if h == 0 and fc % 8 == 0:
                P.step("tensor", waits=[(f"w{fc // 8}", 16 * (o8 + 1))])
            for gi in range(4):
                g = h * 4 + gi
                inc = [("tq", 1)] if (h == 1 and fc % 8 == 7 and gi == 3) \
                    else []
                P.step("tensor", lambda e, gg=g, f=fc, s=st, i=gi: e.matmul(
                    s[i][:], x1_blk(gg, f), w0_blk(f),
                    start=(f == 0), stop=False), incs=inc)
            if k > 0 and fc % 8 == 7:
                app_lora(k - 1, fc // 8)
                app_sel(k - 1, fc // 8)
        P.step("tensor", waits=[("wl", 48 * (o8 + 1))])
        for gi in range(4):
            g = h * 4 + gi
            P.step("tensor", lambda e, o=o8, s=st, i=gi: e.matmul(
                s[i][:], onesb_b[:], b0_b[:, o % 2],
                start=False, stop=False))
            for j in range(4):
                P.step("tensor", lambda e, gg=g, o=o8, jj=j, s=st, i=gi:
                       e.matmul(
                           s[i][:, jj * 128:(jj + 1) * 128],
                           x1_blk(gg, 4 * o + jj), ident_b[:],
                           start=False, stop=(jj == 3)),
                       incs=[("te", 1)] if j == 3 else [])
        tail_te[k] = P.cnt("te")
        for gi in range(4):
            g = h * 4 + gi
            P.step("vector", lambda e, gg=g, o=o8, s=st, i=gi: e.tensor_copy(
                out=v_tile(gg, o), in_=s[i][:]),
                waits=[("te", tail_te[k]), ("ve", P.cnt("ve"))],
                incs=[("ve", 1)])
        evac_ve[k] = P.cnt("ve")
        if h == 0 and o8 + 1 < NO8:
            b_quarters(o8 + 1)
            b_small(o8 + 1, [("te", P.cnt("te"))])
        if h == 0 and o8 >= 1:
            b_masks(o8, [("ve", P.cnt("ve"))])
    for gi in range(4):
        app_lora(NH - 1, gi)
        app_sel(NH - 1, gi)

    te_b = P.cnt("te")
    for g in range(TB):
        P.step("vector", lambda e, gg=g: e.reduce_sum(
            sm[:, 5:6], vsum_p[:, gg * 8:(gg + 1) * 8], axis=AX),
            waits=[("ve", P.cnt("ve")), ("te", te_b)], incs=[("ve", 1)])
        P.step("vector", lambda e, gg=g: e.reduce_sum(
            sm[:, 6:7], vsq_p[:, gg * 8:(gg + 1) * 8], axis=AX),
            incs=[("ve", 1)])
        P.step("vector", lambda e: e.tensor_scalar_mul(
            sm[:, 5:6], sm[:, 5:6], 1.0 / D),
            waits=[("ve", P.cnt("ve"))], incs=[("ve", 1)])
        P.step("vector", lambda e: e.tensor_scalar_mul(
            sm[:, 6:7], sm[:, 6:7], 1.0 / D), incs=[("ve", 1)])
        P.step("vector", lambda e: e.tensor_mul(
            sm[:, 7:8], sm[:, 5:6], sm[:, 5:6]),
            waits=[("ve", P.cnt("ve"))], incs=[("ve", 1)])
        P.step("vector", lambda e: e.tensor_sub(
            sm[:, 7:8], sm[:, 6:7], sm[:, 7:8]),
            waits=[("ve", P.cnt("ve"))], incs=[("ve", 1)])
        P.step("vector", lambda e: e.tensor_scalar_add(
            sm[:, 7:8], sm[:, 7:8], EPS),
            waits=[("ve", P.cnt("ve"))], incs=[("ve", 1)])
        P.step("vector", lambda e: e.reciprocal(
            sm[:, 8:9], sm[:, 7:8]),
            waits=[("ve", P.cnt("ve"))], incs=[("ve", 1)])
        P.step("scalar", lambda e: e.activation(
            out=sm[:, 9:10], in_=sm[:, 8:9], func=AF.Sqrt),
            waits=[("ve", P.cnt("ve"))], incs=[("se", 1)])
        P.step("vector", lambda e: e.tensor_mul(
            sm[:, 10:11], sm[:, 5:6], sm[:, 9:10]),
            waits=[("se", P.cnt("se"))], incs=[("ve", 1)])
        P.step("vector", lambda e: e.tensor_scalar_mul(
            sm[:, 10:11], sm[:, 10:11], -1.0),
            waits=[("ve", P.cnt("ve"))], incs=[("ve", 1)])
        b = g % 2
        ssem = "sA" if b == 0 else "sB"
        w = [("ve", P.cnt("ve"))]
        if g >= 2:
            w.append((ssem, P.cnt(ssem)))
        P.step("vector", lambda e, gg=g, bb=b: e.tensor_scalar(
            out=out_view(bb), in0=v_row(gg), scalar1=sm[:, 9:10],
            scalar2=sm[:, 10:11], op0=OP.mult, op1=OP.add),
            waits=w, incs=[("ve", 1)])
        dma("sync", out_d[g], out_view(b), ssem, waits=[("ve", P.cnt("ve"))])

    P.step("sync", waits=[("sA", P.cnt("sA")), ("sB", P.cnt("sB")),
                          ("wl", P.cnt("wl")), ("xf", P.cnt("xf"))])

    with nc.Block() as block:
        @block.sync
        def _(eng):
            P.emit("sync", eng, sems)

        @block.gpsimd
        def _(eng):
            P.emit("gpsimd", eng, sems)

        @block.tensor
        def _(eng):
            P.emit("tensor", eng, sems)

        @block.vector
        def _(eng):
            P.emit("vector", eng, sems)

        @block.scalar
        def _(eng):
            P.emit("scalar", eng, sems)

    ctx.close()
    return nc


_NC = {}


def kernel(L0_attn_x, L0_ffn_W0, L0_ffn_b0, L0_ffn_A, L0_ffn_B):
    x = np.asarray(L0_attn_x, dtype=np.float32)
    w0 = np.asarray(L0_ffn_W0, dtype=np.float32)
    b0 = np.asarray(L0_ffn_b0, dtype=np.float32)
    a_w = np.asarray(L0_ffn_A, dtype=np.float32)
    b_w = np.asarray(L0_ffn_B, dtype=np.float32)
    if "nc" not in _NC:
        _NC["nc"] = build_program()
    in_maps = [_core_inputs(c, x, w0, b0, a_w, b_w) for c in range(N_CORES)]
    res = run_bass_kernel_spmd(_NC["nc"], in_maps, core_ids=list(range(N_CORES)))
    out = np.concatenate([r["out"].reshape(-1, D) for r in res.results], axis=0)
    return out.reshape(1, SEQ, D).astype(np.float32)


# version 23
# speedup vs baseline: 1.3782x; 1.0051x over previous
import numpy as np
import ml_dtypes

import concourse.bass as bass
import concourse.mybir as mybir
from concourse.bass_utils import run_bass_kernel_spmd

F32 = mybir.dt.float32
BF16 = mybir.dt.bfloat16
U8 = mybir.dt.uint8
AX = mybir.AxisListType.X
AF = mybir.ActivationFunctionType
OP = mybir.AluOpType
BF = ml_dtypes.bfloat16

N_CORES = 8
SEQ = 8192
D = 4096
KHALF = 2049
KPAD = 2176
KC = KPAD // 128
FC = D // 128
OSLAB = 512
NO8 = D // OSLAB
EPS = 1e-5
NPASS = 2
NTOK = SEQ // N_CORES
PTOK = NTOK // NPASS
TBP = PTOK // 128
TB = NTOK // 128

_CONST = {}


def _constants():
    if _CONST:
        return _CONST
    k = np.arange(KPAD, dtype=np.float64)
    f = np.arange(D, dtype=np.float64)
    valid = (k < KHALF).astype(np.float64)
    ang = 2.0 * np.pi * np.outer(f, k) / D
    wc = np.cos(ang) * valid
    ws = -np.sin(ang) * valid
    ck = np.where((k == 0) | (k == D // 2), 1.0, 2.0) * valid / D
    tau = np.arange(D, dtype=np.float64)
    iang = 2.0 * np.pi * np.outer(k, tau) / D
    ctab = np.cos(iang) * ck[:, None]
    vc = np.cos(iang) * ck[:, None]
    vs = -np.sin(iang) * ck[:, None]

    def fm_chunks(tab):
        out = np.empty((KC, 128, FC, 128), dtype=BF)
        for j in range(KC):
            out[j] = tab[:, j * 128:(j + 1) * 128].reshape(FC, 128, 128) \
                .transpose(1, 0, 2)
        return out

    wtab = np.ascontiguousarray(
        np.concatenate([fm_chunks(wc), fm_chunks(ws)], axis=0))

    def tm_chunks(tab):
        out = np.empty((8, KC, 128, 512), dtype=BF)
        for sl in range(8):
            for j in range(KC):
                out[sl, j] = tab[j * 128:(j + 1) * 128,
                                 sl * 512:(sl + 1) * 512]
        return out

    TAU = 2176
    ctab_h = np.zeros((KPAD, TAU), dtype=np.float64)
    ctab_h[:, :2049] = ctab[:, :2049]
    ctab_c = np.empty((5, KC, 128, 512), dtype=BF)
    for sl in range(5):
        w = 512 if sl < 4 else 128
        for j in range(KC):
            ctab_c[sl, j, :, :w] = ctab_h[j * 128:(j + 1) * 128,
                                          sl * 512:sl * 512 + w]
            if w < 512:
                ctab_c[sl, j, :, w:] = 0
    ctab_c = np.ascontiguousarray(ctab_c.transpose(0, 2, 1, 3))
    cpf = np.where((f == 0) | (f == D // 2), 1.0, 2.0) * (f < 2049)
    wc2 = np.cos(ang) * valid * cpf[:, None]
    wtab2 = np.ascontiguousarray(fm_chunks(wc2)[:KC, :, :KC, :])
    irtab = np.ascontiguousarray(
        np.concatenate([tm_chunks(vc), tm_chunks(vs)], axis=1)
        .transpose(0, 2, 1, 3))

    ident = np.ascontiguousarray(np.eye(128), dtype=BF)
    onesb = np.ones((1, 128), dtype=BF)

    import jax
    with jax.default_device(jax.devices("cpu")[0]):
        u = np.asarray(jax.random.uniform(
            jax.random.key(42, impl="threefry2x32"), (1, SEQ, D)))[0]
    masks = [((u >= 0.25 * l) & (u < 0.25 * (l + 1))).astype(np.uint8)
             for l in (1, 2, 3)]

    _CONST.update(wtab=wtab, wtab2=wtab2, ctab=ctab_c, irtab=irtab,
                  ident=ident, onesb=onesb, masks=masks)
    return _CONST


def _core_inputs(core, x, w0, b0, a_w, b_w):
    c = _constants()
    t0 = core * NTOK
    xs = x[0, t0:t0 + NTOK]
    x_fm = np.ascontiguousarray(
        xs.reshape(NPASS, PTOK, D).transpose(0, 2, 1)
        .reshape(NPASS, FC, 128, PTOK).transpose(0, 2, 1, 3))
    x_tm = np.ascontiguousarray(
        xs.reshape(NPASS, TBP, 128, 8, 512).transpose(0, 3, 2, 1, 4))
    w0t = np.ascontiguousarray(
        w0.T.reshape(FC, 128, NO8, OSLAB).transpose(2, 1, 0, 3))
    acat0 = a_w.transpose(2, 0, 1).reshape(FC, 128, 4, 16)
    acat = np.zeros((FC, 128, 4, 32), dtype=np.float64)
    acat[:, :, :, :16] = acat0
    acat = np.ascontiguousarray(
        acat.reshape(FC, 128, 128).transpose(1, 0, 2))
    blt0 = b_w.transpose(0, 2, 1).reshape(4, 16, NO8, OSLAB)
    blt = np.zeros((4, 32, NO8, OSLAB), dtype=np.float64)
    blt[:, :16] = blt0
    blt = np.ascontiguousarray(
        blt.reshape(128, NO8, OSLAB).transpose(1, 0, 2))
    m_sl = [np.ascontiguousarray(
        m[t0:t0 + NTOK].reshape(TB, 128, NO8, OSLAB).transpose(2, 1, 0, 3))
        for m in c["masks"]]
    b0sl = np.ascontiguousarray(b0.reshape(NO8, 1, OSLAB))
    return dict(
        x_fm=x_fm.astype(np.float32), x_tm=x_tm.astype(np.float32),
        w0t=w0t.astype(np.float32), b0sl=b0sl.astype(np.float32),
        acat=acat.astype(np.float32), blt=blt.astype(np.float32),
        m1=m_sl[0], m2=m_sl[1], m3=m_sl[2],
        wtab=c["wtab"], wtab2=c["wtab2"], ctab=c["ctab"], irtab=c["irtab"],
        ident=c["ident"], identf=np.eye(128, dtype=np.float32),
        onesb=c["onesb"],
    )


class Plan:
    def __init__(self):
        self.steps = []
        self.counts = {}

    def cnt(self, s):
        return self.counts.get(s, 0)

    def step(self, engine, fn=None, waits=(), incs=()):
        self.steps.append((engine, list(waits), fn, list(incs)))
        for s, n in incs:
            self.counts[s] = self.counts.get(s, 0) + n

    def emit(self, engine_name, eng, sems):
        for e, waits, fn, incs in self.steps:
            if e != engine_name:
                continue
            for s, v in waits:
                if v > 0:
                    eng.wait_ge(sems[s], v)
            if fn is None:
                continue
            instr = fn(eng)
            for s, n in incs:
                instr.then_inc(sems[s], n)


def build_program(debug=False):
    nc = bass.Bass("TRN2", target_bir_lowering=False, debug=False)

    din = lambda name, shape, dt: nc.dram_tensor(
        name, list(shape), dt, kind="ExternalInput").ap()
    x_fm_d = din("x_fm", [NPASS, 128, FC, PTOK], F32)
    x_tm_d = din("x_tm", [NPASS, 8, 128, TBP, 512], F32)
    w0t_d = din("w0t", [NO8, 128, FC, OSLAB], F32)
    b0_d = din("b0sl", [NO8, 1, OSLAB], F32)
    acat_d = din("acat", [128, FC, 128], F32)
    blt_d = din("blt", [NO8, 128, OSLAB], F32)
    m_d = [din(f"m{l}", [NO8, 128, TB, OSLAB], U8) for l in (1, 2, 3)]
    wtab_d = din("wtab", [2 * KC, 128, FC, 128], BF16)
    ctab_d = din("ctab", [5, 128, KC, 512], BF16)
    wtab2_d = din("wtab2", [KC, 128, KC, 128], BF16)
    irtab_d = din("irtab", [8, 128, 2 * KC, 512], BF16)
    ident_d = din("ident", [128, 128], BF16)
    identf_d = din("identf", [128, 128], F32)
    onesb_d = din("onesb", [1, 128], BF16)
    out_d = nc.dram_tensor("out", [TB, 128, D], F32, kind="ExternalOutput").ap()
    dbg = {}
    if debug:
        for nm, shape in [("dbg_F", [128, 2 * KC * PTOK]),
                          ("dbg_corr", [128, TBP * 2176]),
                          ("dbg_sfm", [128, TBP * 2176]),
                          ("dbg_u", [128, 2 * KC * PTOK]),
                          ("dbg_v1", [128, TBP * D]),
                          ("dbg_x1", [128, TB * D]),
                          ("dbg_h", [16, 4 * NTOK]),
                          ("dbg_acat", [128, 2048]),
                          ("dbg_v", [128, TB * D])]:
            dbg[nm] = nc.dram_tensor(nm, shape, BF16, kind="ExternalOutput").ap()

    def dump(nm, src_ap, eng="sync"):
        if debug:
            P.step(eng, lambda e, o=dbg[nm], i=src_ap: e.dma_start(out=o, in_=i),
                   waits=[("ve", P.cnt("ve")), ("te", P.cnt("te")),
                          ("se", P.cnt("se"))], incs=[("sA", 16)])
            for en in ("vector", "tensor", "scalar", "gpsimd"):
                P.step(en, waits=[("sA", P.cnt("sA"))])

    from contextlib import ExitStack
    ctx = ExitStack()
    sb = lambda n, s, dt: ctx.enter_context(nc.sbuf_tensor(n, list(s), dt))
    psT = lambda n, s, dt=F32: ctx.enter_context(nc.psum_tensor(n, list(s), dt))

    ALPHA = sb("ALPHA", [128, 34816], BF16)
    BETA = sb("BETA", [128, 32768], BF16)
    GAMMA = sb("GAMMA", [128, 16384], BF16)
    PTS = sb("PTS", [128, 8704], BF16)
    h_b = sb("h_b", [128, NTOK], BF16)
    h3_b = sb("h3_b", [16, NTOK], BF16)
    blt3_b = sb("blt3_b", [16, 2, OSLAB], BF16)
    e_scr = sb("e_scr", [128, 512], BF16)
    s2_scr = sb("s2_scr", [128, 512], BF16)
    blt_b = sb("blt_b", [128, 2, OSLAB], BF16)
    b0_b = sb("b0_b", [1, 2, OSLAB], BF16)
    ctab_b = sb("ctab_b", [128, 4, 512], BF16)
    ident_b = sb("ident_b", [128, 128], BF16)
    identf_b = sb("identf_b", [128, 128], F32)
    onesb_b = sb("onesb_b", [1, 128], BF16)
    m8 = sb("m8", [128, 8], BF16)
    mf = sb("mf", [128, 8], F32)
    sm = sb("sm", [128, 16], F32)
    vsum_p = sb("vsum_p", [128, 64], F32)
    vsq_p = sb("vsq_p", [128, 64], F32)

    def A(o, n, dt=BF16):
        v = ALPHA[:, o:o + n]
        return v.bitcast(dt) if dt != BF16 else v

    def G(o, n, dt=BF16):
        v = GAMMA[:, o:o + n]
        return v.bitcast(dt) if dt != BF16 else v

    def PT(o, n, dt=BF16):
        v = PTS[:, o:o + n]
        return v.bitcast(dt) if dt != BF16 else v

    F_sz = 2 * KC * PTOK
    F_at = lambda kc: A(kc * PTOK, PTOK)
    CORR0 = F_sz
    TAU = 2176
    TSW = [512, 512, 512, 512, 128]
    corr_tb = lambda t: A(CORR0 + t * TAU, TAU)
    corr_tile = lambda t, sl, w: A(CORR0 + t * TAU + sl * 512, w)
    corr_pad = lambda t: A(CORR0 + t * TAU + 2049, 127)
    sfm_blk = lambda t, fcc: A(CORR0 + t * TAU + fcc * 128, 128)
    v1_tile = lambda t, isl: A(CORR0 + t * D + isl * 512, 512)
    v1_row = lambda t: A(CORR0 + t * D, D)
    xfm_blk = lambda fcc: A(CORR0 + fcc * PTOK, PTOK)
    v_tile = lambda g, o8: A((g * 8 + o8) * 512, 512)
    v_row = lambda g: A(g * D, D)
    acat_blk = lambda fcc, l: PT(fcc * 64 + l * 16, 16)

    wtab_blk = lambda buf, fcc: G(buf * 4096 + fcc * 128, 128)
    w2_blk = lambda buf, fcc: G(buf * 2176 + fcc * 128, 128)
    xtm_tile = lambda buf, t: G(buf * 2048 + t * 512, 512)
    irt_g = lambda buf, j: G(4096 + buf * 4096 + j * 512, 512)
    ctab_g = lambda buf, j: G(buf * 4096 + j * 512, 512)
    w0_blk = lambda fcc: G(fcc * 512, 512)
    out_view = lambda buf: G(buf * 8192, 8192, F32)

    P_blk = lambda kc, t: PT(kc * 512 + t * 128, 128)
    tscr = PT(0, 8192, F32)
    mscr = lambda i: PT(i * 1024, 1024, F32)
    x1t = PT(0, 8192, F32)
    acc_v = PT(0, 1024, F32)
    mkt = lambda i: PTS[:, 2048 + i * 2048: 2048 + (i + 1) * 2048].bitcast(U8)

    x1_blk = lambda g, fcc: BETA[:, g * D + fcc * 128: g * D + fcc * 128 + 128]

    bank = [psT(f"bk{i}", [128, 512]) for i in range(8)]
    tr_ps = lambda b: bank[4 + b][:, 0:128]

    def warm():
        P.step("tensor", lambda e: e.matmul(
            bank[6][:, 0:128], ident_b[:], ident_b[:], start=True, stop=True),
            waits=[("ve", P.cnt("ve"))])

    sems = {s: ctx.enter_context(nc.semaphore(s))
            for s in ("te", "ve", "se", "cs", "xf", "tA", "tB",
                      "cA", "cB", "cC", "cD", "iA", "iB", "iC", "iD",
                      "xA", "xB", "wl", "w0", "w1", "w2", "w3",
                      "mk", "ac", "sA", "sB", "tq")}
    P = Plan()

    def dma(engine, out_ap, in_ap, sname, waits=()):
        P.step(engine, lambda e, o=out_ap, i=in_ap: e.dma_start(out=o, in_=i),
               waits=waits, incs=[(sname, 16)])

    dma("sync", ident_b[:], ident_d[:], "cs")
    dma("sync", identf_b[:], identf_d[:], "cs")
    dma("sync", onesb_b[:], onesb_d[:], "cs")

    ln1_affine_ve = []
    for ps in range(NPASS):
        for q in range(4):
            w = FC // 4
            dma("gpsimd",
                A(CORR0 + q * w * PTOK, w * PTOK)
                .rearrange("p (c t) -> p c t", c=w),
                x_fm_d[ps, :, q * w:(q + 1) * w, :], "xf",
                waits=[("ve", ln1_affine_ve[q] if ps > 0 else P.cnt("ve"))])

        dma("sync", GAMMA[:, 0:4096].rearrange("p (c k) -> p c k", c=FC),
            wtab_d[0], "tA", waits=[("te", P.cnt("te"))])
        dma("sync", GAMMA[:, 4096:8192].rearrange("p (c k) -> p c k", c=FC),
            wtab_d[1], "tB")
        P.step("tensor", waits=[("xf", P.cnt("xf")), ("cs", P.cnt("cs")),
                                ("ve", P.cnt("ve"))])
        a2_evac = {}
        a3_ve = {}
        for kc in range(2 * KC):
            buf = kc % 2
            tsem = "tA" if buf == 0 else "tB"
            w = [(tsem, P.cnt(tsem))]
            if kc >= 2:
                w.append(("ve", a2_evac[kc - 2]))
            P.step("tensor", waits=w)
            for fc in range(FC):
                P.step("tensor", lambda e, b=buf, f=fc, k=kc: e.matmul(
                    bank[b][:], wtab_blk(b, f), xfm_blk(f),
                    start=(f == 0), stop=(f == FC - 1)),
                    incs=[("te", 1)] if fc == FC - 1 else [])
            if kc + 2 < 2 * KC:
                dma("sync",
                    GAMMA[:, buf * 4096:buf * 4096 + 4096]
                    .rearrange("p (c k) -> p c k", c=FC),
                    wtab_d[kc + 2], tsem, waits=[("te", P.cnt("te"))])
            P.step("vector", lambda e, b=buf, k=kc: e.tensor_copy(
                out=F_at(k), in_=bank[b][:]),
                waits=[("te", P.cnt("te"))], incs=[("ve", 1)])
            a2_evac[kc] = P.cnt("ve")
            if kc == 2 * KC - 2:
                a2_te32 = P.cnt("te")
            if kc >= KC:
                k3 = kc - KC
                P.step("scalar", lambda e, k=k3: e.activation(
                    out=e_scr[:], in_=F_at(k), func=AF.Square),
                    waits=[("ve", a2_evac[kc])], incs=[("se", 1)])
                P.step("scalar", lambda e, k=k3: e.activation(
                    out=s2_scr[:], in_=F_at(KC + k), func=AF.Square),
                    incs=[("se", 1)])
                P.step("vector", lambda e, k=k3: e.tensor_add(
                    PTS[:, k * 512:(k + 1) * 512], e_scr[:], s2_scr[:]),
                    waits=[("se", P.cnt("se"))], incs=[("ve", 1)])
                a3_ve[k3] = P.cnt("ve")
        ve_F = P.cnt("ve")
        ve_P = P.cnt("ve")
        if ps == 0:
            dump("dbg_F", ALPHA[:, 0:F_sz])

        CGRP = [(0, 6), (6, 6), (12, 5)]
        def a4_tab(ts_, gi_, buf_, w=()):
            k0, ng = CGRP[gi_]
            dma("sync", GAMMA[:, buf_ * 4096: buf_ * 4096 + ng * 512]
                .rearrange("p (k c) -> p k c", k=ng),
                ctab_d[ts_][:, k0:k0 + ng, :], "cA" if buf_ == 0 else "cB", w)
        a4_tab(0, 0, 0, [("te", a2_te32)])
        a4_tab(0, 1, 1, [("te", P.cnt("te"))])
        pend_evac = []
        gidx = 0
        for ts in range(5):
            bb = (ts % 2) * 4
            for gi in range(3):
                k0, ng = CGRP[gi]
                buf = gidx % 2
                csem = "cA" if buf == 0 else "cB"
                vtgt = a3_ve[k0 + ng - 1]
                if ts == 0:
                    vtgt = max(vtgt, a2_evac[2 * KC - 1])
                w = [(csem, P.cnt(csem)), ("ve", vtgt)]
                if gi == 0 and ts >= 2:
                    w.append(("ve", pend_evac[ts - 2]))
                P.step("tensor", waits=w)
                for kk2 in range(ng):
                    kc = k0 + kk2
                    for t in range(TBP):
                        P.step("tensor", lambda e, tt=t, k=kc, b=buf, j=kk2,
                               bo=bb, s=ts: e.matmul(
                            bank[bo + tt][:, 0:TSW[s]], P_blk(k, tt),
                            ctab_g(b, j)[:, 0:TSW[s]],
                            start=(k == 0), stop=(k == KC - 1)),
                            incs=[("te", 1)] if t == TBP - 1 else [])
                ngi = gidx + 2
                if ngi < 15:
                    nts, ngj = divmod(ngi, 3)
                    a4_tab(nts, ngj, buf, [("te", P.cnt("te"))])
                gidx += 1
            te_now = P.cnt("te")
            for t in range(TBP):
                P.step("vector", lambda e, tt=t, s=ts, bo=bb: e.tensor_copy(
                    out=corr_tile(tt, s, TSW[s]), in_=bank[bo + tt][:, 0:TSW[s]]),
                    waits=[("te", te_now)], incs=[("ve", 1)])
            pend_evac.append(P.cnt("ve"))
        for t in range(TBP):
            P.step("vector", lambda e, tt=t: e.memset(corr_pad(tt), -1e9),
                   waits=[("ve", P.cnt("ve"))], incs=[("ve", 1)])

        ve_corr = P.cnt("ve")
        if ps == 0:
            dump("dbg_corr", A(CORR0, TBP * TAU))
        for t in range(TBP):
            P.step("vector", lambda e, tt=t: e.max(
                out=m8[:], in_=corr_tb(tt)),
                waits=[("ve", ve_corr)], incs=[("ve", 1)])
            warm()
            P.step("vector", lambda e: e.tensor_copy(out=mf[:], in_=m8[:]),
                   waits=[("ve", P.cnt("ve"))], incs=[("ve", 1)])
            P.step("vector", lambda e: e.tensor_scalar_mul(
                sm[:, 0:1], mf[:, 0:1], -1.0),
                waits=[("ve", P.cnt("ve"))], incs=[("ve", 1)])
            P.step("scalar", lambda e: e.activation(
                out=sm[:, 8:16], in_=mf[:], func=AF.Exp,
                bias=sm[:, 0:1], scale=1.0),
                waits=[("ve", P.cnt("ve"))], incs=[("se", 1)])
            P.step("vector", lambda e: e.reduce_sum(
                sm[:, 1:2], sm[:, 8:16], axis=AX),
                waits=[("se", P.cnt("se"))], incs=[("ve", 1)])
            P.step("scalar", lambda e: e.activation(
                out=sm[:, 2:3], in_=sm[:, 1:2], func=AF.Ln),
                waits=[("ve", P.cnt("ve"))], incs=[("se", 1)])
            P.step("vector", lambda e: e.tensor_sub(
                sm[:, 3:4], sm[:, 0:1], sm[:, 2:3]),
                waits=[("se", P.cnt("se"))], incs=[("ve", 1)])
            P.step("vector", lambda e: e.tensor_copy(
                out=sm[:, 4:5], in_=m8[:, 7:8]),
                waits=[("ve", P.cnt("ve"))], incs=[("ve", 1)])
            warm()
            for cchunk in range(5):
                cs = corr_tile(t, cchunk, TSW[cchunk])
                ww = TSW[cchunk]
                P.step("scalar", lambda e, c=cs, w2=ww: e.activation(
                    out=e_scr[:, 0:w2], in_=c, func=AF.Exp,
                    bias=sm[:, 3:4], scale=1.0),
                    waits=[("ve", P.cnt("ve"))], incs=[("se", 1)])
                P.step("vector", lambda e, c=cs, cc=cchunk, w2=ww:
                       e.scalar_tensor_tensor(
                    out=tscr[:, cc * 512:cc * 512 + w2], in0=c,
                    scalar=sm[:, 4:5], in1=e_scr[:, 0:w2],
                    op0=OP.is_ge, op1=OP.mult),
                    waits=[("se", P.cnt("se"))], incs=[("ve", 1)])
                warm()
            ve_s = P.cnt("ve")
            for fc in range(KC):
                b = fc % 2
                w = [("ve", ve_s)] if fc < 2 else [("ve", P.cnt("ve") - 1)]
                P.step("tensor", lambda e, f=fc, bb=b: e.transpose(
                    tr_ps(bb), tscr[:, f * 128:(f + 1) * 128], identf_b[:]),
                    waits=w, incs=[("te", 1)])
                P.step("vector", lambda e, f=fc, bb=b, tt=t: e.tensor_copy(
                    out=sfm_blk(tt, f), in_=tr_ps(bb)),
                    waits=[("te", P.cnt("te"))], incs=[("ve", 1)])
        ve_sfm = P.cnt("ve")
        if ps == 0:
            dump("dbg_sfm", A(CORR0, TBP * TAU))

        dma("sync", GAMMA[:, 0:2176].rearrange("p (c k) -> p c k", c=KC),
            wtab2_d[0], "tA", waits=[("te", P.cnt("te")), ("ve", ve_sfm)])
        dma("sync", GAMMA[:, 2176:4352].rearrange("p (c k) -> p c k", c=KC),
            wtab2_d[1], "tB")

        def sfm_mov(fcc):
            return A(CORR0, TBP * TAU).rearrange("p (t q) -> p t q", q=TAU)[
                :, :, fcc * 128:fcc * 128 + 128]

        P.step("tensor", waits=[("ve", ve_sfm)])
        for kc in range(KC):
            buf = kc % 2
            tsem = "tA" if buf == 0 else "tB"
            P.step("tensor", waits=[(tsem, P.cnt(tsem)),
                                    ("ve", max(0, P.cnt("ve") - 1))])
            for fc in range(KC):
                P.step("tensor", lambda e, b=buf, f=fc, k=kc: e.matmul(
                    bank[k % 2][:], w2_blk(b, f), sfm_mov(f),
                    start=(f == 0), stop=(f == KC - 1)),
                    incs=[("te", 1)] if fc == KC - 1 else [])
            if kc + 2 < KC:
                dma("sync",
                    GAMMA[:, buf * 2176:buf * 2176 + 2176]
                    .rearrange("p (c k) -> p c k", c=KC),
                    wtab2_d[kc + 2], tsem, waits=[("te", P.cnt("te"))])
            te_now = P.cnt("te")
            fre, fim = F_at(kc), F_at(KC + kc)
            P.step("vector", lambda e, a=fre, k=kc: e.tensor_mul(
                a, a, bank[k % 2][:]),
                waits=[("te", te_now), ("ve", P.cnt("ve"))], incs=[("ve", 1)])
            P.step("vector", lambda e, a=fim, k=kc: e.tensor_mul(
                a, a, bank[k % 2][:]), incs=[("ve", 1)])
            ve_u = P.cnt("ve")
        if ps == 0:
            dump("dbg_u", ALPHA[:, 0:F_sz])

        IGRP = [(0, 7), (7, 7), (14, 7), (21, 7), (28, 6)]
        def a9_tab(isl_, gi_, buf_, w=()):
            k0, ng = IGRP[gi_]
            dma("sync", GAMMA[:, 4096 + buf_ * 4096: 4096 + buf_ * 4096 + ng * 512]
                .rearrange("p (k c) -> p k c", k=ng),
                irtab_d[isl_][:, k0:k0 + ng, :], "iA" if buf_ == 0 else "iB", w)
        a9_tab(0, 0, 0, [("te", P.cnt("te")), ("ve", ve_u)])
        a9_tab(0, 1, 1)
        dma("gpsimd", G(0, 2048).rearrange("p (t c) -> p t c", t=TBP),
            x_tm_d[ps, 0], "xA", waits=[("te", P.cnt("te")), ("ve", ve_u)])
        P.step("tensor", waits=[("ve", ve_u)])
        pend_v1 = []
        gidx9 = 0
        for isl in range(8):
            bb9 = (isl % 2) * 4
            for gi in range(5):
                k0, ng = IGRP[gi]
                buf = gidx9 % 2
                isem = "iA" if buf == 0 else "iB"
                w = [(isem, P.cnt(isem))]
                if gi == 0 and isl >= 2:
                    w.append(("ve", pend_v1[isl - 2]))
                P.step("tensor", waits=w)
                for kk2 in range(ng):
                    kc = k0 + kk2
                    for t in range(TBP):
                        P.step("tensor", lambda e, tt=t, k=kc, b=buf, j=kk2,
                               bo=bb9: e.matmul(
                            bank[bo + tt][:],
                            A(k * PTOK + tt * 128, 128), irt_g(b, j),
                            start=(k == 0), stop=(k == 2 * KC - 1)),
                            incs=[("te", 1)] if t == TBP - 1 else [])
                ngi = gidx9 + 2
                if ngi < 40:
                    nisl, ngj = divmod(ngi, 5)
                    a9_tab(nisl, ngj, buf, [("te", P.cnt("te"))])
                gidx9 += 1
            if isl + 1 < 8:
                xsem2 = "xA" if (isl + 1) % 2 == 0 else "xB"
                dma("gpsimd",
                    G(((isl + 1) % 2) * 2048, 2048)
                    .rearrange("p (t c) -> p t c", t=TBP),
                    x_tm_d[ps, isl + 1], xsem2,
                    waits=[("ve", P.cnt("ve"))])
            te_now = P.cnt("te")
            xb = isl % 2
            xsem = "xA" if xb == 0 else "xB"
            for t in range(TBP):
                P.step("vector", lambda e, tt=t, b=xb, i=isl, bo=bb9:
                       e.scalar_tensor_tensor(
                           out=v1_tile(tt, i), in0=bank[bo + tt][:],
                           scalar=1.0, in1=xtm_tile(b, tt),
                           op0=OP.mult, op1=OP.add,
                           accum_out=vsum_p[:, tt * 8 + i: tt * 8 + i + 1]),
                       waits=[("te", te_now), (xsem, P.cnt(xsem))],
                       incs=[("ve", 1)])
                P.step("vector", lambda e, tt=t, i=isl:
                       e.scalar_tensor_tensor(
                           out=s2_scr[:], in0=v1_tile(tt, i), scalar=1.0,
                           in1=v1_tile(tt, i), op0=OP.mult, op1=OP.mult,
                           accum_out=vsq_p[:, tt * 8 + i: tt * 8 + i + 1]),
                       waits=[("ve", P.cnt("ve"))], incs=[("ve", 1)])
            pend_v1.append(P.cnt("ve"))

        if ps == NPASS - 1:
            post_a9 = (P.cnt("te"), P.cnt("ve"))
        if ps == 0:
            dump("dbg_v1", A(CORR0, TBP * D))
        for t in range(TBP):
            g = ps * TBP + t
            P.step("vector", lambda e, tt=t: e.reduce_sum(
                sm[:, 5:6], vsum_p[:, tt * 8:(tt + 1) * 8], axis=AX),
                waits=[("ve", P.cnt("ve"))], incs=[("ve", 1)])
            P.step("vector", lambda e, tt=t: e.reduce_sum(
                sm[:, 6:7], vsq_p[:, tt * 8:(tt + 1) * 8], axis=AX),
                incs=[("ve", 1)])
            P.step("vector", lambda e: e.tensor_scalar_mul(
                sm[:, 5:6], sm[:, 5:6], 1.0 / D),
                waits=[("ve", P.cnt("ve"))], incs=[("ve", 1)])
            P.step("vector", lambda e: e.tensor_scalar_mul(
                sm[:, 6:7], sm[:, 6:7], 1.0 / D), incs=[("ve", 1)])
            P.step("vector", lambda e: e.tensor_mul(
                sm[:, 7:8], sm[:, 5:6], sm[:, 5:6]),
                waits=[("ve", P.cnt("ve"))], incs=[("ve", 1)])
            P.step("vector", lambda e: e.tensor_sub(
                sm[:, 7:8], sm[:, 6:7], sm[:, 7:8]),
                waits=[("ve", P.cnt("ve"))], incs=[("ve", 1)])
            P.step("vector", lambda e: e.tensor_scalar_add(
                sm[:, 7:8], sm[:, 7:8], EPS),
                waits=[("ve", P.cnt("ve"))], incs=[("ve", 1)])
            P.step("vector", lambda e: e.reciprocal(
                sm[:, 8:9], sm[:, 7:8]),
                waits=[("ve", P.cnt("ve"))], incs=[("ve", 1)])
            warm()
            P.step("scalar", lambda e: e.activation(
                out=sm[:, 9:10], in_=sm[:, 8:9], func=AF.Sqrt),
                waits=[("ve", P.cnt("ve"))], incs=[("se", 1)])
            P.step("vector", lambda e: e.tensor_mul(
                sm[:, 10:11], sm[:, 5:6], sm[:, 9:10]),
                waits=[("se", P.cnt("se"))], incs=[("ve", 1)])
            P.step("vector", lambda e: e.tensor_scalar_mul(
                sm[:, 10:11], sm[:, 10:11], -1.0),
                waits=[("ve", P.cnt("ve"))], incs=[("ve", 1)])
            P.step("vector", lambda e, tt=t: e.tensor_scalar(
                out=x1t[:], in0=v1_row(tt), scalar1=sm[:, 9:10],
                scalar2=sm[:, 10:11], op0=OP.mult, op1=OP.add),
                waits=[("ve", P.cnt("ve"))], incs=[("ve", 1)])
            if ps == 0:
                ln1_affine_ve.append(P.cnt("ve"))
            warm()
            ve_x1 = P.cnt("ve")
            for fc in range(FC):
                b = fc % 2
                w = [("ve", ve_x1)] if fc < 2 else [("ve", P.cnt("ve") - 1)]
                P.step("tensor", lambda e, f=fc, bb=b: e.transpose(
                    tr_ps(bb), x1t[:, f * 128:(f + 1) * 128], identf_b[:]),
                    waits=w, incs=[("te", 1)])
                if ps == NPASS - 1 and t == TBP - 1 and fc == 15:
                    acat_rel = (P.cnt("te"), ve_x1)
                P.step("vector", lambda e, f=fc, bb=b, gg=g: e.tensor_copy(
                    out=x1_blk(gg, f), in_=tr_ps(bb)),
                    waits=[("te", P.cnt("te"))], incs=[("ve", 1)])
    ve_x1fm = P.cnt("ve")
    if debug:
        dump("dbg_x1", BETA[:, :])

    dma("gpsimd", PTS[:, 0:4096].rearrange("p (c r) -> p c r", c=FC),
        acat_d[:], "ac", waits=[("ve", acat_rel[1]), ("te", acat_rel[0])])

    def x1_mov(th, fcc):
        return BETA[:, :].rearrange("p (g q) -> p g q", q=D)[
            :, th * 4:(th + 1) * 4, fcc * 128:fcc * 128 + 128]

    if debug:
        P.step("sync", waits=[("ac", P.cnt("ac"))])
        dump("dbg_acat", PTS[:, 0:2048])
    P.step("tensor", waits=[("ve", ve_x1fm), ("ac", P.cnt("ac"))])
    for th in range(2):
        for fc in range(FC):
            P.step("tensor", lambda e, t=th, f=fc: e.matmul(
                bank[t][0:96, :], PT(f * 128, 96), x1_mov(t, f),
                start=(f == 0), stop=(f == FC - 1)),
                incs=[("te", 1)] if fc == FC - 1 else [])
        for fc in range(FC):
            P.step("tensor", lambda e, t=th, f=fc: e.matmul(
                bank[2 + t][0:16, :], PT(f * 128 + 96, 16), x1_mov(t, f),
                start=(f == 0), stop=(f == FC - 1)),
                incs=[("te", 1)] if fc == FC - 1 else [])
    te_h = P.cnt("te")
    for th in range(2):
        P.step("vector", lambda e, t=th: e.tensor_copy(
            out=h_b[0:96, t * 512: t * 512 + 512], in_=bank[t][0:96, :]),
            waits=[("te", te_h)], incs=[("ve", 1)])
        P.step("vector", lambda e, t=th: e.tensor_copy(
            out=h3_b[:, t * 512: t * 512 + 512], in_=bank[2 + t][0:16, :]),
            incs=[("ve", 1)])
    ve_h = P.cnt("ve")
    if debug:
        dump("dbg_h", h_b[:, :])


    def b_quarters(o8, extra_waits=()):
        for q in range(4):
            w = list(extra_waits) if q == 0 else []
            if o8 >= 1:
                w.append(("tq", 4 * (o8 - 1) + q + 1))
            dma("gpsimd",
                GAMMA[:, q * 4096:(q + 1) * 4096]
                .rearrange("p (c k) -> p c k", c=8),
                w0t_d[o8, :, q * 8:(q + 1) * 8, :], f"w{q}", waits=w)

    def b_small(o8, waits_b):
        dma("gpsimd", b0_b[:, o8 % 2], b0_d[o8], "wl", waits=waits_b)
        dma("gpsimd", blt_b[:, o8 % 2], blt_d[o8], "wl")
        dma("gpsimd", blt3_b[:, o8 % 2], blt_d[o8, 96:112, :], "wl")

    def b_masks(o8, waits_m):
        for i in range(3):
            dma("sync", mkt(i).rearrange("p (t c) -> p t c", t=TB),
                m_d[i][o8], "mk", waits=waits_m if i == 0 else ())

    b_quarters(0, extra_waits=[("te", post_a9[0]), ("ve", post_a9[1])])
    b_small(0, ())
    b_masks(0, [("te", P.cnt("te"))])

    SETS = [bank[0:4], bank[4:8]]
    NH = NO8 * 2
    evac_ve = {}
    sel_ve = {}
    lora_te = {}
    tail_te = {}

    def app_lora(k, gi):
        o8, h = k // 2, k % 2
        g = h * 4 + gi
        st = SETS[k % 2]
        if gi == 0:
            w = [("ve", evac_ve[k]), ("wl", 48 * (o8 + 1))]
        else:
            w = [("ve", sel_ve[(k, gi - 1)])]
        P.step("tensor", waits=w)
        for l in range(4):
            if l < 3:
                lh = h_b[32 * l:32 * l + 16, g * 128: g * 128 + 128]
                lb = blt_b[32 * l:32 * l + 16, o8 % 2, :]
            else:
                lh = h3_b[:, g * 128: g * 128 + 128]
                lb = blt3_b[:, o8 % 2, :]
            P.step("tensor", lambda e, ll=l, s=st, a=lh, b=lb: e.matmul(
                s[ll][:], a, b, start=True, stop=True),
                incs=[("te", 1)] if l == 3 else [])
        lora_te[(k, gi)] = P.cnt("te")

    def app_sel(k, gi):
        o8, h = k // 2, k % 2
        g = h * 4 + gi
        st = SETS[k % 2]
        P.step("vector", lambda e, s=st: e.tensor_copy(
            out=acc_v[:], in_=s[0][:]),
            waits=[("te", lora_te[(k, gi)]), ("mk", 48 * (o8 + 1)),
                   ("ve", P.cnt("ve"))], incs=[("ve", 1)])
        for i in range(3):
            P.step("vector", lambda e, ii=i, gg=g, s=st: e.copy_predicated(
                out=acc_v[:],
                mask=mkt(ii)[:, gg * OSLAB:(gg + 1) * OSLAB],
                data=s[1 + ii][:]),
                waits=[("ve", P.cnt("ve"))], incs=[("ve", 1)])
        sel_ve[(k, gi)] = P.cnt("ve")
        P.step("vector", lambda e, gg=g, o=o8: e.scalar_tensor_tensor(
            out=v_tile(gg, o), in0=acc_v[:], scalar=8.0, in1=v_tile(gg, o),
            op0=OP.mult, op1=OP.add,
            accum_out=vsum_p[:, gg * 8 + o: gg * 8 + o + 1]),
            waits=[("ve", P.cnt("ve"))], incs=[("ve", 1)])
        P.step("vector", lambda e, gg=g, o=o8: e.scalar_tensor_tensor(
            out=s2_scr[:], in0=v_tile(gg, o), scalar=1.0, in1=v_tile(gg, o),
            op0=OP.mult, op1=OP.mult,
            accum_out=vsq_p[:, gg * 8 + o: gg * 8 + o + 1]),
            waits=[("ve", P.cnt("ve"))], incs=[("ve", 1)])

    for k in range(NH):
        o8, h = k // 2, k % 2
        st = SETS[k % 2]
        P.step("tensor",
               waits=[("ve", ve_h if k < 2 else sel_ve[(k - 2, 3)])])
        for fc in range(FC):
            if h == 0 and fc % 8 == 0:
                P.step("tensor", waits=[(f"w{fc // 8}", 16 * (o8 + 1))])
            for gi in range(4):
                g = h * 4 + gi
                inc = [("tq", 1)] if (h == 1 and fc % 8 == 7 and gi == 3) \
                    else []
                P.step("tensor", lambda e, gg=g, f=fc, s=st, i=gi: e.matmul(
                    s[i][:], x1_blk(gg, f), w0_blk(f),
                    start=(f == 0), stop=False), incs=inc)
            if k > 0 and fc % 8 == 7:
                app_lora(k - 1, fc // 8)
                app_sel(k - 1, fc // 8)
        P.step("tensor", waits=[("wl", 48 * (o8 + 1))])
        for gi in range(4):
            g = h * 4 + gi
            P.step("tensor", lambda e, o=o8, s=st, i=gi: e.matmul(
                s[i][:], onesb_b[:], b0_b[:, o % 2],
                start=False, stop=False))
            for j in range(4):
                P.step("tensor", lambda e, gg=g, o=o8, jj=j, s=st, i=gi:
                       e.matmul(
                           s[i][:, jj * 128:(jj + 1) * 128],
                           x1_blk(gg, 4 * o + jj), ident_b[:],
                           start=False, stop=(jj == 3)),
                       incs=[("te", 1)] if j == 3 else [])
        tail_te[k] = P.cnt("te")
        for gi in range(4):
            g = h * 4 + gi
            P.step("vector", lambda e, gg=g, o=o8, s=st, i=gi: e.tensor_copy(
                out=v_tile(gg, o), in_=s[i][:]),
                waits=[("te", tail_te[k]), ("ve", P.cnt("ve"))],
                incs=[("ve", 1)])
        evac_ve[k] = P.cnt("ve")
        if h == 0 and o8 + 1 < NO8:
            b_quarters(o8 + 1)
            b_small(o8 + 1, [("te", P.cnt("te"))])
        if h == 0 and o8 >= 1:
            b_masks(o8, [("ve", P.cnt("ve"))])
    for gi in range(4):
        app_lora(NH - 1, gi)
        app_sel(NH - 1, gi)

    te_b = P.cnt("te")
    for g in range(TB):
        P.step("vector", lambda e, gg=g: e.reduce_sum(
            sm[:, 5:6], vsum_p[:, gg * 8:(gg + 1) * 8], axis=AX),
            waits=[("ve", P.cnt("ve")), ("te", te_b)], incs=[("ve", 1)])
        P.step("vector", lambda e, gg=g: e.reduce_sum(
            sm[:, 6:7], vsq_p[:, gg * 8:(gg + 1) * 8], axis=AX),
            incs=[("ve", 1)])
        P.step("vector", lambda e: e.tensor_scalar_mul(
            sm[:, 5:6], sm[:, 5:6], 1.0 / D),
            waits=[("ve", P.cnt("ve"))], incs=[("ve", 1)])
        P.step("vector", lambda e: e.tensor_scalar_mul(
            sm[:, 6:7], sm[:, 6:7], 1.0 / D), incs=[("ve", 1)])
        P.step("vector", lambda e: e.tensor_mul(
            sm[:, 7:8], sm[:, 5:6], sm[:, 5:6]),
            waits=[("ve", P.cnt("ve"))], incs=[("ve", 1)])
        P.step("vector", lambda e: e.tensor_sub(
            sm[:, 7:8], sm[:, 6:7], sm[:, 7:8]),
            waits=[("ve", P.cnt("ve"))], incs=[("ve", 1)])
        P.step("vector", lambda e: e.tensor_scalar_add(
            sm[:, 7:8], sm[:, 7:8], EPS),
            waits=[("ve", P.cnt("ve"))], incs=[("ve", 1)])
        P.step("vector", lambda e: e.reciprocal(
            sm[:, 8:9], sm[:, 7:8]),
            waits=[("ve", P.cnt("ve"))], incs=[("ve", 1)])
        P.step("scalar", lambda e: e.activation(
            out=sm[:, 9:10], in_=sm[:, 8:9], func=AF.Sqrt),
            waits=[("ve", P.cnt("ve"))], incs=[("se", 1)])
        P.step("vector", lambda e: e.tensor_mul(
            sm[:, 10:11], sm[:, 5:6], sm[:, 9:10]),
            waits=[("se", P.cnt("se"))], incs=[("ve", 1)])
        P.step("vector", lambda e: e.tensor_scalar_mul(
            sm[:, 10:11], sm[:, 10:11], -1.0),
            waits=[("ve", P.cnt("ve"))], incs=[("ve", 1)])
        b = g % 2
        ssem = "sA" if b == 0 else "sB"
        w = [("ve", P.cnt("ve"))]
        if g >= 2:
            w.append((ssem, P.cnt(ssem)))
        P.step("vector", lambda e, gg=g, bb=b: e.tensor_scalar(
            out=out_view(bb), in0=v_row(gg), scalar1=sm[:, 9:10],
            scalar2=sm[:, 10:11], op0=OP.mult, op1=OP.add),
            waits=w, incs=[("ve", 1)])
        dma("sync", out_d[g], out_view(b), ssem, waits=[("ve", P.cnt("ve"))])

    P.step("sync", waits=[("sA", P.cnt("sA")), ("sB", P.cnt("sB")),
                          ("wl", P.cnt("wl")), ("xf", P.cnt("xf"))])

    with nc.Block() as block:
        @block.sync
        def _(eng):
            P.emit("sync", eng, sems)

        @block.gpsimd
        def _(eng):
            P.emit("gpsimd", eng, sems)

        @block.tensor
        def _(eng):
            P.emit("tensor", eng, sems)

        @block.vector
        def _(eng):
            P.emit("vector", eng, sems)

        @block.scalar
        def _(eng):
            P.emit("scalar", eng, sems)

    ctx.close()
    return nc


_NC = {}


def kernel(L0_attn_x, L0_ffn_W0, L0_ffn_b0, L0_ffn_A, L0_ffn_B):
    x = np.asarray(L0_attn_x, dtype=np.float32)
    w0 = np.asarray(L0_ffn_W0, dtype=np.float32)
    b0 = np.asarray(L0_ffn_b0, dtype=np.float32)
    a_w = np.asarray(L0_ffn_A, dtype=np.float32)
    b_w = np.asarray(L0_ffn_B, dtype=np.float32)
    if "nc" not in _NC:
        _NC["nc"] = build_program()
    in_maps = [_core_inputs(c, x, w0, b0, a_w, b_w) for c in range(N_CORES)]
    res = run_bass_kernel_spmd(_NC["nc"], in_maps, core_ids=list(range(N_CORES)))
    out = np.concatenate([r["out"].reshape(-1, D) for r in res.results], axis=0)
    return out.reshape(1, SEQ, D).astype(np.float32)
